# version 25
# baseline (speedup 1.0000x reference)
import contextlib
import numpy as np

import concourse.bass as bass
import concourse.mybir as mybir
from concourse.alu_op_type import AluOpType as ALU
from concourse.bass_utils import run_bass_kernel_spmd

F32 = mybir.dt.float32
BF16 = mybir.dt.bfloat16
I32 = mybir.dt.int32
AF = mybir.ActivationFunctionType
AX = mybir.AxisListType

S = 4096
D = 1024
H = 2048
Z = 2 * H
N = 32
E = 8
FF = 2048
CH = 512
T = 128
NT = CH // T
NCH = S // CH


class Sched:

    ENGS = ("sync", "gpsimd", "tensor", "vector", "scalar")

    def __init__(self, nc, sems):
        self.nc = nc
        self.sems = sems
        self.cnt = {k: 0 for k in sems}
        self.streams = {e: [] for e in self.ENGS}
        self.last_wait = {e: {} for e in self.ENGS}

    def wait(self, eng, tok):
        if tok is None:
            return
        sem, val = tok
        if self.last_wait[eng].get(sem, 0) >= val:
            return
        self.last_wait[eng][sem] = val
        h = self.sems[sem]
        self.streams[eng].append(lambda e, h=h, val=val: e.wait_ge(h, val))

    def waits(self, eng, toks):
        best = {}
        for t in toks:
            if t is None:
                continue
            sem, val = t
            if val > best.get(sem, 0):
                best[sem] = val
        for sem, val in best.items():
            self.wait(eng, (sem, val))

    def op(self, eng, fn, sem=None, amt=1, waits=()):
        self.waits(eng, waits)
        if sem is None:
            self.streams[eng].append(lambda e, fn=fn: fn(e))
            return None
        self.cnt[sem] += amt
        tok = (sem, self.cnt[sem])
        h = self.sems[sem]

        def wrapped(e, fn=fn, h=h, amt=amt):
            fn(e).then_inc(h, amt)
        self.streams[eng].append(wrapped)
        return tok

    def replay(self, block):
        for eng in self.ENGS:
            stream = self.streams[eng]

            def make(stream):
                def body(h):
                    for fn in stream:
                        fn(h)
                return body
            getattr(block, eng)(make(stream))


def build_nc(n_chunks=NCH, dbg=False):
    nc = bass.Bass("TRN2", target_bir_lowering=False)
    S_ = n_chunks * CH

    x_d = nc.dram_tensor("x", [S_, D], F32, kind="ExternalInput").ap()
    w_in = nc.dram_tensor("w_in", [Z, D], F32, kind="ExternalInput").ap()
    b_in = nc.dram_tensor("b_in", [Z], F32, kind="ExternalInput").ap()
    conv_w = nc.dram_tensor("conv_w", [H, 1, 4], F32, kind="ExternalInput").ap()
    conv_b = nc.dram_tensor("conv_b", [H], F32, kind="ExternalInput").ap()
    w_ssm = nc.dram_tensor("w_ssm", [N, H], F32, kind="ExternalInput").ap()
    b_ssm = nc.dram_tensor("b_ssm", [N], F32, kind="ExternalInput").ap()
    A_d = nc.dram_tensor("A", [N, N], F32, kind="ExternalInput").ap()
    Bv_d = nc.dram_tensor("Bv", [N], F32, kind="ExternalInput").ap()
    Cv_d = nc.dram_tensor("Cv", [N], F32, kind="ExternalInput").ap()
    Dv_d = nc.dram_tensor("Dv", [H], F32, kind="ExternalInput").ap()
    w_out = nc.dram_tensor("w_out", [D, H], F32, kind="ExternalInput").ap()
    b_out = nc.dram_tensor("b_out", [D], F32, kind="ExternalInput").ap()
    wg = nc.dram_tensor("wg", [D, E], F32, kind="ExternalInput").ap()
    w1 = nc.dram_tensor("w1", [E, D, FF], F32, kind="ExternalInput").ap()
    w2 = nc.dram_tensor("w2", [E, FF, D], F32, kind="ExternalInput").ap()
    out_d = nc.dram_tensor("out", [S_, D], F32, kind="ExternalOutput").ap()

    if dbg:
        dbg_xn = nc.dram_tensor("dbg_xn", [128, 4096], F32, kind="ExternalOutput").ap()
        dbg_xnt = nc.dram_tensor("dbg_xnt", [128, 4096], F32, kind="ExternalOutput").ap()
        dbg_p1 = nc.dram_tensor("dbg_p1", [128, 8192], F32, kind="ExternalOutput").ap()
        dbg_vt = nc.dram_tensor("dbg_vt", [32, 512], F32, kind="ExternalOutput").ap()
        dbg_sbc = nc.dram_tensor("dbg_sbc", [128, 512], F32, kind="ExternalOutput").ap()
        dbg_ht = nc.dram_tensor("dbg_ht", [32, 33], F32, kind="ExternalOutput").ap()
        dbg_smix = nc.dram_tensor("dbg_smix", [128, 4096], F32, kind="ExternalOutput").ap()
        dbg_cw = nc.dram_tensor("dbg_cw", [128, 32], F32, kind="ExternalOutput").ap()
        dbg_out = nc.dram_tensor("dbg_out", [128, 4096], F32, kind="ExternalOutput").ap()
        dbg_yin = nc.dram_tensor("dbg_yin", [128, 8192], F32, kind="ExternalOutput").ap()
        dbg_btz = nc.dram_tensor("dbg_btz", [128, 4096], F32, kind="ExternalOutput").ap()
        dbg_mr = nc.dram_tensor("dbg_mr", [128, 1024], F32, kind="ExternalOutput").ap()
        dbg_cmat = nc.dram_tensor("dbg_cmat", [32, 130], F32, kind="ExternalOutput").ap()
        dbg_sx = nc.dram_tensor("dbg_sx", [128, 4096], F32, kind="ExternalOutput").ap()
        dbg_regs = nc.dram_tensor("dbg_regs", [128, 8], F32, kind="ExternalOutput").ap()
        dbg_idx = nc.dram_tensor("dbg_idx", [128, 16], F32, kind="ExternalOutput").ap()
        dbg_wab = nc.dram_tensor("dbg_wab", [128, 8], F32, kind="ExternalOutput").ap()
        dbg_m12a = nc.dram_tensor("dbg_m12a", [128, 32], F32, kind="ExternalOutput").ap()
        dbg_m12b = nc.dram_tensor("dbg_m12b", [128, 32], F32, kind="ExternalOutput").ap()
        dbg_slot = nc.dram_tensor("dbg_slot", [128, 64], F32, kind="ExternalOutput").ap()
        dbg_tab = nc.dram_tensor("dbg_tab", [128, 32], F32, kind="ExternalOutput").ap()
        dbg_yab = nc.dram_tensor("dbg_yab", [1024, 1024], F32, kind="ExternalOutput").ap()
    scr_wit = nc.dram_tensor("scr_wit", [16, 128, 2048], BF16,
                             kind="Internal").ap()
    scr_wout = nc.dram_tensor("scr_wout", [16, 128, D], BF16,
                              kind="Internal").ap()
    scr_w1 = nc.dram_tensor("scr_w1", [E, D, FF], BF16, kind="Internal").ap()
    scr_w2 = nc.dram_tensor("scr_w2", [E, FF, D], BF16, kind="Internal").ap()
    scr_btz = nc.dram_tensor("scr_btz", [128, 4096], BF16, kind="Internal").ap()
    scr_idx = nc.dram_tensor("scr_idx", [2048], mybir.dt.float32,
                             kind="Internal").ap()
    scr_idx2 = nc.dram_tensor("scr_idx2", [2048], mybir.dt.float32,
                              kind="Internal").ap()
    scr_yab = nc.dram_tensor("scr_yab", [1024, D], BF16, kind="Internal").ap()
    scr_xn = nc.dram_tensor("scr_xn", [2 * CH, D], BF16, kind="Internal").ap()

    with contextlib.ExitStack() as st:
        def sb(name, shape, dt):
            return st.enter_context(nc.sbuf_tensor(name, shape, dt))

        def ps(name, shape, dt):
            return st.enter_context(nc.psum_tensor(name, shape, dt))

        pool = [sb(f"pool{i}", [128, 16384], BF16) for i in range(3)]
        btz = sb("btz", [128, 32 * T], BF16)
        mr = sb("mr", [128, 32 * N], BF16)
        wssmT = sb("wssmT", [128, 16 * N], BF16)
        sBOUT = sb("sBOUT", [128, D], BF16)
        xnt = sb("xnt", [128, 8 * CH], BF16)
        smix = sb("smix", [128, NT * D], BF16)
        swit = [sb(f"swit{i}", [128, 2048], BF16) for i in range(2)]
        ssbc = sb("ssbc", [128, CH], F32)
        ar1 = sb("ar1", [128, 4096], F32)
        ar2 = sb("ar2", [128, 4096], F32)
        ar3 = sb("ar3", [128, 4096], F32)
        U32 = mybir.dt.uint32
        ar1b = ar1.bitcast(BF16)
        shts = ar1b
        sxg = ar1
        syr = ar1b
        sxn = ar2
        sout = ar2
        sp1 = ar3.bitcast(BF16)
        sxe = ar3
        soutb = ar3
        syp = ar3.bitcast(BF16)
        spad = [sb(f"spad{i}", [128, CH + 8], F32) for i in range(2)]
        ssg = [sb(f"ssg{i}", [128, CH], BF16) for i in range(2)]
        su = [sb(f"su{i}", [128, CH], F32) for i in range(2)]
        sub = [sb(f"sub{i}", [128, CH], BF16) for i in range(4)]
        stmp = su[0]
        sidf = sb("sidf", [128, 128], F32)
        sidb = sb("sidb", [128, 128], BF16)
        cmatT = sb("cmatT", [N, T + 2], F32)
        cb = [sb(f"cb{i}", [N, N + 1], F32) for i in range(2)]
        sPm = sb("sPm", [N, N], F32)
        sA = sb("sA", [N, N], F32)
        sHT = sb("sHT", [N, NCH * NT + 1], F32)
        sVT = sb("sVT", [N, CH], F32)
        svr = sb("svr", [128, T], BF16)
        sS = sb("sS", [NT, T], F32)
        s_row = sb("s_row", [1, CH], F32)
        sU = sb("sU", [NT, N], F32)
        s32 = sb("s32", [N, 16], F32)
        sUT = s32[:, 8:8 + NT]
        sBINT = sb("sBINT", [128, 32], F32)
        sCBT = sb("sCBT", [128, 16], F32)
        sDVT = sb("sDVT", [128, 16], F32)
        sCWS = sb("sCWS", [128, 64], F32)
        sWg = sb("sWg", [128, 64], BF16)
        shalo = sb("shalo", [128, 16 * 3], F32)
        sones = sb("sones", [1, 128], F32)
        sones8 = sb("sones8", [128, 8], F32)
        sbv = s32[:, 0:1]
        scv = s32[:, 1:2]
        sbs = s32[:, 2:3]
        sbsbv = s32[:, 3:4]
        smixf = smix.bitcast(F32)
        sbrow = smixf
        srawT = su[1]
        swsraw = pool[2].bitcast(F32)
        srt = [sb(f"srt{i}", [128, 32], F32) for i in range(6)]

        sslotm = sb("sslotm", [128, 64], F32)
        sidxf = sb("sidxf", [128, 16], F32)
        sidxu = sb("sidxu", [128, 16], U32)
        siotg = sb("siotg", [128, 4], F32)
        siotl = sb("siotl", [128, 4], F32)
        shug = sb("shug", [128, 16], F32)
        stabT = sb("stabT", [128, 32], F32)
        stabTi = sb("stabTi", [128, 32], U32)
        swab = sb("swab", [128, 8], F32)
        sb8 = sb("sb8", [8, 4], F32)
        smaskT = ssbc[0:8, 0:512]
        scum = su[0][0:8, 0:512]
        sregs = sb("sregs", [128, 8], F32)
        sregs2 = sb("sregs2", [128, 8], F32)
        srm = [sregs[:, i:i + 1] for i in range(4)]

        pb = [ps(f"pb{i}", [128, 512], F32) for i in range(8)]
        pbb = [b.bitcast(BF16) for b in pb]

        sems = {}
        for s_name in ("pe", "dv", "ac", "gp", "sd", "gd", "ad",
                       "gi0", "gi1", "gs0", "gs1", "gx", "gt",
                       "gy0", "gy1", "gn0", "gn1", "go", "gz",
                       "ap0", "ap1", "ap2", "aw0", "aw1", "ar"):
            sems[s_name] = st.enter_context(nc.semaphore(s_name))
        block = st.enter_context(nc.Block())
        P = Sched(nc, sems)

        bank_w = [None] * 8
        bank_r = [None] * 8

        def mm_group(items, bank_i, extra_waits=(), sem="pe"):
            P.waits("tensor", list(extra_waits) + [bank_r[bank_i]])
            n = len(items)
            tok = None
            for i, (o, l, r) in enumerate(items):
                s0, s1 = (i == 0), (i == n - 1)
                if s1:
                    tok = P.op("tensor",
                               lambda t, o=o, l=l, r=r, s0=s0, s1=s1:
                               t.matmul(o, l, r, start=s0, stop=s1), sem=sem)
                else:
                    P.op("tensor",
                         lambda t, o=o, l=l, r=r, s0=s0, s1=s1:
                         t.matmul(o, l, r, start=s0, stop=s1))
            bank_w[bank_i] = tok
            return tok

        def evac(eng, fn, bank_i, extra_waits=(), sem=None):
            sem = sem or ("dv" if eng == "vector" else "ac")
            tok = P.op(eng, fn, sem=sem,
                       waits=list(extra_waits) + [bank_w[bank_i]])
            bank_r[bank_i] = tok
            return tok

        regs = {}

        def _alloc_regs(g):
            b_idx = g.alloc_register(name="b_idx")
            regs["b_idx"] = b_idx
            g.reg_mov(regs["b_idx"], 2047)
            b_tok = g.alloc_register(name="b_tok")
            regs["b_tok"] = b_tok
            g.reg_mov(regs["b_tok"], 2 * CH - 1)
            b_slot = g.alloc_register(name="b_slot")
            regs["b_slot"] = b_slot
            g.reg_mov(regs["b_slot"], 1023)
        P.streams["gpsimd"].append(_alloc_regs)

        t_iota = P.op("gpsimd", lambda g: g.iota(
            sidf[:], [[-1, 128]], base=0, channel_multiplier=1,
            allow_small_or_imprecise_dtypes=True), sem="gp")
        t_ones = P.op("gpsimd", lambda g: g.memset(sones[:], 1.0), sem="gp")
        t_ones8 = P.op("gpsimd", lambda g: g.memset(sones8[:], 1.0), sem="gp")
        t_halo0 = P.op("gpsimd", lambda g: g.memset(shalo[:], 0.0), sem="gp")
        t_b8 = P.op("gpsimd", lambda g: g.iota(
            sb8[0:8, 0:1], [[1, 1]], base=-1, channel_multiplier=256,
            allow_small_or_imprecise_dtypes=True), sem="gp")
        t_b8 = P.op("gpsimd", lambda g: g.iota(
            sb8[0:8, 2:3], [[1, 1]], base=255, channel_multiplier=256,
            allow_small_or_imprecise_dtypes=True), sem="gp")
        t_iotl = P.op("gpsimd", lambda g: g.iota(
            siotl[:], [[128, 4]], base=0, channel_multiplier=1,
            allow_small_or_imprecise_dtypes=True), sem="gp")
        t_hug = P.op("gpsimd", lambda g: g.memset(
            shug[:], float(1 << 20)), sem="gp")
        t_arena1_z = P.op("gpsimd", lambda g: g.memset(ar1[:], 0.0),
                          sem="gp")


        t_idf = P.op("vector", lambda v: v.tensor_scalar(
            sidf[:], sidf[:], 0, None, ALU.is_equal), sem="dv",
            waits=[t_iota])
        t_idb = P.op("vector", lambda v: v.tensor_copy(
            sidb[:], sidf[:]), sem="dv", waits=[t_idf])
        t_ht0 = P.op("vector", lambda v: v.memset(sHT[:, 0:1], 0.0), sem="dv")

        def ld(dst, src, waits=()):
            return P.op("sync", lambda s, d=dst, sr=src: s.dma_start(d, sr),
                        sem="sd", amt=16, waits=waits)

        t_a = ld(sA[:], A_d)
        t_bv = ld(sbv, bass.AP(Bv_d.tensor, 0, [[1, N], [1, 1]]))
        t_cv = ld(scv, bass.AP(Cv_d.tensor, 0, [[1, N], [1, 1]]))
        t_bs = ld(sbs, bass.AP(b_ssm.tensor, 0, [[1, N], [1, 1]]))
        t_binr = ld(srawT[0:32, 0:128],
                    bass.AP(b_in.tensor, 0, [[128, 32], [1, 128]]))
        t_cws = ld(sCWS[:],
                   bass.AP(conv_w.tensor, 0, [[4, 128], [512, 16], [1, 4]]))
        t_brow = ld(sbrow[0:1, 0:D], bass.AP(b_out.tensor, 0, [[D, 1], [1, D]]))
        t_wsr = ld(swsraw[0:N, 0:H], w_ssm)
        t_wg = P.op("gpsimd", lambda g: g.dma_start(
            sWg[:], bass.AP(wg.tensor, 0, [[8, 128], [1024, 8], [1, 8]])),
            sem="gd", amt=16)

        t_bsbv = P.op("vector", lambda v: v.tensor_tensor(
            sbsbv, sbs, sbv, ALU.mult), sem="dv",
            waits=[t_bv, t_bs])

        def small_transpose(dst_ap, src_ap, in_p, load_tok):
            tw = P.op("tensor", lambda t, src_ap=src_ap, in_p=in_p:
                      t.transpose(pb[6][0:128, 0:in_p], src_ap,
                                  sidf[0:in_p, 0:in_p]),
                      sem="pe", waits=[load_tok, t_idf, bank_r[6]])
            bank_w[6] = tw
            return evac("vector", lambda v, dst_ap=dst_ap, in_p=in_p:
                        v.tensor_copy(dst_ap, pb[6][0:128, 0:in_p]), 6)

        t_bint = small_transpose(sBINT[:], srawT[0:32, 0:128], 32, t_binr)
        t_cbr = ld(srawT[0:16, 0:128],
                   bass.AP(conv_b.tensor, 0, [[128, 16], [1, 128]]),
                   waits=[t_bint])
        t_cbt = small_transpose(sCBT[:], srawT[0:16, 0:128], 16, t_cbr)
        t_dvr = ld(srawT[0:16, 0:128],
                   bass.AP(Dv_d.tensor, 0, [[128, 16], [1, 128]]),
                   waits=[t_cbt])
        t_dvt = small_transpose(sDVT[:], srawT[0:16, 0:128], 16, t_dvr)

        t_wssm_pe = None
        for hb in range(16):
            tw = P.op("tensor", lambda t, hb=hb: t.transpose(
                pb[6][0:128, 0:32], swsraw[0:N, 128 * hb:128 * (hb + 1)],
                sidf[0:32, 0:32]), sem="pe",
                waits=[t_wsr, t_idf, bank_r[6]])
            bank_w[6] = tw
            t_wssm_pe = tw
            evac("vector", lambda v, hb=hb: v.tensor_copy(
                wssmT[:, 32 * hb:32 * (hb + 1)], pb[6][0:128, 0:32]), 6)

        for h2 in range(2):
            mm_group([(pb[h2][0:128, 0:512], sones[0:1, :],
                       sbrow[0:1, 512 * h2:512 * (h2 + 1)])], h2,
                     extra_waits=[t_brow, t_ones])
            evac("vector", lambda v, h2=h2: v.tensor_copy(
                sBOUT[:, 512 * h2:512 * (h2 + 1)], pb[h2][0:128, 0:512]), h2)

        t0 = P.op("vector", lambda v: v.tensor_copy(
            cb[0][:, 0:N], sidf[0:N, 0:N]), waits=[t_idf])
        t0b = P.op("vector", lambda v: v.tensor_copy(
            cb[0][:, N:N + 1], scv), sem="dv", waits=[t_cv])
        P.op("vector", lambda v: v.tensor_copy(cmatT[:, 0:1], scv))
        P.op("vector", lambda v: v.tensor_copy(
            mr[96:128, 31 * N:32 * N], sidf[0:N, 0:N]))
        chain_state = {"prev": t0b}

        def chain_step(k):
            tw = P.op("tensor", lambda t, k=k: t.matmul(
                pb[7][0:N, 0:N + 1], sA[:], cb[k % 2][:],
                start=True, stop=True),
                sem="pe", waits=[t_a, chain_state["prev"], bank_r[7]])
            bank_w[7] = tw
            te = evac("vector", lambda v, k=k: v.tensor_copy(
                cb[(k + 1) % 2][:], pb[7][0:N, 0:N + 1]), 7)
            kk = k + 1
            P.op("vector", lambda v, kk=kk: v.tensor_copy(
                cmatT[:, kk:kk + 1], cb[kk % 2][:, N:N + 1]), waits=[te])
            if kk <= T - 1:
                q = T - 1 - kk
                kc, j = q // 4, q % 4
                P.op("vector", lambda v, kk=kk, kc=kc, j=j: v.tensor_copy(
                    mr[32 * j:32 * (j + 1), N * kc:N * (kc + 1)],
                    cb[kk % 2][:, 0:N]))
            elif kk == T:
                P.op("vector", lambda v, kk=kk: v.tensor_copy(
                    sPm[:], cb[kk % 2][:, 0:N]))
            chain_state["prev"] = te

        chain_steps = [lambda k=k: chain_step(k) for k in range(T + 1)]

        def emit_btz(t_zero):
            P.wait("vector", t_zero)
            P.wait("vector", chain_state["prev"])
            t_btz = None
            for q in range(T):
                kc, j = q // 4, q % 4
                t_btz = P.op("vector", lambda v, kc=kc, j=j, q=q:
                             v.tensor_copy(
                    btz[32 * j:32 * (j + 1), T * kc + q:T * (kc + 1)],
                    cmatT[:, 0:T - q]), sem="dv" if q == T - 1 else None)
            return t_btz

        stage_a = pool[2].bitcast(F32)
        stage_b = pool[2]
        bst = {"store": [None, None], "first": True}
        wit_tok = [None] * 16
        wit_ready = [None] * 16
        wit_mm_tok = [None, None]
        wout_done = [None]

        wit_steps = []
        for hb_ in range(16):
            for half_ in range(2):
                def wstep(hb=hb_, half=half_):
                    w = half
                    zt = hb + 16 * half
                    extra = [t_wssm_pe] if bst["first"] else []
                    bst["first"] = False
                    t_l = P.op("sync", lambda s, zt=zt, w=w: s.dma_start(
                        stage_a[:, 2048 * w:2048 * w + 1024],
                        w_in[128 * zt:128 * (zt + 1), :]), sem="sd",
                        amt=16, waits=[bst["store"][w]] + extra)
                    ring = hb % 2
                    for g2 in range(2):
                        bk = 4 + g2
                        for dq in range(4):
                            dblk = 4 * g2 + dq
                            tw = P.op("tensor", lambda t, w=w, dblk=dblk,
                                      dq=dq, bk=bk: t.transpose(
                                pb[bk][0:128, 128 * dq:128 * (dq + 1)],
                                stage_a[:, 2048 * w + 128 * dblk:
                                        2048 * w + 128 * (dblk + 1)],
                                sidf[:]),
                                sem="pe" if dq == 3 else None,
                                waits=([t_l, t_idf, bank_r[bk]]
                                       if dq == 0 else []))
                        bank_w[bk] = tw
                        t_e = evac("vector", lambda v, ring=ring, half=half,
                                   bk=bk, g2=g2: v.tensor_copy(
                            swit[ring][:, 1024 * half + 512 * g2:
                                       1024 * half + 512 * (g2 + 1)],
                            pb[bk][0:128, 0:512]), bk,
                            extra_waits=[wit_mm_tok[ring]] if g2 == 0 else ())
                    t_st = P.op("sync", lambda s, hb=hb, half=half, ring=ring:
                                s.dma_start(
                        bass.AP(scr_wit.tensor,
                                hb * 128 * 2048 + half * 1024,
                                [[2048, 128], [1, 1024]]),
                        swit[ring][:, 1024 * half:1024 * half + 1024]),
                        sem="sd", amt=16, waits=[t_e])
                    bst["store"][w] = t_st
                    if half == 1:
                        wit_tok[hb] = t_st
                        wit_ready[hb] = t_e
                wit_steps.append(wstep)

        wout_steps = []
        for dt_ in range(8):
            def wostep(dt=dt_):
                w = dt % 2
                t_l = P.op("sync", lambda s, dt=dt, w=w: s.dma_start(
                    stage_a[:, 2048 * w:2048 * (w + 1)],
                    w_out[128 * dt:128 * (dt + 1), :]), sem="sd", amt=16,
                    waits=[bst["store"][w]])
                t_e = None
                for g4 in range(4):
                    bk = 4 + g4 % 2
                    for dq in range(4):
                        hb = 4 * g4 + dq
                        tw = P.op("tensor", lambda t, hb=hb, dq=dq, bk=bk,
                                  w=w: t.transpose(
                            pb[bk][0:128, 128 * dq:128 * (dq + 1)],
                            stage_a[:, 2048 * w + 128 * hb:
                                    2048 * w + 128 * (hb + 1)], sidf[:]),
                            sem="pe" if dq == 3 else None,
                            waits=([t_l, bank_r[bk]] if dq == 0 else []))
                    bank_w[bk] = tw
                    t_e = evac("vector", lambda v, w=w, bk=bk, g4=g4:
                               v.tensor_copy(
                        stage_b[:, 8192 + 2048 * w + 512 * g4:
                                8192 + 2048 * w + 512 * (g4 + 1)],
                        pb[bk][0:128, 0:512]), bk)
                t_st = P.op("sync", lambda s, dt=dt, w=w: s.dma_start(
                    bass.AP(scr_wout.tensor, dt * 128,
                            [[1024, 128], [128 * 1024, 16], [1, 128]]),
                    stage_b[:, 8192 + 2048 * w:8192 + 2048 * (w + 1)]),
                    sem="sd", amt=16, waits=[t_e])
                bst["store"][w] = t_st
                if dt == 7:
                    wout_done[0] = t_st
            wout_steps.append(wostep)

        pump_state = {"wit": 0, "wout": 0, "chain": 0}

        def pump(name, steps, n):
            i = pump_state[name]
            while n > 0 and i < len(steps):
                steps[i]()
                i += 1
                n -= 1
            pump_state[name] = i

        pool_rot = [0]
        pool_tok = [None, None, None]
        pool_stok = [None, None, None]
        w1_scr_tok = [None] * E
        w2_scr_tok = [None] * E
        swit_tok = [None] * 3
        t_sht_reader = None
        t_sout_reader = None
        t_p1_reader = None
        t_e_bufs_reader = None
        t_xnt_reader = None
        t_svr_reader = None
        t_ex_prev = None

        WOUT_AP = bass.AP(scr_wout.tensor, 0,
                          [[1024, 128], [128 * 1024, 16], [1, 1024]])

        def w1_ap(e, t):
            return bass.AP(t.tensor, e * D * FF,
                           [[2048, 128], [128 * 2048, 8], [1, 2048]])

        def w2_ap(e, t):
            return bass.AP(t.tensor, e * FF * D,
                           [[1024, 128], [128 * 1024, 16], [1, 1024]])

        def pool_load(src_ap, waits=(), store_to=None, eng="gpsimd",
                      sem=None):
            slot = pool_rot[0] % 3
            pool_rot[0] += 1
            tok = P.op(eng, lambda g, s=slot, src=src_ap: g.dma_start(
                pool[s][:, 0:16384], src), sem=f"ap{slot}", amt=16,
                waits=list(waits) + [pool_tok[slot], pool_stok[slot]])
            stok = None
            if store_to is not None:
                stok = P.op("sync", lambda s, sl=slot, d=store_to:
                            s.dma_start(d, pool[sl][:, 0:16384]),
                            sem="sd", amt=16, waits=[tok])
            pool_stok[slot] = stok
            return slot, tok, stok

        def dump(dst, src_ap, waits):
            return P.op("gpsimd", lambda g, d=dst, s=src_ap: g.dma_start(d, s),
                        sem="gd", amt=16, waits=waits)

        front_tok = {}
        scrxn_reader = {}

        def emit_xload(c, btz_reader_waits):
            return P.op("gpsimd", lambda g, c=c: g.dma_start(
                btz[:].rearrange("p (t d) -> p t d", t=NT),
                bass.AP(x_d.tensor, c * CH * D,
                        [[1024, 128], [128 * 1024, NT], [1, 1024]])),
                sem="gz", amt=16, waits=list(btz_reader_waits))

        def emit_front_stats(c, sq_waits, t_x):
            t_red = None
            t_sqa = None
            for t in range(NT):
                xs = btz[:, 1024 * t:1024 * (t + 1)]
                t_red = P.op("vector", lambda v, xs=xs, t=t: v.tensor_reduce(
                    sregs[:, t:t + 1], xs, AX.X, ALU.add),
                    sem="dv" if t == NT - 1 else None,
                    waits=[t_x])
                P.op("scalar", lambda s, xs=xs, t=t: s.activation(
                    su[0][:], xs[:, 0:512], AF.Square,
                    accum_out=sregs[:, 4 + t:5 + t]),
                    waits=[t_x] + (list(sq_waits) if t == 0 else []))
                t_sqa = P.op("scalar", lambda s, xs=xs, t=t: s.activation(
                    su[1][:], xs[:, 512:1024], AF.Square,
                    accum_out=sregs2[:, 4 + t:5 + t]),
                    sem="ac" if t == NT - 1 else None)
            su_silu_tok[0] = t_sqa
            su_silu_tok[1] = t_sqa
            tc = P.op("vector", lambda v: v.tensor_scalar(
                sregs2[:, 0:4], sregs[:, 0:4], 1.0 / D, None, ALU.mult),
                sem="dv", waits=[t_red])
            tc = P.op("vector", lambda v: v.scalar_tensor_tensor(
                sregs[:, 0:4], sregs2[:, 0:4], -1.0, sregs2[:, 0:4],
                ALU.mult, ALU.mult), sem="dv", waits=[tc])
            tc2 = P.op("vector", lambda v: v.tensor_tensor(
                sregs[:, 4:8], sregs[:, 4:8], sregs2[:, 4:8], ALU.add),
                sem="dv", waits=[t_sqa])
            tc2 = P.op("vector", lambda v: v.tensor_scalar(
                sregs[:, 4:8], sregs[:, 4:8], 1.0 / D, None, ALU.mult),
                sem="dv", waits=[tc2])
            tc = P.op("vector", lambda v: v.tensor_tensor(
                sregs2[:, 4:8], sregs[:, 4:8], sregs[:, 0:4], ALU.add),
                sem="dv", waits=[tc, tc2])
            tc = P.op("vector", lambda v: v.tensor_scalar(
                sregs2[:, 4:8], sregs2[:, 4:8], 1e-5, None, ALU.add),
                sem="dv", waits=[tc])
            tc = P.op("scalar", lambda s: s.activation(
                sregs[:, 4:8], sregs2[:, 4:8], AF.Sqrt), sem="ac",
                waits=[tc])
            t_rstd = P.op("vector", lambda v: v.reciprocal(
                sregs[:, 4:8], sregs[:, 4:8]), sem="dv", waits=[tc])
            front_tok["t_x"] = t_x
            front_tok["t_sq_last"] = t_sqa
            front_tok["t_rstd"] = t_rstd

        def emit_front_tile(c, t, first=False):
            t_rstd = front_tok["t_rstd"]
            par = c % 2
            xs = btz[:, 1024 * t:1024 * (t + 1)]
            t_xn = P.op("vector", lambda v, xs=xs, t=t: v.tensor_scalar(
                sxn[:, 1024 * t:1024 * (t + 1)], xs,
                sregs2[:, t:t + 1], sregs[:, 4 + t:5 + t],
                ALU.subtract, ALU.mult), sem="dv",
                waits=[t_rstd] +
                ([front_tok.get("t_xnst")] if first else []))
            t_xnst = P.op("gpsimd", lambda g, t=t, par=par: g.dma_start(
                scr_xn[par * CH + T * t:par * CH + T * (t + 1), :],
                sxn[:, 1024 * t:1024 * (t + 1)]),
                sem=f"gn{par}", amt=16,
                waits=[t_xn] + list(scrxn_reader.get(par, ())))
            t_xnt_all = None
            for g2 in range(2):
                bk = 4 + g2
                for dq in range(4):
                    dblk = 4 * g2 + dq
                    tw = P.op("tensor", lambda tn, t=t, dblk=dblk,
                              dq=dq, bk=bk: tn.transpose(
                        pb[bk][0:128, 128 * dq:128 * (dq + 1)],
                        sxn[:, 1024 * t + 128 * dblk:
                            1024 * t + 128 * (dblk + 1)], sidf[:]),
                        sem="pe" if dq == 3 else None,
                        waits=([t_xn, bank_r[bk]] if dq == 0 else []))
                bank_w[bk] = tw
                t_xnt_all = evac(
                    "vector", lambda v, t=t, bk=bk, g2=g2: v.tensor_copy(
                        xnt[:].rearrange("p (d ch) -> p d ch", d=8)
                        [:, 4 * g2:4 * (g2 + 1), T * t:T * (t + 1)],
                        pb[bk][0:128, 0:512].rearrange(
                            "p (d ch) -> p d ch", d=4)),
                    bk, extra_waits=[front_tok.get("xnt_reader")])
            front_tok["t_xn_last"] = t_xn
            front_tok["t_xnt_all"] = t_xnt_all
            front_tok["t_xnst"] = t_xnst

        t_btz = None
        t_mm2_prev = None
        t_iog_reader = None
        t_tab_reader = None
        t_stab_reader = None
        t_xgt_cur = None
        xg_tok = [None, None]
        t_yab_last = None
        t_srow_prev = None
        yab_fin = [None, None]
        t_smaskT_war = None
        su_scrib = {"t": None}
        su_silu_tok = [None, None]
        xgt_tok = [None, None]
        last_ga = [None, None]
        yp_tok = [None, None]
        yab_tok = [None] * 4
        t_yscat_all = None
        for c in range(n_chunks):
            if c == 0:
                t_x0 = emit_xload(0, [None])
                emit_front_stats(0, [None], t_x0)
                for _t in range(NT):
                    emit_front_tile(0, _t, first=(_t == 0))
            t_xnt_all = front_tok["t_xnt_all"]
            t_ln_last = front_tok["t_xn_last"]
            t_sq_last = front_tok["t_sq_last"]
            if c == 0:
                w1_slot, t_w1, w1_scr_tok[0] = pool_load(
                    w1_ap(0, w1), store_to=w1_ap(0, scr_w1))
            else:
                wout_slot, t_wout, _ = pool_load(
                    WOUT_AP, [wout_done[0]], eng="scalar")
                w1_slot, t_w1, _ = pool_load(
                    w1_ap(0, scr_w1), [w1_scr_tok[0]],
                    eng="scalar")
                t_btz = P.op("sync", lambda s: s.dma_start(
                    btz[:], scr_btz), sem="sd", amt=16,
                    waits=[t_ln_last, t_sq_last, t_scrbtz])

            t_ssm_group = None
            t_p1_last = None
            su_tok = [None] * 4
            sub_tok = [None] * 4
            sub_silu_tok = [None] * 4
            spad_tok = [None, None]
            ssg_tok = [None, None]
            for hb in range(16):
                if c == 0:
                    pump("wit", wit_steps, 2)
                    pump("chain", chain_steps, 9)
                slot = hb % 2
                if c == 0:
                    t_wl = wit_ready[hb]
                else:
                    t_wl = P.op("sync", lambda s, hb=hb, slot=slot:
                                s.dma_start(
                        swit[slot][:],
                        bass.AP(scr_wit.tensor, hb * 128 * 2048,
                                [[2048, 128], [1, 2048]])),
                        sem=f"aw{slot}", amt=16,
                        waits=[swit_tok[slot], wit_tok[hb]])
                pba, pbg = (0, 1) if hb % 2 == 0 else (2, 3)
                t_xh = mm_group(
                    [(pb[pba][0:128, 0:512],
                      swit[slot][:, 128 * dblk:128 * (dblk + 1)],
                      xnt[:, CH * dblk:CH * (dblk + 1)])
                     for dblk in range(8)],
                    pba, extra_waits=[t_wl, t_xnt_all])
                t_gt = mm_group(
                    [(pb[pbg][0:128, 0:512],
                      swit[slot][:, 1024 + 128 * dblk:
                                 1024 + 128 * (dblk + 1)],
                      xnt[:, CH * dblk:CH * (dblk + 1)])
                     for dblk in range(8)],
                    pbg, extra_waits=[t_wl, t_xnt_all])
                swit_tok[slot] = t_gt
                wit_mm_tok[slot] = t_gt
                w = hb % 2
                t_xhe = evac("scalar", lambda s, hb=hb, w=w, pba=pba:
                             s.activation(
                    spad[w][:, 3:3 + CH], pb[pba][0:128, 0:512],
                    AF.Identity, bias=sBINT[:, hb:hb + 1]), pba,
                    extra_waits=[spad_tok[w], t_bint])
                t_sge = evac("scalar", lambda s, hb=hb, w=w, pbg=pbg:
                             s.activation(
                    ssg[w][:], pb[pbg][0:128, 0:512],
                    AF.Silu, bias=sBINT[:, 16 + hb:17 + hb]), pbg,
                    extra_waits=[ssg_tok[w]])
                P.op("vector", lambda v, hb=hb, w=w: v.tensor_copy(
                    spad[w][:, 0:3], shalo[:, 3 * hb:3 * (hb + 1)]),
                    waits=[t_xhe, t_halo0])
                P.op("vector", lambda v, hb=hb, w=w: v.tensor_copy(
                    shalo[:, 3 * hb:3 * (hb + 1)], spad[w][:, CH:CH + 3]))
                P.op("vector", lambda v, hb=hb, w=w: v.tensor_scalar(
                    su[w][:], spad[w][:, 0:CH], sCWS[:, 4 * hb:4 * hb + 1],
                    None, ALU.mult), waits=[t_cws, su_silu_tok[w]])
                t_conv = None
                for k in range(1, 4):
                    t_conv = P.op("vector", lambda v, hb=hb, w=w, k=k:
                                  v.scalar_tensor_tensor(
                        su[w][:], spad[w][:, k:k + CH],
                        sCWS[:, 4 * hb + k:4 * hb + k + 1], su[w][:],
                        ALU.mult, ALU.add),
                        sem="dv" if k == 3 else None)
                spad_tok[w] = t_conv
                t_us = P.op("scalar", lambda s, hb=hb, w=w: s.activation(
                    sub[hb % 4][:], su[w][:], AF.Silu,
                    bias=sCBT[:, hb:hb + 1]), sem="ac",
                    waits=[t_conv, t_cbt, sub_tok[hb % 4], su_tok[hb % 4]])
                if hb >= 2:
                    hs = hb - 2
                    t_ssm_group = P.op("tensor", lambda t, hs=hs: t.matmul(
                        pb[6][0:N, 0:512], wssmT[:, N * hs:N * (hs + 1)],
                        sub[hs % 4][:], start=(hs == 0), stop=False),
                        sem="pe",
                        waits=[sub_silu_tok[hs % 4]] +
                        ([bank_r[6]] if hs == 0 else []))
                    sub_tok[hs % 4] = t_ssm_group
                t_p1w = P.op("vector", lambda v, hb=hb, w=w: v.tensor_tensor(
                    sp1[:, CH * hb:CH * (hb + 1)], sub[hb % 4][:], ssg[w][:],
                    ALU.mult), sem="dv",
                    waits=[t_us, t_sge] +
                    [t_e_bufs_reader, t_p1_reader])
                su_tok[hb % 4] = t_p1w
                sub_silu_tok[hb % 4] = t_us
                su_silu_tok[w] = t_us
                ssg_tok[w] = t_p1w
                t_p1_last = t_p1w

            for hs in (14, 15):
                t_ssm_group = P.op("tensor", lambda t, hs=hs: t.matmul(
                    pb[6][0:N, 0:512], wssmT[:, N * hs:N * (hs + 1)],
                    sub[hs % 4][:], start=False, stop=(hs == 15)),
                    sem="pe", waits=[sub_silu_tok[hs % 4]])
                sub_tok[hs % 4] = t_ssm_group
            bank_w[6] = t_ssm_group
            t_v = evac("scalar", lambda s: s.activation(
                sVT[:], pb[6][0:N, 0:512], AF.Identity,
                scale=sbv, bias=sbsbv), 6, extra_waits=[t_bsbv])
            if c == 0:
                pump("chain", chain_steps, T + 1)
                t_z = P.op("gpsimd", lambda g: g.memset(btz[:], 0.0),
                           sem="gp", waits=[t_ln_last, t_sq_last])
                t_btz = emit_btz(t_z)
                t_scrbtz = P.op("sync", lambda s: s.dma_start(
                    scr_btz, btz[:]), sem="sd", amt=16, waits=[t_btz])
                pump("wout", wout_steps, 8)
                pool_tok[2] = wout_done[0]
                wout_slot, t_wout, _ = pool_load(WOUT_AP, [wout_done[0]])
            if dbg and c == 0:
                t_dmp = None
                t_dmp = dump(dbg_p1, sp1[:], [t_p1_last, t_dmp])
                t_dmp = dump(dbg_vt, sVT[0:32, 0:512], [t_v, t_dmp])

            t_vr = None
            for j in range(4):
                t_vr = P.op("vector", lambda v, j=j: v.tensor_copy(
                    svr[32 * j:32 * (j + 1), :].rearrange(
                        "p (kc i) -> p kc i", kc=32),
                    bass.AP(sVT, j, [[CH, 32], [4, 32], [T, 4]])),
                    sem="dv" if j == 3 else None,
                    waits=[t_v] + ([t_svr_reader] if j == 0 else []))
            t_umm = mm_group(
                [(pb[5][0:NT, 0:N], svr[:, 4 * kc:4 * (kc + 1)],
                  mr[:, N * kc:N * (kc + 1)]) for kc in range(32)],
                5, extra_waits=[t_vr])

            t_ue = evac("vector", lambda v: v.tensor_copy(
                sU[:], pb[5][0:NT, 0:N]), 5)
            tw = P.op("tensor", lambda t: t.transpose(
                pb[5][0:N, 256:256 + NT], sU[:], sidf[0:NT, 0:NT]),
                sem="pe", waits=[t_ue, bank_r[5]])
            bank_w[5] = tw
            t_ut = evac("vector", lambda v: v.tensor_copy(
                sUT[:, 0:NT], pb[5][0:N, 256:256 + NT]), 5)
            P.wait("vector", t_ut)
            for i in range(NT):
                g = c * NT + i
                tw = P.op("tensor", lambda t, g=g: t.matmul(
                    pb[5][0:N, 300:301], sPm[:], sHT[:, g:g + 1],
                    start=True, stop=True), sem="pe",
                    waits=[t_ut, t_ht0, bank_r[5]])
                bank_w[5] = tw
                evac("vector", lambda v, g=g, i=i: v.tensor_tensor(
                    sHT[:, g + 1:g + 2], pb[5][0:N, 300:301],
                    sUT[:, i:i + 1], ALU.add), 5)
            t_hcols = bank_r[5]
            P.waits("tensor", [t_vr, bank_r[4]])
            for kc in range(32):
                P.op("tensor", lambda t, kc=kc: t.matmul(
                    pb[4][0:NT, 0:T], svr[:, 4 * kc:4 * (kc + 1)],
                    btz[:, T * kc:T * (kc + 1)], start=(kc == 0),
                    stop=False), waits=[t_btz] if kc == 0 else ())
            t_smm = P.op("tensor", lambda t, c=c: t.matmul(
                pb[4][0:NT, 0:T], sHT[:, c * NT:c * NT + NT],
                cmatT[:, 1:1 + T], start=False, stop=True), sem="pe",
                waits=[t_hcols])
            bank_w[4] = t_smm
            t_svr_reader = t_smm
            t_se = evac("vector", lambda v: v.tensor_copy(
                sS[:], pb[4][0:NT, 0:T]), 4,
                extra_waits=[t_srow_prev])
            t_srow_prev = None
            t_srow = P.op("sync", lambda s: s.dma_start(
                s_row[0:1, :], sS[:]), sem="ar", amt=16, waits=[t_se])
            t_srow_prev = t_srow
            t_lg = None
            P.waits("tensor", [t_ex_prev, bank_r[7]])
            for t in range(NT):
                t_lg = mm_group(
                    [(pb[7][0:128, 8 * t:8 * (t + 1)],
                      xnt[:, CH * dblk + T * t:CH * dblk + T * (t + 1)],
                      sWg[:, 8 * dblk:8 * (dblk + 1)]) for dblk in range(8)],
                    7, extra_waits=[t_wg, t_xnt_all])
            pb7v = pb[7][0:128, 0:32].rearrange("p (t e) -> p t e", t=NT)
            tr = P.op("vector", lambda v: v.tensor_reduce(
                sregs[:, 0:4], pb7v, AX.X, ALU.max, negate=True),
                sem="dv", waits=[t_lg])
            neg_b = sregs[:, 0:4].broadcast_to((128, NT, 8))
            tr = P.op("vector", lambda v: v.tensor_tensor(
                srt[0][:].rearrange("p (t e) -> p t e", t=NT), pb7v,
                neg_b, ALU.add), sem="dv", waits=[tr])
            t_ex = P.op("scalar", lambda s: s.activation(
                srt[1][:], srt[0][:], AF.Exp), sem="ac", waits=[tr])
            t_ex_prev = t_ex
            bank_r[7] = t_ex
            tr = P.op("vector", lambda v: v.tensor_reduce(
                sregs[:, 0:4], srt[1][:].rearrange("p (t e) -> p t e", t=NT),
                AX.X, ALU.add), sem="dv", waits=[t_ex])
            tr = P.op("vector", lambda v: v.reciprocal(
                sregs[:, 0:4], sregs[:, 0:4]), sem="dv", waits=[tr])
            rec_b = sregs[:, 0:4].broadcast_to((128, NT, 8))
            tr = P.op("vector", lambda v: v.tensor_tensor(
                srt[1][:].rearrange("p (t e) -> p t e", t=NT),
                srt[1][:].rearrange("p (t e) -> p t e", t=NT),
                rec_b, ALU.mult), sem="dv", waits=[tr])
            tr = P.op("vector", lambda v: v.tensor_reduce(
                sregs[:, 0:4], srt[1][:].rearrange("p (t e) -> p t e", t=NT),
                AX.X, ALU.max), sem="dv", waits=[tr])
            m1_b = sregs[:, 0:4].broadcast_to((128, NT, 8))
            tr = P.op("vector", lambda v: v.tensor_tensor(
                srt[2][:].rearrange("p (t e) -> p t e", t=NT),
                srt[1][:].rearrange("p (t e) -> p t e", t=NT),
                m1_b, ALU.is_equal), sem="dv", waits=[tr])
            tsc = None
            for t in range(NT):
                tsc = P.op("vector", lambda v, t=t: v.tensor_tensor_scan(
                    srt[3][:, 8 * t:8 * (t + 1)], sones8[:],
                    srt[2][:, 8 * t:8 * (t + 1)], 0.0, ALU.mult, ALU.add),
                    sem="dv" if t == NT - 1 else None,
                    waits=[tr, t_ones8])
            tr = P.op("vector", lambda v: v.tensor_scalar(
                srt[3][:], srt[3][:], 1.0, None, ALU.is_equal),
                sem="dv", waits=[tsc])
            tr = P.op("vector", lambda v: v.tensor_tensor(
                srt[2][:], srt[2][:], srt[3][:], ALU.mult),
                sem="dv", waits=[tr])
            tsc1_tok = tr
            tr = P.op("vector", lambda v: v.scalar_tensor_tensor(
                srt[4][:], srt[2][:], -1e30, srt[1][:], ALU.mult, ALU.add),
                sem="dv", waits=[tr])
            tr = P.op("vector", lambda v: v.tensor_reduce(
                sregs2[:, 0:4], srt[4][:].rearrange("p (t e) -> p t e", t=NT),
                AX.X, ALU.max), sem="dv", waits=[tr])
            m2_b = sregs2[:, 0:4].broadcast_to((128, NT, 8))
            tr = P.op("vector", lambda v: v.tensor_tensor(
                srt[5][:].rearrange("p (t e) -> p t e", t=NT),
                srt[4][:].rearrange("p (t e) -> p t e", t=NT),
                m2_b, ALU.is_equal), sem="dv", waits=[tr])
            tsc = None
            for t in range(NT):
                tsc = P.op("vector", lambda v, t=t: v.tensor_tensor_scan(
                    srt[3][:, 8 * t:8 * (t + 1)], sones8[:],
                    srt[5][:, 8 * t:8 * (t + 1)], 0.0, ALU.mult, ALU.add),
                    sem="dv" if t == NT - 1 else None, waits=[tr])
            tr = P.op("vector", lambda v: v.tensor_scalar(
                srt[3][:], srt[3][:], 1.0, None, ALU.is_equal),
                sem="dv", waits=[tsc])
            t_mask2 = P.op("vector", lambda v: v.tensor_tensor(
                srt[5][:], srt[5][:], srt[3][:], ALU.mult),
                sem="dv", waits=[tr])
            t_mask1 = tsc1_tok
            tden = P.op("vector", lambda v: v.tensor_tensor(
                sregs2[:, 4:8], sregs[:, 0:4], sregs2[:, 0:4], ALU.add),
                sem="dv", waits=[t_mask2])
            tden = P.op("vector", lambda v: v.reciprocal(
                sregs2[:, 4:8], sregs2[:, 4:8]), sem="dv", waits=[tden])
            tw1 = P.op("vector", lambda v: v.tensor_tensor(
                swab[:, 0:4], sregs[:, 0:4], sregs2[:, 4:8], ALU.mult),
                sem="dv", waits=[tden])
            t_wab = P.op("vector", lambda v: v.tensor_tensor(
                swab[:, 4:8], sregs2[:, 0:4], sregs2[:, 4:8], ALU.mult),
                sem="dv", waits=[tw1])
            bank_r[7] = t_wab

            t_on = P.op("vector", lambda v: v.memset(
                su[1][0:8, 0:512], 1.0), sem="dv", waits=[su_silu_tok[1]])
            t_s12 = [None, None]
            for r in range(2):
                mk = srt[2] if r == 0 else srt[5]
                t_mk = t_mask1 if r == 0 else t_mask2
                bkx = 6 + r
                for t in range(NT):
                    tw = P.op("tensor", lambda tn, t=t, mk=mk, bkx=bkx:
                              tn.transpose(
                        pb[bkx][0:8, 128 * t:128 * (t + 1)],
                        mk[:, 8 * t:8 * (t + 1)], sidf[:]),
                        sem="pe" if t == NT - 1 else None,
                        waits=([t_mk, bank_r[bkx], t_smaskT_war if r == 0
                                else None] if t == 0 else []))
                bank_w[bkx] = tw
                t_mT = evac("vector", lambda v, bkx=bkx: v.tensor_copy(
                    smaskT, pb[bkx][0:8, 0:512]), bkx)
                tk = P.op("vector", lambda v: v.tensor_tensor_scan(
                    scum, su[1][0:8, 0:512], smaskT,
                    0.0, ALU.mult, ALU.add), sem="dv",
                    waits=[t_on, t_mT])
                if r == 0:
                    tk = P.op("vector", lambda v: v.tensor_tensor(
                        sb8[0:8, 1:2], scum[:, 511:512], sb8[0:8, 0:1],
                        ALU.add), sem="dv", waits=[tk, t_b8])
                tk = P.op("vector", lambda v, r=r: v.tensor_scalar(
                    scum, scum, sb8[0:8, r:r + 1], None,
                    ALU.add), sem="dv", waits=[tk])
                tk = P.op("vector", lambda v: v.tensor_scalar(
                    smaskT, scum, sb8[0:8, 2:3], None,
                    ALU.is_gt), sem="dv", waits=[tk])
                tk = P.op("vector", lambda v: v.scalar_tensor_tensor(
                    scum, smaskT, float(1 << 20), scum,
                    ALU.mult, ALU.add), sem="dv", waits=[tk])
                bky = 0 + r
                for t in range(NT):
                    tw = P.op("tensor", lambda tn, t=t, bky=bky:
                              tn.transpose(
                        pb[bky][0:128, 8 * t:8 * (t + 1)],
                        scum[:, 128 * t:128 * (t + 1)], sidf[0:8, 0:8]),
                        sem="pe" if t == NT - 1 else None,
                        waits=([tk, bank_r[bky]] if t == 0 else []))
                bank_w[bky] = tw
                t_s12[r] = evac("vector", lambda v, r=r, bky=bky: v.tensor_copy(
                    sslotm[:, 32 * r:32 * (r + 1)], pb[bky][0:128, 0:32]),
                    bky)
            su_scrib["t"] = tk
            tk2 = None
            for r in range(2):
                mk = srt[2] if r == 0 else srt[5]
                tmu = P.op("vector", lambda v, mk=mk, r=r: v.tensor_tensor(
                    srt[0][:], mk[:],
                    sslotm[:, 32 * r:32 * (r + 1)], ALU.mult), sem="dv",
                    waits=([t_s12[0], t_s12[1]] if r == 0 else [tk2]))
                tk2 = P.op("vector", lambda v, r=r: v.tensor_reduce(
                    sidxf[:, 4 * r:4 * (r + 1)],
                    srt[0][:].rearrange("p (t e) -> p t e", t=NT),
                    AX.X, ALU.add), sem="dv", waits=[tmu])
            P.op("vector", lambda v: v.tensor_scalar(
                sidxf[:, 8:12], siotl[:], 0.0, None, ALU.add),
                waits=[t_iotl])
            tk3 = P.op("vector", lambda v: v.tensor_scalar(
                sidxf[:, 12:16], siotl[:], 512.0, None, ALU.add), sem="dv")
            t_idxu = P.op("vector", lambda v: v.tensor_copy(
                sidxu[:, 0:8], sidxf[:, 0:8]), sem="dv", waits=[tk2, tk3])
            t_iog = P.op("gpsimd", lambda g, c=c: g.iota(
                siotg[:], [[128, 4]], base=(c % 2) * CH,
                channel_multiplier=1, allow_small_or_imprecise_dtypes=True),
                sem="gp", waits=[t_iog_reader])
            t_init1 = P.op("gpsimd", lambda g: g.dma_start(
                bass.AP(scr_idx.tensor, 0, [[16, 128], [1, 16]]), shug[:]),
                sem="gx", amt=16, waits=[t_hug, t_tab_reader])
            t_init2 = P.op("gpsimd", lambda g: g.dma_start(
                bass.AP(scr_idx2.tensor, 0, [[16, 128], [1, 16]]), shug[:]),
                sem="gx", amt=16)
            for r in range(2):
                for t in range(NT):
                    t_i1 = P.op("gpsimd", lambda g, r=r, t=t:
                                g.indirect_dma_start(
                        bass.AP(scr_idx.tensor, 0, [[1, 2048], [1, 1]]),
                        bass.IndirectOffsetOnAxis(
                            ap=sidxu[:, 4 * r + t:4 * r + t + 1], axis=0),
                        siotg[:, t:t + 1], None,
                        bounds_check=regs["b_idx"], oob_is_err=False),
                        sem="gx", amt=16,
                        waits=[t_idxu, t_iog, t_init1, t_init2])
                    t_i2 = P.op("gpsimd", lambda g, r=r, t=t:
                                g.indirect_dma_start(
                        bass.AP(scr_idx2.tensor, 0, [[1, 2048], [1, 1]]),
                        bass.IndirectOffsetOnAxis(
                            ap=sidxu[:, 4 * r + t:4 * r + t + 1], axis=0),
                        sidxf[:, 8 + 4 * r + t:9 + 4 * r + t], None,
                        bounds_check=regs["b_idx"], oob_is_err=False),
                        sem="gx", amt=16)
            stab16 = ssbc
            t_tl = P.op("sync", lambda s: s.dma_start(
                stab16[0:16, 0:128],
                bass.AP(scr_idx.tensor, 0, [[128, 16], [1, 128]])),
                sem="gt", amt=16,
                waits=[t_i1, t_i2, t_xgt_cur, t_yscat_all,
                       yp_tok[0], yp_tok[1], t_stab_reader])
            t_tl = P.op("sync", lambda s: s.dma_start(
                stab16[0:16, 128:256],
                bass.AP(scr_idx2.tensor, 0, [[128, 16], [1, 128]])),
                sem="gt", amt=16)
            for h2 in range(2):
                tw = P.op("tensor", lambda tn, h2=h2: tn.transpose(
                    pb[6 + h2][0:128, 448:464],
                    stab16[0:16, 128 * h2:128 * (h2 + 1)],
                    sidf[0:16, 0:16]), sem="pe",
                    waits=[t_tl, bank_r[6 + h2]])
                bank_w[6 + h2] = tw
                evac("vector", lambda v, h2=h2: v.tensor_copy(
                    stabT[:, 16 * h2:16 * (h2 + 1)],
                    pb[6 + h2][0:128, 448:464]), 6 + h2)
            t_tab = P.op("vector", lambda v: v.tensor_copy(
                stabTi[:], stabT[:]), sem="dv",
                waits=[bank_r[6], bank_r[7], last_ga[0], last_ga[1],
                       yp_tok[0], yp_tok[1]])
            t_stab_reader = tw
            t_tab_reader = t_tl
            t_iog_reader = t_i1
            stabTu = stabTi
            if dbg and c == 0:
                t_dmp = dump(dbg_idx, sidxf[:], [t_idxu, t_dmp])
                t_dmp = dump(dbg_wab, swab[:], [t_wab, t_dmp])
                t_dmp = dump(dbg_m12a, srt[2][:], [t_mask1, t_dmp])
                t_dmp = dump(dbg_m12b, srt[5][:], [t_mask2, t_dmp])
                t_dmp = dump(dbg_slot, sslotm[:, 0:64],
                             [t_s12[0], t_s12[1], t_dmp])
                t_dmp = dump(dbg_tab, stabT[:], [t_tab, t_dmp])
            mm_group([(pb[6][0:128, 0:512], sones[0:1, :], s_row[0:1, :])],
                     6, extra_waits=[t_srow, t_ones])
            t_sbc = evac("vector", lambda v: v.tensor_copy(
                ssbc[:], pb[6][0:128, 0:512]), 6)

            if dbg and c == 0:
                t_dmp = dump(dbg_sbc, ssbc[:], [t_sbc, t_dmp])
                t_dmp = dump(dbg_ht, sHT[0:32, 0:33], [t_hcols, t_dmp])
            if c < n_chunks - 1:
                t_x_next = emit_xload(
                    c + 1, [t_svr_reader, t_scrbtz if c == 0 else None])
            t_mm2_last = None
            t_out_stt = [None] * NT

            def emit_xgt(e):
                nonlocal_toks = []
                for ti in range(2):
                    g = 2 * e + ti
                    xgs = sxg[:, 2048 + 1024 * ti:2048 + 1024 * (ti + 1)]
                    t_ga = P.op("gpsimd", lambda gq, g=g, xgs=xgs:
                                gq.indirect_dma_start(
                        xgs, None, scr_xn,
                        bass.IndirectOffsetOnAxis(
                            ap=stabTu[:, g:g + 1], axis=0),
                        bounds_check=regs["b_tok"], oob_is_err=False),
                        sem=f"gi{ti}", amt=16,
                        waits=[t_tab, xg_tok[ti], t_arena1_z,
                               yab_tok[2 * ti], yab_tok[2 * ti + 1],
                               front_tok["t_xnst"]])
                    last_ga[ti] = t_ga
                    for g2 in range(2):
                        bk = 6 + g2
                        for dq in range(4):
                            dblk = 4 * g2 + dq
                            tw_ = P.op("tensor", lambda tn, xgs=xgs,
                                       dblk=dblk, dq=dq, bk=bk:
                                       tn.transpose(
                                pb[bk][0:128, 128 * dq:128 * (dq + 1)],
                                xgs[:, 128 * dblk:128 * (dblk + 1)],
                                sidf[:]),
                                sem="pe" if dq == 3 else None,
                                waits=([t_ga, bank_r[bk]]
                                       if dq == 0 else []))
                        bank_w[bk] = tw_
                        tok_ = evac(
                            "vector", lambda v, e=e, ti=ti, g2=g2, bk=bk:
                            v.tensor_copy(
                                swit[e % 2][:].rearrange(
                                    "p (d cc) -> p d cc", d=8)
                                [:, 4 * g2:4 * (g2 + 1),
                                 128 * ti:128 * (ti + 1)],
                                pb[bk][0:128, 0:512].rearrange(
                                    "p (d cc) -> p d cc", d=4)),
                            bk, extra_waits=[xgt_tok[e % 2],
                                             wit_mm_tok[e % 2]])
                    nonlocal_toks.append(tok_)
                    xg_tok[ti] = tw_
                return nonlocal_toks[-1]

            def emit_mm1s(e, slot_, tok_, t_xgt):
                t_relu_last = None
                t_h = None
                for fb in range(16):
                    bk = 4 + fb % 2
                    t_h = mm_group(
                        [(pb[bk][0:128, 0:256],
                          pool[slot_][:, 2048 * dc + 128 * fb:
                                      2048 * dc + 128 * (fb + 1)],
                          swit[e % 2][:, 256 * dc:256 * (dc + 1)])
                         for dc in range(8)],
                        bk, extra_waits=[tok_, t_xgt])
                    t_relu_last = evac(
                        "scalar", lambda s, fb=fb, bk=bk: s.activation(
                            shts[:, 256 * fb:256 * (fb + 1)],
                            pb[bk][0:128, 0:256], AF.Relu), bk,
                        extra_waits=([t_mm2_prev] if fb == 0 else ()))
                pool_tok[slot_] = t_h
                xgt_tok[e % 2] = t_h
                return t_relu_last

            def emit_mm2s(e, slot_, tok_, t_relu, t_smix_dep):
                nonlocal t_yscat_all
                last = None
                for ti in range(2):
                    bks = (0, 1) if ti == 0 else (2, 3)
                    for h2 in range(2):
                        bk = bks[h2]
                        last = mm_group(
                            [(pb[bk][0:128, 0:512],
                              shts[:, 256 * fc + 128 * ti:
                                   256 * fc + 128 * (ti + 1)],
                              pool[slot_][:, 1024 * fc + 512 * h2:
                                          1024 * fc + 512 * (h2 + 1)])
                             for fc in range(16)],
                            bk, extra_waits=[t_relu, tok_] +
                            ([t_smix_dep] if e == 0 else []))
                        tk_ = evac("vector",
                                   lambda v, ti=ti, h2=h2, bk=bk:
                                   v.tensor_copy(
                            syp[:, 1024 * ti + 512 * h2:
                                1024 * ti + 512 * (h2 + 1)],
                            pb[bk][0:128, 0:512]), bk,
                            extra_waits=[yp_tok[ti]] if h2 == 0 else ())
                    g = 2 * e + ti
                    _ioa = bass.IndirectOffsetOnAxis(
                        ap=stabTu[:, 16 + g:17 + g], axis=0)
                    _vap = syp[:, 1024 * ti:1024 * (ti + 1)]
                    t_sc = P.op("gpsimd", lambda gq, _ioa=_ioa, _vap=_vap:
                                gq.indirect_dma_start(
                        scr_yab, _ioa, _vap, None,
                        bounds_check=regs["b_slot"], oob_is_err=False),
                        sem=f"gs{ti}", amt=16,
                        waits=[tk_, t_tab, yab_fin[0], yab_fin[1]])
                    yp_tok[ti] = t_sc
                    t_yscat_all = t_sc
                pool_tok[slot_] = last
                return last

            def load_w1(e):
                if c == 0:
                    s_, t_, w1_scr_tok[e] = pool_load(
                        w1_ap(e, w1), store_to=w1_ap(e, scr_w1))
                else:
                    s_, t_, _ = pool_load(w1_ap(e, scr_w1),
                                          [w1_scr_tok[e]],
                                          eng="sync")
                return s_, t_

            def load_w2(e):
                if c == 0:
                    s_, t_, w2_scr_tok[e] = pool_load(
                        w2_ap(e, w2), store_to=w2_ap(e, scr_w2))
                else:
                    s_, t_, _ = pool_load(w2_ap(e, scr_w2),
                                          [w2_scr_tok[e]],
                                          eng="sync")
                return s_, t_

            w2_slot, t_w2 = load_w2(0)
            t_xgt_cur = emit_xgt(0)
            t_xgt_next = None
            t_yin = None
            for hb in range(16):
                t_yin = P.op("vector", lambda v, hb=hb:
                             v.scalar_tensor_tensor(
                    sp1[:, CH * hb:CH * (hb + 1)], ssbc[:],
                    sDVT[:, hb:hb + 1], sp1[:, CH * hb:CH * (hb + 1)],
                    ALU.add, ALU.mult), sem="dv" if hb == 15 else None,
                    waits=[t_sbc, t_dvt] if hb == 0 else ())
            t_y = None
            t_smix_done = None
            for t in range(NT):
                bks = (0, 1) if t % 2 == 0 else (2, 3)
                for h2 in range(2):
                    bk = bks[h2]
                    t_y = mm_group(
                        [(pb[bk][0:128, 0:512],
                          sp1[:, CH * hb2 + T * t:CH * hb2 + T * (t + 1)],
                          pool[wout_slot][:, 1024 * hb2 + 512 * h2:
                                          1024 * hb2 + 512 * (h2 + 1)])
                         for hb2 in range(16)],
                        bk, extra_waits=[t_yin, t_wout])
                    evac("vector", lambda v, bk=bk, h2=h2: v.tensor_tensor(
                        stmp[:], pb[bk][0:128, 0:512],
                        sBOUT[:, 512 * h2:512 * (h2 + 1)], ALU.add), bk)
                    t_smix_done = P.op(
                        "vector", lambda v, t=t, h2=h2: v.tensor_tensor(
                            smix[:, 1024 * t + 512 * h2:
                                 1024 * t + 512 * (h2 + 1)],
                            stmp[:],
                            sxn[:, 1024 * t + 512 * h2:
                                1024 * t + 512 * (h2 + 1)], ALU.add),
                        sem="dv")
            if dbg and c == 0:
                t_dmp = dump(dbg_yin, sp1[:], [t_yin, t_dmp])
                t_dmp = dump(dbg_smix, smix[:], [t_smix_done, t_dmp])
            pool_tok[wout_slot] = t_y
            t_p1_reader = t_y if not (dbg and c == 0) else t_dmp

            cur_xgt = t_xgt_cur
            t_relu = emit_mm1s(0, w1_slot, t_w1, cur_xgt)
            nw1_slot, nt_w1 = load_w1(1)
            t_xgt_next = emit_xgt(1)
            for e in range(E):
                if e > 0:
                    t_relu = emit_mm1s(e, w1_slot, t_w1, cur_xgt)
                w2_next = None
                if e < E - 1:
                    w1_slot, t_w1 = nw1_slot, nt_w1
                    cur_xgt = t_xgt_next
                    w2_next = load_w2(e + 1)
                    if e < E - 2:
                        t_xgt_next = emit_xgt(e + 2)
                t_mm2_last = emit_mm2s(e, w2_slot, t_w2, t_relu,
                                       t_smix_done)
                if e < E - 1:
                    w2_slot, t_w2 = w2_next
                    if e < E - 2:
                        nw1_slot, nt_w1 = load_w1(e + 2)
            t_mm2_prev = t_mm2_last
            t_sht_reader = t_mm2_last
            scrxn_reader[c % 2] = (last_ga[0], last_ga[1])
            if c < n_chunks - 1:
                emit_front_stats(c + 1,
                                 [t_smix_done, su_scrib["t"], su_tok[2],
                                  su_tok[3], sub_silu_tok[2],
                                  sub_silu_tok[3]],
                                 t_x_next)
            if dbg and c == 0:
                t_dmp = dump(dbg_yab, scr_yab[:],
                             [t_yscat_all, yp_tok[0], yp_tok[1], t_dmp])
            if c < n_chunks - 1:
                for _t in (0, 2, 1, 3):
                    emit_front_tile(c + 1, _t, first=(_t == 0))
            t_store = None
            t_fin = None
            t_g = None
            t_fin_tiles = [None] * NT
            sxe_tok = [None, None]
            soutb_tok = [None, None]
            t_su_free = [su_silu_tok[0], su_silu_tok[1], t_smix_done,
                         su_scrib["t"], su_tok[2], su_tok[3],
                         sub_silu_tok[2], sub_silu_tok[3]]
            yab = ar1b
            first_e = True
            for t in (0, 2, 1, 3):
                w = t % 2
                r0 = c * CH + t * T
                ya_s = yab[:, 4096 + 2048 * w:4096 + 2048 * w + 1024]
                yb_s = yab[:, 4096 + 2048 * w + 1024:4096 + 2048 * (w + 1)]
                t_ya = P.op("scalar", lambda s, t=t, ya_s=ya_s: s.dma_start(
                    ya_s, scr_yab[128 * t:128 * (t + 1), :]),
                    sem=f"gy{w}", amt=16,
                    waits=[t_yscat_all, yp_tok[0], yp_tok[1],
                           yab_tok[2 * w], xg_tok[0], xg_tok[1]])
                t_yb = P.op("scalar", lambda s, t=t, yb_s=yb_s: s.dma_start(
                    yb_s, scr_yab[512 + 128 * t:512 + 128 * (t + 1), :]),
                    sem=f"gy{w}", amt=16, waits=[yab_tok[2 * w + 1]])
                t_xe = P.op("scalar", lambda s, r0=r0, w=w: s.dma_start(
                    sxe[:, 1024 * w:1024 * (w + 1)], x_d[r0:r0 + T, :]),
                    sem=f"gy{w}", amt=16,
                    waits=[sxe_tok[w], t_p1_reader])
                t_mul = [None, None]
                tko = None
                for h in range(2):
                    P.op("vector", lambda v, t=t, h=h, ya_s=ya_s:
                         v.tensor_scalar(
                        su[h][:], ya_s[:, 512 * h:512 * (h + 1)],
                        swab[:, t:t + 1], None, ALU.mult),
                        waits=[t_ya, t_yb, t_xe, t_wab] +
                        (t_su_free if first_e else []))
                    tko = P.op("vector", lambda v, t=t, h=h, yb_s=yb_s:
                               v.scalar_tensor_tensor(
                        su[h][:], yb_s[:, 512 * h:512 * (h + 1)],
                        swab[:, 4 + t:5 + t], su[h][:],
                        ALU.mult, ALU.add), sem="dv")
                first_e = False
                yab_tok[2 * w] = tko
                yab_tok[2 * w + 1] = tko
                t_out_stt[t] = tko
                for h in range(2):
                    t_g = P.op("scalar", lambda s, h=h: s.activation(
                        su[h][:], su[h][:], AF.Silu),
                        sem="ac", waits=[tko, t_sout_reader
                                         if (dbg and c == 0) else None])
                    su_silu_tok[h] = t_g
                    t_mul[h] = P.op("vector", lambda v, t=t, w=w, h=h:
                                    v.tensor_tensor(
                        soutb[:, 2048 + 1024 * w + 512 * h:
                              2048 + 1024 * w + 512 * (h + 1)],
                        smix[:, 1024 * t + 512 * h:
                             1024 * t + 512 * (h + 1)],
                        su[h][:], ALU.mult), sem="dv",
                        waits=[t_g, soutb_tok[0] if h == 0 else None,
                               soutb_tok[1] if h == 0 else None,
                               t_smix_done])
                t_fin = P.op("vector", lambda v, w=w: v.tensor_tensor(
                    soutb[:, 2048 + 1024 * w:2048 + 1024 * (w + 1)],
                    soutb[:, 2048 + 1024 * w:2048 + 1024 * (w + 1)],
                    sxe[:, 1024 * w:1024 * (w + 1)], ALU.add),
                    sem="dv", waits=[t_xe, t_mul[0], t_mul[1]])
                sxe_tok[w] = t_fin
                yab_fin[w] = t_xe
                t_fin_tiles[t] = t_fin
                t_store = P.op("scalar", lambda s, r0=r0, w=w: s.dma_start(
                    out_d[r0:r0 + T, :],
                    soutb[:, 2048 + 1024 * w:2048 + 1024 * (w + 1)]),
                    sem="go", amt=16, waits=[t_fin])
                soutb_tok[w] = t_store
            t_sout_reader = t_fin
            t_e_bufs_reader = t_store

            swit_tok[0] = xgt_tok[0]
            swit_tok[1] = xgt_tok[1]
            front_tok["xnt_reader"] = t_xnt_reader

        P.streams["sync"].append(
            lambda s: s.wait_ge(sems["sd"], P.cnt["sd"]))
        for _sn in ("gt", "gy0", "gy1", "ap0", "ap1", "ap2",
                    "aw0", "aw1", "ar"):
            if P.cnt[_sn]:
                P.streams["sync"].append(
                    lambda s, _sn=_sn: s.wait_ge(sems[_sn], P.cnt[_sn]))
        P.streams["gpsimd"].append(
            lambda g: g.wait_ge(sems["gd"], P.cnt["gd"]))
        for _sn in ("gi0", "gi1", "gs0", "gs1", "gx", "gn0", "gn1", "gz"):
            if P.cnt[_sn]:
                P.streams["gpsimd"].append(
                    lambda g, _sn=_sn: g.wait_ge(sems[_sn], P.cnt[_sn]))
        P.streams["scalar"].append(
            lambda s: s.wait_ge(sems["ad"], P.cnt["ad"]))
        P.streams["scalar"].append(
            lambda s: s.wait_ge(sems["go"], P.cnt["go"]))
        P.replay(block)
    return nc


_NC_CACHE = {}


def _get_nc():
    if "nc" not in _NC_CACHE:
        _NC_CACHE["nc"] = build_nc()
    return _NC_CACHE["nc"]


def kernel(L0_attn_x, L0_attn_W_in, L0_attn_b_in, L0_attn_conv_w,
           L0_attn_conv_b, L0_attn_W_ssm, L0_attn_b_ssm, L0_attn_A,
           L0_attn_Bv, L0_attn_Cv, L0_attn_Dv, L0_attn_W_out, L0_attn_b_out,
           L0_ffn_Wg, L0_ffn_W1, L0_ffn_W2):
    nc = _get_nc()
    wmap = {
        "w_in": np.ascontiguousarray(L0_attn_W_in, dtype=np.float32),
        "b_in": np.ascontiguousarray(L0_attn_b_in, dtype=np.float32),
        "conv_w": np.ascontiguousarray(L0_attn_conv_w, dtype=np.float32),
        "conv_b": np.ascontiguousarray(L0_attn_conv_b, dtype=np.float32),
        "w_ssm": np.ascontiguousarray(L0_attn_W_ssm, dtype=np.float32),
        "b_ssm": np.ascontiguousarray(L0_attn_b_ssm, dtype=np.float32),
        "A": np.ascontiguousarray(L0_attn_A, dtype=np.float32),
        "Bv": np.ascontiguousarray(L0_attn_Bv, dtype=np.float32),
        "Cv": np.ascontiguousarray(L0_attn_Cv, dtype=np.float32),
        "Dv": np.ascontiguousarray(L0_attn_Dv, dtype=np.float32),
        "w_out": np.ascontiguousarray(L0_attn_W_out, dtype=np.float32),
        "b_out": np.ascontiguousarray(L0_attn_b_out, dtype=np.float32),
        "wg": np.ascontiguousarray(L0_ffn_Wg, dtype=np.float32),
        "w1": np.ascontiguousarray(L0_ffn_W1, dtype=np.float32),
        "w2": np.ascontiguousarray(L0_ffn_W2, dtype=np.float32),
    }
    xs = np.ascontiguousarray(L0_attn_x, dtype=np.float32)
    in_maps = [dict(wmap, x=xs[i]) for i in range(8)]
    res = run_bass_kernel_spmd(nc, in_maps, core_ids=list(range(8)))
    return np.stack([r["out"] for r in res.results], axis=0)


# version 26
# speedup vs baseline: 1.0316x; 1.0316x over previous
import contextlib
import numpy as np

import concourse.bass as bass
import concourse.mybir as mybir
from concourse.alu_op_type import AluOpType as ALU
from concourse.bass_utils import run_bass_kernel_spmd

F32 = mybir.dt.float32
BF16 = mybir.dt.bfloat16
I32 = mybir.dt.int32
AF = mybir.ActivationFunctionType
AX = mybir.AxisListType

S = 4096
D = 1024
H = 2048
Z = 2 * H
N = 32
E = 8
FF = 2048
CH = 512
T = 128
NT = CH // T
NCH = S // CH


class Sched:

    ENGS = ("sync", "gpsimd", "tensor", "vector", "scalar")

    def __init__(self, nc, sems):
        self.nc = nc
        self.sems = sems
        self.cnt = {k: 0 for k in sems}
        self.streams = {e: [] for e in self.ENGS}
        self.last_wait = {e: {} for e in self.ENGS}

    def wait(self, eng, tok):
        if tok is None:
            return
        sem, val = tok
        if self.last_wait[eng].get(sem, 0) >= val:
            return
        self.last_wait[eng][sem] = val
        h = self.sems[sem]
        self.streams[eng].append(lambda e, h=h, val=val: e.wait_ge(h, val))

    def waits(self, eng, toks):
        best = {}
        for t in toks:
            if t is None:
                continue
            sem, val = t
            if val > best.get(sem, 0):
                best[sem] = val
        for sem, val in best.items():
            self.wait(eng, (sem, val))

    def op(self, eng, fn, sem=None, amt=1, waits=()):
        self.waits(eng, waits)
        if sem is None:
            self.streams[eng].append(lambda e, fn=fn: fn(e))
            return None
        self.cnt[sem] += amt
        tok = (sem, self.cnt[sem])
        h = self.sems[sem]

        def wrapped(e, fn=fn, h=h, amt=amt):
            fn(e).then_inc(h, amt)
        self.streams[eng].append(wrapped)
        return tok

    def replay(self, block):
        for eng in self.ENGS:
            stream = self.streams[eng]

            def make(stream):
                def body(h):
                    for fn in stream:
                        fn(h)
                return body
            getattr(block, eng)(make(stream))


def build_nc(n_chunks=NCH, dbg=False):
    nc = bass.Bass("TRN2", target_bir_lowering=False)
    S_ = n_chunks * CH

    x_d = nc.dram_tensor("x", [S_, D], F32, kind="ExternalInput").ap()
    w_in = nc.dram_tensor("w_in", [Z, D], F32, kind="ExternalInput").ap()
    b_in = nc.dram_tensor("b_in", [Z], F32, kind="ExternalInput").ap()
    conv_w = nc.dram_tensor("conv_w", [H, 1, 4], F32, kind="ExternalInput").ap()
    conv_b = nc.dram_tensor("conv_b", [H], F32, kind="ExternalInput").ap()
    w_ssm = nc.dram_tensor("w_ssm", [N, H], F32, kind="ExternalInput").ap()
    b_ssm = nc.dram_tensor("b_ssm", [N], F32, kind="ExternalInput").ap()
    A_d = nc.dram_tensor("A", [N, N], F32, kind="ExternalInput").ap()
    Bv_d = nc.dram_tensor("Bv", [N], F32, kind="ExternalInput").ap()
    Cv_d = nc.dram_tensor("Cv", [N], F32, kind="ExternalInput").ap()
    Dv_d = nc.dram_tensor("Dv", [H], F32, kind="ExternalInput").ap()
    w_out = nc.dram_tensor("w_out", [D, H], F32, kind="ExternalInput").ap()
    b_out = nc.dram_tensor("b_out", [D], F32, kind="ExternalInput").ap()
    wg = nc.dram_tensor("wg", [D, E], F32, kind="ExternalInput").ap()
    w1 = nc.dram_tensor("w1", [E, D, FF], F32, kind="ExternalInput").ap()
    w2 = nc.dram_tensor("w2", [E, FF, D], F32, kind="ExternalInput").ap()
    out_d = nc.dram_tensor("out", [S_, D], F32, kind="ExternalOutput").ap()

    if dbg:
        dbg_xn = nc.dram_tensor("dbg_xn", [128, 4096], F32, kind="ExternalOutput").ap()
        dbg_xnt = nc.dram_tensor("dbg_xnt", [128, 4096], F32, kind="ExternalOutput").ap()
        dbg_p1 = nc.dram_tensor("dbg_p1", [128, 8192], F32, kind="ExternalOutput").ap()
        dbg_vt = nc.dram_tensor("dbg_vt", [32, 512], F32, kind="ExternalOutput").ap()
        dbg_sbc = nc.dram_tensor("dbg_sbc", [128, 512], F32, kind="ExternalOutput").ap()
        dbg_ht = nc.dram_tensor("dbg_ht", [32, 33], F32, kind="ExternalOutput").ap()
        dbg_smix = nc.dram_tensor("dbg_smix", [128, 4096], F32, kind="ExternalOutput").ap()
        dbg_cw = nc.dram_tensor("dbg_cw", [128, 32], F32, kind="ExternalOutput").ap()
        dbg_out = nc.dram_tensor("dbg_out", [128, 4096], F32, kind="ExternalOutput").ap()
        dbg_yin = nc.dram_tensor("dbg_yin", [128, 8192], F32, kind="ExternalOutput").ap()
        dbg_btz = nc.dram_tensor("dbg_btz", [128, 4096], F32, kind="ExternalOutput").ap()
        dbg_mr = nc.dram_tensor("dbg_mr", [128, 1024], F32, kind="ExternalOutput").ap()
        dbg_cmat = nc.dram_tensor("dbg_cmat", [32, 130], F32, kind="ExternalOutput").ap()
        dbg_sx = nc.dram_tensor("dbg_sx", [128, 4096], F32, kind="ExternalOutput").ap()
        dbg_regs = nc.dram_tensor("dbg_regs", [128, 8], F32, kind="ExternalOutput").ap()
        dbg_idx = nc.dram_tensor("dbg_idx", [128, 16], F32, kind="ExternalOutput").ap()
        dbg_wab = nc.dram_tensor("dbg_wab", [128, 8], F32, kind="ExternalOutput").ap()
        dbg_m12a = nc.dram_tensor("dbg_m12a", [128, 32], F32, kind="ExternalOutput").ap()
        dbg_m12b = nc.dram_tensor("dbg_m12b", [128, 32], F32, kind="ExternalOutput").ap()
        dbg_slot = nc.dram_tensor("dbg_slot", [128, 64], F32, kind="ExternalOutput").ap()
        dbg_tab = nc.dram_tensor("dbg_tab", [128, 32], F32, kind="ExternalOutput").ap()
        dbg_yab = nc.dram_tensor("dbg_yab", [1024, 1024], F32, kind="ExternalOutput").ap()
    scr_wit = nc.dram_tensor("scr_wit", [16, 128, 2048], BF16,
                             kind="Internal").ap()
    scr_wout = nc.dram_tensor("scr_wout", [16, 128, D], BF16,
                              kind="Internal").ap()
    scr_w1 = nc.dram_tensor("scr_w1", [E, D, FF], BF16, kind="Internal").ap()
    scr_w2 = nc.dram_tensor("scr_w2", [E, FF, D], BF16, kind="Internal").ap()
    scr_btz = nc.dram_tensor("scr_btz", [128, 4096], BF16, kind="Internal").ap()
    scr_idx = nc.dram_tensor("scr_idx", [2048], mybir.dt.float32,
                             kind="Internal").ap()
    scr_idx2 = nc.dram_tensor("scr_idx2", [2048], mybir.dt.float32,
                              kind="Internal").ap()
    scr_yab = nc.dram_tensor("scr_yab", [1024, D], BF16, kind="Internal").ap()
    scr_xn = nc.dram_tensor("scr_xn", [2 * CH, D], BF16, kind="Internal").ap()

    with contextlib.ExitStack() as st:
        def sb(name, shape, dt):
            return st.enter_context(nc.sbuf_tensor(name, shape, dt))

        def ps(name, shape, dt):
            return st.enter_context(nc.psum_tensor(name, shape, dt))

        pool = [sb(f"pool{i}", [128, 16384], BF16) for i in range(3)]
        btz = sb("btz", [128, 32 * T], BF16)
        mr = sb("mr", [128, 32 * N], BF16)
        wssmT = sb("wssmT", [128, 16 * N], BF16)
        sBOUT = sb("sBOUT", [128, D], BF16)
        xnt = sb("xnt", [128, 8 * CH], BF16)
        smix = sb("smix", [128, NT * D], BF16)
        swit = [sb(f"swit{i}", [128, 2048], BF16) for i in range(2)]
        ssbc = sb("ssbc", [128, CH], F32)
        ar1 = sb("ar1", [128, 4096], F32)
        ar2 = sb("ar2", [128, 4096], F32)
        ar3 = sb("ar3", [128, 4096], F32)
        U32 = mybir.dt.uint32
        ar1b = ar1.bitcast(BF16)
        shts = ar1b
        sxg = ar1
        syr = ar1b
        sxn = ar2
        sout = ar2
        sp1 = ar3.bitcast(BF16)
        sxe = ar3
        soutb = ar3
        syp = ar3.bitcast(BF16)
        spad = [sb(f"spad{i}", [128, CH + 8], F32) for i in range(2)]
        ssg = [sb(f"ssg{i}", [128, CH], BF16) for i in range(2)]
        su = [sb(f"su{i}", [128, CH], F32) for i in range(2)]
        sub = [sb(f"sub{i}", [128, CH], BF16) for i in range(4)]
        stmp = su[0]
        sidf = sb("sidf", [128, 128], F32)
        sidb = sb("sidb", [128, 128], BF16)
        cmatT = sb("cmatT", [N, T + 2], F32)
        cb = [sb(f"cb{i}", [N, N + 1], F32) for i in range(2)]
        sPm = sb("sPm", [N, N], F32)
        sA = sb("sA", [N, N], F32)
        sHT = sb("sHT", [N, NCH * NT + 1], F32)
        sVT = sb("sVT", [N, CH], F32)
        svr = sb("svr", [128, T], BF16)
        sS = sb("sS", [NT, T], F32)
        s_row = sb("s_row", [1, CH], F32)
        sU = sb("sU", [NT, N], F32)
        s32 = sb("s32", [N, 16], F32)
        sUT = s32[:, 8:8 + NT]
        sBINT = sb("sBINT", [128, 32], F32)
        sCBT = sb("sCBT", [128, 16], F32)
        sDVT = sb("sDVT", [128, 16], F32)
        sCWS = sb("sCWS", [128, 64], F32)
        sWg = sb("sWg", [128, 64], BF16)
        shalo = sb("shalo", [128, 16 * 3], F32)
        sones = sb("sones", [1, 128], F32)
        sones8 = sb("sones8", [128, 8], F32)
        sbv = s32[:, 0:1]
        scv = s32[:, 1:2]
        sbs = s32[:, 2:3]
        sbsbv = s32[:, 3:4]
        smixf = smix.bitcast(F32)
        sbrow = smixf
        srawT = su[1]
        swsraw = pool[2].bitcast(F32)
        srt = [sb(f"srt{i}", [128, 32], F32) for i in range(6)]

        sslotm = sb("sslotm", [128, 64], F32)
        sidxf = sb("sidxf", [128, 16], F32)
        sidxu = sb("sidxu", [128, 16], U32)
        siotg = sb("siotg", [128, 4], F32)
        siotl = sb("siotl", [128, 4], F32)
        shug = sb("shug", [128, 16], F32)
        stabT = sb("stabT", [128, 32], F32)
        stabTi = sb("stabTi", [128, 32], U32)
        swab = sb("swab", [128, 8], F32)
        sb8 = sb("sb8", [8, 4], F32)
        smaskT = ssbc[0:8, 0:512]
        scum = su[0][0:8, 0:512]
        sregs = sb("sregs", [128, 8], F32)
        sregs2 = sb("sregs2", [128, 8], F32)
        srm = [sregs[:, i:i + 1] for i in range(4)]

        pb = [ps(f"pb{i}", [128, 512], F32) for i in range(8)]
        pbb = [b.bitcast(BF16) for b in pb]

        sems = {}
        for s_name in ("pe", "dv", "ac", "gp", "sd", "gd", "ad",
                       "gi0", "gi1", "gs0", "gs1", "gx", "gt",
                       "gy0", "gy1", "gn0", "gn1", "go", "gz",
                       "ap0", "ap1", "ap2", "aw0", "aw1", "ar"):
            sems[s_name] = st.enter_context(nc.semaphore(s_name))
        block = st.enter_context(nc.Block())
        P = Sched(nc, sems)

        bank_w = [None] * 8
        bank_r = [None] * 8

        def mm_group(items, bank_i, extra_waits=(), sem="pe"):
            P.waits("tensor", list(extra_waits) + [bank_r[bank_i]])
            n = len(items)
            tok = None
            for i, (o, l, r) in enumerate(items):
                s0, s1 = (i == 0), (i == n - 1)
                if s1:
                    tok = P.op("tensor",
                               lambda t, o=o, l=l, r=r, s0=s0, s1=s1:
                               t.matmul(o, l, r, start=s0, stop=s1), sem=sem)
                else:
                    P.op("tensor",
                         lambda t, o=o, l=l, r=r, s0=s0, s1=s1:
                         t.matmul(o, l, r, start=s0, stop=s1))
            bank_w[bank_i] = tok
            return tok

        def evac(eng, fn, bank_i, extra_waits=(), sem=None):
            sem = sem or ("dv" if eng == "vector" else "ac")
            tok = P.op(eng, fn, sem=sem,
                       waits=list(extra_waits) + [bank_w[bank_i]])
            bank_r[bank_i] = tok
            return tok

        regs = {}

        def _alloc_regs(g):
            b_idx = g.alloc_register(name="b_idx")
            regs["b_idx"] = b_idx
            g.reg_mov(regs["b_idx"], 2047)
            b_tok = g.alloc_register(name="b_tok")
            regs["b_tok"] = b_tok
            g.reg_mov(regs["b_tok"], 2 * CH - 1)
            b_slot = g.alloc_register(name="b_slot")
            regs["b_slot"] = b_slot
            g.reg_mov(regs["b_slot"], 1023)
        P.streams["gpsimd"].append(_alloc_regs)

        t_iota = P.op("gpsimd", lambda g: g.iota(
            sidf[:], [[-1, 128]], base=0, channel_multiplier=1,
            allow_small_or_imprecise_dtypes=True), sem="gp")
        t_ones = P.op("gpsimd", lambda g: g.memset(sones[:], 1.0), sem="gp")
        t_ones8 = P.op("gpsimd", lambda g: g.memset(sones8[:], 1.0), sem="gp")
        t_halo0 = P.op("gpsimd", lambda g: g.memset(shalo[:], 0.0), sem="gp")
        t_b8 = P.op("gpsimd", lambda g: g.iota(
            sb8[0:8, 0:1], [[1, 1]], base=-1, channel_multiplier=256,
            allow_small_or_imprecise_dtypes=True), sem="gp")
        t_b8 = P.op("gpsimd", lambda g: g.iota(
            sb8[0:8, 2:3], [[1, 1]], base=255, channel_multiplier=256,
            allow_small_or_imprecise_dtypes=True), sem="gp")
        t_iotl = P.op("gpsimd", lambda g: g.iota(
            siotl[:], [[128, 4]], base=0, channel_multiplier=1,
            allow_small_or_imprecise_dtypes=True), sem="gp")
        t_hug = P.op("gpsimd", lambda g: g.memset(
            shug[:], float(1 << 20)), sem="gp")
        t_arena1_z = P.op("gpsimd", lambda g: g.memset(ar1[:], 0.0),
                          sem="gp")


        t_idf = P.op("vector", lambda v: v.tensor_scalar(
            sidf[:], sidf[:], 0, None, ALU.is_equal), sem="dv",
            waits=[t_iota])
        t_idb = P.op("vector", lambda v: v.tensor_copy(
            sidb[:], sidf[:]), sem="dv", waits=[t_idf])
        t_ht0 = P.op("vector", lambda v: v.memset(sHT[:, 0:1], 0.0), sem="dv")

        def ld(dst, src, waits=()):
            return P.op("sync", lambda s, d=dst, sr=src: s.dma_start(d, sr),
                        sem="sd", amt=16, waits=waits)

        t_a = ld(sA[:], A_d)
        t_bv = ld(sbv, bass.AP(Bv_d.tensor, 0, [[1, N], [1, 1]]))
        t_cv = ld(scv, bass.AP(Cv_d.tensor, 0, [[1, N], [1, 1]]))
        t_bs = ld(sbs, bass.AP(b_ssm.tensor, 0, [[1, N], [1, 1]]))
        t_binr = ld(srawT[0:32, 0:128],
                    bass.AP(b_in.tensor, 0, [[128, 32], [1, 128]]))
        t_cws = ld(sCWS[:],
                   bass.AP(conv_w.tensor, 0, [[4, 128], [512, 16], [1, 4]]))
        t_brow = ld(sbrow[0:1, 0:D], bass.AP(b_out.tensor, 0, [[D, 1], [1, D]]))
        t_wsr = ld(swsraw[0:N, 0:H], w_ssm)
        t_wg = P.op("gpsimd", lambda g: g.dma_start(
            sWg[:], bass.AP(wg.tensor, 0, [[8, 128], [1024, 8], [1, 8]])),
            sem="gd", amt=16)

        t_bsbv = P.op("vector", lambda v: v.tensor_tensor(
            sbsbv, sbs, sbv, ALU.mult), sem="dv",
            waits=[t_bv, t_bs])

        def small_transpose(dst_ap, src_ap, in_p, load_tok):
            tw = P.op("tensor", lambda t, src_ap=src_ap, in_p=in_p:
                      t.transpose(pb[6][0:128, 0:in_p], src_ap,
                                  sidf[0:in_p, 0:in_p]),
                      sem="pe", waits=[load_tok, t_idf, bank_r[6]])
            bank_w[6] = tw
            return evac("vector", lambda v, dst_ap=dst_ap, in_p=in_p:
                        v.tensor_copy(dst_ap, pb[6][0:128, 0:in_p]), 6)

        t_bint = small_transpose(sBINT[:], srawT[0:32, 0:128], 32, t_binr)
        t_cbr = ld(srawT[0:16, 0:128],
                   bass.AP(conv_b.tensor, 0, [[128, 16], [1, 128]]),
                   waits=[t_bint])
        t_cbt = small_transpose(sCBT[:], srawT[0:16, 0:128], 16, t_cbr)
        t_dvr = ld(srawT[0:16, 0:128],
                   bass.AP(Dv_d.tensor, 0, [[128, 16], [1, 128]]),
                   waits=[t_cbt])
        t_dvt = small_transpose(sDVT[:], srawT[0:16, 0:128], 16, t_dvr)

        t_wssm_pe = None
        for hb in range(16):
            tw = P.op("tensor", lambda t, hb=hb: t.transpose(
                pb[6][0:128, 0:32], swsraw[0:N, 128 * hb:128 * (hb + 1)],
                sidf[0:32, 0:32]), sem="pe",
                waits=[t_wsr, t_idf, bank_r[6]])
            bank_w[6] = tw
            t_wssm_pe = tw
            evac("vector", lambda v, hb=hb: v.tensor_copy(
                wssmT[:, 32 * hb:32 * (hb + 1)], pb[6][0:128, 0:32]), 6)

        for h2 in range(2):
            mm_group([(pb[h2][0:128, 0:512], sones[0:1, :],
                       sbrow[0:1, 512 * h2:512 * (h2 + 1)])], h2,
                     extra_waits=[t_brow, t_ones])
            evac("vector", lambda v, h2=h2: v.tensor_copy(
                sBOUT[:, 512 * h2:512 * (h2 + 1)], pb[h2][0:128, 0:512]), h2)

        t0 = P.op("vector", lambda v: v.tensor_copy(
            cb[0][:, 0:N], sidf[0:N, 0:N]), waits=[t_idf])
        t0b = P.op("vector", lambda v: v.tensor_copy(
            cb[0][:, N:N + 1], scv), sem="dv", waits=[t_cv])
        P.op("vector", lambda v: v.tensor_copy(cmatT[:, 0:1], scv))
        P.op("vector", lambda v: v.tensor_copy(
            mr[96:128, 31 * N:32 * N], sidf[0:N, 0:N]))
        chain_state = {"prev": t0b}

        def chain_step(k):
            tw = P.op("tensor", lambda t, k=k: t.matmul(
                pb[7][0:N, 0:N + 1], sA[:], cb[k % 2][:],
                start=True, stop=True),
                sem="pe", waits=[t_a, chain_state["prev"], bank_r[7]])
            bank_w[7] = tw
            te = evac("vector", lambda v, k=k: v.tensor_copy(
                cb[(k + 1) % 2][:], pb[7][0:N, 0:N + 1]), 7)
            kk = k + 1
            P.op("vector", lambda v, kk=kk: v.tensor_copy(
                cmatT[:, kk:kk + 1], cb[kk % 2][:, N:N + 1]), waits=[te])
            if kk <= T - 1:
                q = T - 1 - kk
                kc, j = q // 4, q % 4
                P.op("vector", lambda v, kk=kk, kc=kc, j=j: v.tensor_copy(
                    mr[32 * j:32 * (j + 1), N * kc:N * (kc + 1)],
                    cb[kk % 2][:, 0:N]))
            elif kk == T:
                P.op("vector", lambda v, kk=kk: v.tensor_copy(
                    sPm[:], cb[kk % 2][:, 0:N]))
            chain_state["prev"] = te

        chain_steps = [lambda k=k: chain_step(k) for k in range(T + 1)]

        def emit_btz(t_zero):
            P.wait("vector", t_zero)
            P.wait("vector", chain_state["prev"])
            t_btz = None
            for q in range(T):
                kc, j = q // 4, q % 4
                t_btz = P.op("vector", lambda v, kc=kc, j=j, q=q:
                             v.tensor_copy(
                    btz[32 * j:32 * (j + 1), T * kc + q:T * (kc + 1)],
                    cmatT[:, 0:T - q]), sem="dv" if q == T - 1 else None)
            return t_btz

        stage_a = pool[2].bitcast(F32)
        stage_b = pool[2]
        bst = {"store": [None, None], "first": True}
        wit_tok = [None] * 16
        wit_ready = [None] * 16
        wit_mm_tok = [None, None]
        wout_done = [None]

        wit_steps = []
        for hb_ in range(16):
            for half_ in range(2):
                def wstep(hb=hb_, half=half_):
                    w = half
                    zt = hb + 16 * half
                    extra = [t_wssm_pe] if bst["first"] else []
                    bst["first"] = False
                    t_l = P.op("sync", lambda s, zt=zt, w=w: s.dma_start(
                        stage_a[:, 2048 * w:2048 * w + 1024],
                        w_in[128 * zt:128 * (zt + 1), :]), sem="sd",
                        amt=16, waits=[bst["store"][w]] + extra)
                    ring = hb % 2
                    for g2 in range(2):
                        bk = 4 + g2
                        for dq in range(4):
                            dblk = 4 * g2 + dq
                            tw = P.op("tensor", lambda t, w=w, dblk=dblk,
                                      dq=dq, bk=bk: t.transpose(
                                pb[bk][0:128, 128 * dq:128 * (dq + 1)],
                                stage_a[:, 2048 * w + 128 * dblk:
                                        2048 * w + 128 * (dblk + 1)],
                                sidf[:]),
                                sem="pe" if dq == 3 else None,
                                waits=([t_l, t_idf, bank_r[bk]]
                                       if dq == 0 else []))
                        bank_w[bk] = tw
                        t_e = evac("vector", lambda v, ring=ring, half=half,
                                   bk=bk, g2=g2: v.tensor_copy(
                            swit[ring][:, 1024 * half + 512 * g2:
                                       1024 * half + 512 * (g2 + 1)],
                            pb[bk][0:128, 0:512]), bk,
                            extra_waits=[wit_mm_tok[ring]] if g2 == 0 else ())
                    t_st = P.op("sync", lambda s, hb=hb, half=half, ring=ring:
                                s.dma_start(
                        bass.AP(scr_wit.tensor,
                                hb * 128 * 2048 + half * 1024,
                                [[2048, 128], [1, 1024]]),
                        swit[ring][:, 1024 * half:1024 * half + 1024]),
                        sem="sd", amt=16, waits=[t_e])
                    bst["store"][w] = t_st
                    if half == 1:
                        wit_tok[hb] = t_st
                        wit_ready[hb] = t_e
                wit_steps.append(wstep)

        wout_steps = []
        for dt_ in range(8):
            def wostep(dt=dt_):
                w = dt % 2
                t_l = P.op("sync", lambda s, dt=dt, w=w: s.dma_start(
                    stage_a[:, 2048 * w:2048 * (w + 1)],
                    w_out[128 * dt:128 * (dt + 1), :]), sem="sd", amt=16,
                    waits=[bst["store"][w]])
                t_e = None
                for g4 in range(4):
                    bk = 4 + g4 % 2
                    for dq in range(4):
                        hb = 4 * g4 + dq
                        tw = P.op("tensor", lambda t, hb=hb, dq=dq, bk=bk,
                                  w=w: t.transpose(
                            pb[bk][0:128, 128 * dq:128 * (dq + 1)],
                            stage_a[:, 2048 * w + 128 * hb:
                                    2048 * w + 128 * (hb + 1)], sidf[:]),
                            sem="pe" if dq == 3 else None,
                            waits=([t_l, bank_r[bk]] if dq == 0 else []))
                    bank_w[bk] = tw
                    t_e = evac("vector", lambda v, w=w, bk=bk, g4=g4:
                               v.tensor_copy(
                        stage_b[:, 8192 + 2048 * w + 512 * g4:
                                8192 + 2048 * w + 512 * (g4 + 1)],
                        pb[bk][0:128, 0:512]), bk)
                t_st = P.op("sync", lambda s, dt=dt, w=w: s.dma_start(
                    bass.AP(scr_wout.tensor, dt * 128,
                            [[1024, 128], [128 * 1024, 16], [1, 128]]),
                    stage_b[:, 8192 + 2048 * w:8192 + 2048 * (w + 1)]),
                    sem="sd", amt=16, waits=[t_e])
                bst["store"][w] = t_st
                if dt == 7:
                    wout_done[0] = t_st
            wout_steps.append(wostep)

        pump_state = {"wit": 0, "wout": 0, "chain": 0}

        def pump(name, steps, n):
            i = pump_state[name]
            while n > 0 and i < len(steps):
                steps[i]()
                i += 1
                n -= 1
            pump_state[name] = i

        pool_rot = [0]
        pool_tok = [None, None, None]
        pool_stok = [None, None, None]
        w1_scr_tok = [None] * E
        w2_scr_tok = [None] * E
        swit_tok = [None] * 3
        t_sht_reader = None
        t_sout_reader = None
        t_p1_reader = None
        t_e_bufs_reader = None
        t_xnt_reader = None
        t_svr_reader = None
        t_ex_prev = None

        WOUT_AP = bass.AP(scr_wout.tensor, 0,
                          [[1024, 128], [128 * 1024, 16], [1, 1024]])

        def w1_ap(e, t):
            return bass.AP(t.tensor, e * D * FF,
                           [[2048, 128], [128 * 2048, 8], [1, 2048]])

        def w2_ap(e, t):
            return bass.AP(t.tensor, e * FF * D,
                           [[1024, 128], [128 * 1024, 16], [1, 1024]])

        def pool_load(src_ap, waits=(), store_to=None, eng="gpsimd",
                      sem=None):
            slot = pool_rot[0] % 3
            pool_rot[0] += 1
            tok = P.op(eng, lambda g, s=slot, src=src_ap: g.dma_start(
                pool[s][:, 0:16384], src), sem=f"ap{slot}", amt=16,
                waits=list(waits) + [pool_tok[slot], pool_stok[slot]])
            stok = None
            if store_to is not None:
                stok = P.op("sync", lambda s, sl=slot, d=store_to:
                            s.dma_start(d, pool[sl][:, 0:16384]),
                            sem="sd", amt=16, waits=[tok])
            pool_stok[slot] = stok
            return slot, tok, stok

        def dump(dst, src_ap, waits):
            return P.op("gpsimd", lambda g, d=dst, s=src_ap: g.dma_start(d, s),
                        sem="gd", amt=16, waits=waits)

        front_tok = {}
        scrxn_reader = {}

        def emit_xload(c, btz_reader_waits):
            return P.op("gpsimd", lambda g, c=c: g.dma_start(
                btz[:].rearrange("p (t d) -> p t d", t=NT),
                bass.AP(x_d.tensor, c * CH * D,
                        [[1024, 128], [128 * 1024, NT], [1, 1024]])),
                sem="gz", amt=16, waits=list(btz_reader_waits))

        def emit_front_stats(c, sq_waits, t_x):
            t_red = None
            t_sqa = None
            for t in range(NT):
                xs = btz[:, 1024 * t:1024 * (t + 1)]
                t_red = P.op("vector", lambda v, xs=xs, t=t: v.tensor_reduce(
                    sregs[:, t:t + 1], xs, AX.X, ALU.add),
                    sem="dv" if t == NT - 1 else None,
                    waits=[t_x])
                P.op("scalar", lambda s, xs=xs, t=t: s.activation(
                    su[0][:], xs[:, 0:512], AF.Square,
                    accum_out=sregs[:, 4 + t:5 + t]),
                    waits=[t_x] + (list(sq_waits) if t == 0 else []))
                t_sqa = P.op("scalar", lambda s, xs=xs, t=t: s.activation(
                    su[1][:], xs[:, 512:1024], AF.Square,
                    accum_out=sregs2[:, 4 + t:5 + t]),
                    sem="ac" if t == NT - 1 else None)
            su_silu_tok[0] = t_sqa
            su_silu_tok[1] = t_sqa
            tc = P.op("vector", lambda v: v.tensor_scalar(
                sregs2[:, 0:4], sregs[:, 0:4], 1.0 / D, None, ALU.mult),
                sem="dv", waits=[t_red])
            tc = P.op("vector", lambda v: v.scalar_tensor_tensor(
                sregs[:, 0:4], sregs2[:, 0:4], -1.0, sregs2[:, 0:4],
                ALU.mult, ALU.mult), sem="dv", waits=[tc])
            tc2 = P.op("vector", lambda v: v.tensor_tensor(
                sregs[:, 4:8], sregs[:, 4:8], sregs2[:, 4:8], ALU.add),
                sem="dv", waits=[t_sqa])
            tc2 = P.op("vector", lambda v: v.tensor_scalar(
                sregs[:, 4:8], sregs[:, 4:8], 1.0 / D, None, ALU.mult),
                sem="dv", waits=[tc2])
            tc = P.op("vector", lambda v: v.tensor_tensor(
                sregs2[:, 4:8], sregs[:, 4:8], sregs[:, 0:4], ALU.add),
                sem="dv", waits=[tc, tc2])
            tc = P.op("vector", lambda v: v.tensor_scalar(
                sregs2[:, 4:8], sregs2[:, 4:8], 1e-5, None, ALU.add),
                sem="dv", waits=[tc])
            tc = P.op("scalar", lambda s: s.activation(
                sregs[:, 4:8], sregs2[:, 4:8], AF.Sqrt), sem="ac",
                waits=[tc])
            t_rstd = P.op("vector", lambda v: v.reciprocal(
                sregs[:, 4:8], sregs[:, 4:8]), sem="dv", waits=[tc])
            front_tok["t_x"] = t_x
            front_tok["t_sq_last"] = t_sqa
            front_tok["t_rstd"] = t_rstd

        def emit_front_tile(c, t, first=False):
            t_rstd = front_tok["t_rstd"]
            par = c % 2
            xs = btz[:, 1024 * t:1024 * (t + 1)]
            t_xn = P.op("vector", lambda v, xs=xs, t=t: v.tensor_scalar(
                sxn[:, 1024 * t:1024 * (t + 1)], xs,
                sregs2[:, t:t + 1], sregs[:, 4 + t:5 + t],
                ALU.subtract, ALU.mult), sem="dv",
                waits=[t_rstd] +
                ([front_tok.get("t_xnst")] if first else []))
            t_xnst = P.op("gpsimd", lambda g, t=t, par=par: g.dma_start(
                scr_xn[par * CH + T * t:par * CH + T * (t + 1), :],
                sxn[:, 1024 * t:1024 * (t + 1)]),
                sem=f"gn{par}", amt=16,
                waits=[t_xn] + list(scrxn_reader.get(par, ())))
            t_xnt_all = None
            for g2 in range(2):
                bk = 4 + g2
                for dq in range(4):
                    dblk = 4 * g2 + dq
                    tw = P.op("tensor", lambda tn, t=t, dblk=dblk,
                              dq=dq, bk=bk: tn.transpose(
                        pb[bk][0:128, 128 * dq:128 * (dq + 1)],
                        sxn[:, 1024 * t + 128 * dblk:
                            1024 * t + 128 * (dblk + 1)], sidf[:]),
                        sem="pe" if dq == 3 else None,
                        waits=([t_xn, bank_r[bk]] if dq == 0 else []))
                bank_w[bk] = tw
                t_xnt_all = evac(
                    "vector", lambda v, t=t, bk=bk, g2=g2: v.tensor_copy(
                        xnt[:].rearrange("p (d ch) -> p d ch", d=8)
                        [:, 4 * g2:4 * (g2 + 1), T * t:T * (t + 1)],
                        pb[bk][0:128, 0:512].rearrange(
                            "p (d ch) -> p d ch", d=4)),
                    bk, extra_waits=[front_tok.get("xnt_reader")])
            front_tok["t_xn_last"] = t_xn
            front_tok["t_xnt_all"] = t_xnt_all
            front_tok["t_xnst"] = t_xnst

        t_btz = None
        t_mm2_prev = None
        t_iog_reader = None
        t_tab_reader = None
        t_stab_reader = None
        t_xgt_cur = None
        xg_tok = [None, None]
        t_yab_last = None
        yab_fin = [None, None]
        t_smaskT_war = None
        su_scrib = {"t": None}
        su_silu_tok = [None, None]
        xgt_tok = [None, None]
        last_ga = [None, None]
        yp_tok = [None, None]
        yab_tok = [None] * 4
        t_yscat_all = None
        for c in range(n_chunks):
            if c == 0:
                t_x0 = emit_xload(0, [None])
                emit_front_stats(0, [None], t_x0)
                for _t in range(NT):
                    emit_front_tile(0, _t, first=(_t == 0))
            t_xnt_all = front_tok["t_xnt_all"]
            t_ln_last = front_tok["t_xn_last"]
            t_sq_last = front_tok["t_sq_last"]
            if c == 0:
                w1_slot, t_w1, w1_scr_tok[0] = pool_load(
                    w1_ap(0, w1), store_to=w1_ap(0, scr_w1))
            else:
                wout_slot, t_wout, _ = pool_load(
                    WOUT_AP, [wout_done[0]], eng="gpsimd")
                w1_slot, t_w1, _ = pool_load(
                    w1_ap(0, scr_w1), [w1_scr_tok[0]],
                    eng="gpsimd")
                t_btz = P.op("sync", lambda s: s.dma_start(
                    btz[:], scr_btz), sem="sd", amt=16,
                    waits=[t_ln_last, t_sq_last, t_scrbtz])

            t_ssm_group = None
            t_p1_last = None
            su_tok = [None] * 4
            sub_tok = [None] * 4
            sub_silu_tok = [None] * 4
            spad_tok = [None, None]
            ssg_tok = [None, None]
            for hb in range(16):
                if c == 0:
                    pump("wit", wit_steps, 2)
                    pump("chain", chain_steps, 9)
                slot = hb % 2
                if c == 0:
                    t_wl = wit_ready[hb]
                else:
                    t_wl = P.op("sync" if slot == 0 else "scalar",
                                lambda s, hb=hb, slot=slot:
                                s.dma_start(
                        swit[slot][:],
                        bass.AP(scr_wit.tensor, hb * 128 * 2048,
                                [[2048, 128], [1, 2048]])),
                        sem=f"aw{slot}", amt=16,
                        waits=[swit_tok[slot], wit_tok[hb]])
                pba, pbg = (0, 1) if hb % 2 == 0 else (2, 3)
                t_xh = mm_group(
                    [(pb[pba][0:128, 0:512],
                      swit[slot][:, 128 * dblk:128 * (dblk + 1)],
                      xnt[:, CH * dblk:CH * (dblk + 1)])
                     for dblk in range(8)],
                    pba, extra_waits=[t_wl, t_xnt_all])
                t_gt = mm_group(
                    [(pb[pbg][0:128, 0:512],
                      swit[slot][:, 1024 + 128 * dblk:
                                 1024 + 128 * (dblk + 1)],
                      xnt[:, CH * dblk:CH * (dblk + 1)])
                     for dblk in range(8)],
                    pbg, extra_waits=[t_wl, t_xnt_all])
                swit_tok[slot] = t_gt
                wit_mm_tok[slot] = t_gt
                w = hb % 2
                t_xhe = evac("scalar", lambda s, hb=hb, w=w, pba=pba:
                             s.activation(
                    spad[w][:, 3:3 + CH], pb[pba][0:128, 0:512],
                    AF.Identity, bias=sBINT[:, hb:hb + 1]), pba,
                    extra_waits=[spad_tok[w], t_bint])
                t_sge = evac("scalar", lambda s, hb=hb, w=w, pbg=pbg:
                             s.activation(
                    ssg[w][:], pb[pbg][0:128, 0:512],
                    AF.Silu, bias=sBINT[:, 16 + hb:17 + hb]), pbg,
                    extra_waits=[ssg_tok[w]])
                P.op("vector", lambda v, hb=hb, w=w: v.tensor_copy(
                    spad[w][:, 0:3], shalo[:, 3 * hb:3 * (hb + 1)]),
                    waits=[t_xhe, t_halo0])
                P.op("vector", lambda v, hb=hb, w=w: v.tensor_copy(
                    shalo[:, 3 * hb:3 * (hb + 1)], spad[w][:, CH:CH + 3]))
                P.op("vector", lambda v, hb=hb, w=w: v.tensor_scalar(
                    su[w][:], spad[w][:, 0:CH], sCWS[:, 4 * hb:4 * hb + 1],
                    None, ALU.mult), waits=[t_cws, su_silu_tok[w]])
                t_conv = None
                for k in range(1, 4):
                    t_conv = P.op("vector", lambda v, hb=hb, w=w, k=k:
                                  v.scalar_tensor_tensor(
                        su[w][:], spad[w][:, k:k + CH],
                        sCWS[:, 4 * hb + k:4 * hb + k + 1], su[w][:],
                        ALU.mult, ALU.add),
                        sem="dv" if k == 3 else None)
                spad_tok[w] = t_conv
                t_us = P.op("scalar", lambda s, hb=hb, w=w: s.activation(
                    sub[hb % 4][:], su[w][:], AF.Silu,
                    bias=sCBT[:, hb:hb + 1]), sem="ac",
                    waits=[t_conv, t_cbt, sub_tok[hb % 4], su_tok[hb % 4]])
                if hb >= 2:
                    hs = hb - 2
                    t_ssm_group = P.op("tensor", lambda t, hs=hs: t.matmul(
                        pb[6][0:N, 0:512], wssmT[:, N * hs:N * (hs + 1)],
                        sub[hs % 4][:], start=(hs == 0), stop=False),
                        sem="pe",
                        waits=[sub_silu_tok[hs % 4]] +
                        ([bank_r[6]] if hs == 0 else []))
                    sub_tok[hs % 4] = t_ssm_group
                t_p1w = P.op("vector", lambda v, hb=hb, w=w: v.tensor_tensor(
                    sp1[:, CH * hb:CH * (hb + 1)], sub[hb % 4][:], ssg[w][:],
                    ALU.mult), sem="dv",
                    waits=[t_us, t_sge] +
                    [t_e_bufs_reader, t_p1_reader])
                su_tok[hb % 4] = t_p1w
                sub_silu_tok[hb % 4] = t_us
                su_silu_tok[w] = t_us
                ssg_tok[w] = t_p1w
                t_p1_last = t_p1w

            for hs in (14, 15):
                t_ssm_group = P.op("tensor", lambda t, hs=hs: t.matmul(
                    pb[6][0:N, 0:512], wssmT[:, N * hs:N * (hs + 1)],
                    sub[hs % 4][:], start=False, stop=(hs == 15)),
                    sem="pe", waits=[sub_silu_tok[hs % 4]])
                sub_tok[hs % 4] = t_ssm_group
            bank_w[6] = t_ssm_group
            t_v = evac("scalar", lambda s: s.activation(
                sVT[:], pb[6][0:N, 0:512], AF.Identity,
                scale=sbv, bias=sbsbv), 6, extra_waits=[t_bsbv])
            if c == 0:
                pump("chain", chain_steps, T + 1)
                t_z = P.op("gpsimd", lambda g: g.memset(btz[:], 0.0),
                           sem="gp", waits=[t_ln_last, t_sq_last])
                t_btz = emit_btz(t_z)
                t_scrbtz = P.op("sync", lambda s: s.dma_start(
                    scr_btz, btz[:]), sem="sd", amt=16, waits=[t_btz])
                pump("wout", wout_steps, 8)
                pool_tok[2] = wout_done[0]
                wout_slot, t_wout, _ = pool_load(WOUT_AP, [wout_done[0]])
            if dbg and c == 0:
                t_dmp = None
                t_dmp = dump(dbg_p1, sp1[:], [t_p1_last, t_dmp])
                t_dmp = dump(dbg_vt, sVT[0:32, 0:512], [t_v, t_dmp])

            t_vr = None
            for j in range(4):
                t_vr = P.op("vector", lambda v, j=j: v.tensor_copy(
                    svr[32 * j:32 * (j + 1), :].rearrange(
                        "p (kc i) -> p kc i", kc=32),
                    bass.AP(sVT, j, [[CH, 32], [4, 32], [T, 4]])),
                    sem="dv" if j == 3 else None,
                    waits=[t_v] + ([t_svr_reader] if j == 0 else []))
            t_umm = mm_group(
                [(pb[5][0:NT, 0:N], svr[:, 4 * kc:4 * (kc + 1)],
                  mr[:, N * kc:N * (kc + 1)]) for kc in range(32)],
                5, extra_waits=[t_vr])

            t_ue = evac("vector", lambda v: v.tensor_copy(
                sU[:], pb[5][0:NT, 0:N]), 5)
            tw = P.op("tensor", lambda t: t.transpose(
                pb[5][0:N, 256:256 + NT], sU[:], sidf[0:NT, 0:NT]),
                sem="pe", waits=[t_ue, bank_r[5]])
            bank_w[5] = tw
            t_ut = evac("vector", lambda v: v.tensor_copy(
                sUT[:, 0:NT], pb[5][0:N, 256:256 + NT]), 5)
            P.wait("vector", t_ut)
            for i in range(NT):
                g = c * NT + i
                tw = P.op("tensor", lambda t, g=g: t.matmul(
                    pb[5][0:N, 300:301], sPm[:], sHT[:, g:g + 1],
                    start=True, stop=True), sem="pe",
                    waits=[t_ut, t_ht0, bank_r[5]])
                bank_w[5] = tw
                evac("vector", lambda v, g=g, i=i: v.tensor_tensor(
                    sHT[:, g + 1:g + 2], pb[5][0:N, 300:301],
                    sUT[:, i:i + 1], ALU.add), 5)
            t_hcols = bank_r[5]
            t_lg = None
            P.waits("tensor", [t_ex_prev, bank_r[7]])
            for t in range(NT):
                t_lg = mm_group(
                    [(pb[7][0:128, 8 * t:8 * (t + 1)],
                      xnt[:, CH * dblk + T * t:CH * dblk + T * (t + 1)],
                      sWg[:, 8 * dblk:8 * (dblk + 1)]) for dblk in range(8)],
                    7, extra_waits=[t_wg, t_xnt_all])
            pb7v = pb[7][0:128, 0:32].rearrange("p (t e) -> p t e", t=NT)
            tr = P.op("vector", lambda v: v.tensor_reduce(
                sregs[:, 0:4], pb7v, AX.X, ALU.max, negate=True),
                sem="dv", waits=[t_lg])
            neg_b = sregs[:, 0:4].broadcast_to((128, NT, 8))
            tr = P.op("vector", lambda v: v.tensor_tensor(
                srt[0][:].rearrange("p (t e) -> p t e", t=NT), pb7v,
                neg_b, ALU.add), sem="dv", waits=[tr])
            t_ex = P.op("scalar", lambda s: s.activation(
                srt[1][:], srt[0][:], AF.Exp), sem="ac", waits=[tr])
            t_ex_prev = t_ex
            bank_r[7] = t_ex
            tr = P.op("vector", lambda v: v.tensor_reduce(
                sregs[:, 0:4], srt[1][:].rearrange("p (t e) -> p t e", t=NT),
                AX.X, ALU.add), sem="dv", waits=[t_ex])
            tr = P.op("vector", lambda v: v.reciprocal(
                sregs[:, 0:4], sregs[:, 0:4]), sem="dv", waits=[tr])
            rec_b = sregs[:, 0:4].broadcast_to((128, NT, 8))
            tr = P.op("vector", lambda v: v.tensor_tensor(
                srt[1][:].rearrange("p (t e) -> p t e", t=NT),
                srt[1][:].rearrange("p (t e) -> p t e", t=NT),
                rec_b, ALU.mult), sem="dv", waits=[tr])
            tr = P.op("vector", lambda v: v.tensor_reduce(
                sregs[:, 0:4], srt[1][:].rearrange("p (t e) -> p t e", t=NT),
                AX.X, ALU.max), sem="dv", waits=[tr])
            m1_b = sregs[:, 0:4].broadcast_to((128, NT, 8))
            tr = P.op("vector", lambda v: v.tensor_tensor(
                srt[2][:].rearrange("p (t e) -> p t e", t=NT),
                srt[1][:].rearrange("p (t e) -> p t e", t=NT),
                m1_b, ALU.is_equal), sem="dv", waits=[tr])
            tsc = None
            for t in range(NT):
                tsc = P.op("vector", lambda v, t=t: v.tensor_tensor_scan(
                    srt[3][:, 8 * t:8 * (t + 1)], sones8[:],
                    srt[2][:, 8 * t:8 * (t + 1)], 0.0, ALU.mult, ALU.add),
                    sem="dv" if t == NT - 1 else None,
                    waits=[tr, t_ones8])
            tr = P.op("vector", lambda v: v.tensor_scalar(
                srt[3][:], srt[3][:], 1.0, None, ALU.is_equal),
                sem="dv", waits=[tsc])
            tr = P.op("vector", lambda v: v.tensor_tensor(
                srt[2][:], srt[2][:], srt[3][:], ALU.mult),
                sem="dv", waits=[tr])
            tsc1_tok = tr
            tr = P.op("vector", lambda v: v.scalar_tensor_tensor(
                srt[4][:], srt[2][:], -1e30, srt[1][:], ALU.mult, ALU.add),
                sem="dv", waits=[tr])
            tr = P.op("vector", lambda v: v.tensor_reduce(
                sregs2[:, 0:4], srt[4][:].rearrange("p (t e) -> p t e", t=NT),
                AX.X, ALU.max), sem="dv", waits=[tr])
            m2_b = sregs2[:, 0:4].broadcast_to((128, NT, 8))
            tr = P.op("vector", lambda v: v.tensor_tensor(
                srt[5][:].rearrange("p (t e) -> p t e", t=NT),
                srt[4][:].rearrange("p (t e) -> p t e", t=NT),
                m2_b, ALU.is_equal), sem="dv", waits=[tr])
            tsc = None
            for t in range(NT):
                tsc = P.op("vector", lambda v, t=t: v.tensor_tensor_scan(
                    srt[3][:, 8 * t:8 * (t + 1)], sones8[:],
                    srt[5][:, 8 * t:8 * (t + 1)], 0.0, ALU.mult, ALU.add),
                    sem="dv" if t == NT - 1 else None, waits=[tr])
            tr = P.op("vector", lambda v: v.tensor_scalar(
                srt[3][:], srt[3][:], 1.0, None, ALU.is_equal),
                sem="dv", waits=[tsc])
            t_mask2 = P.op("vector", lambda v: v.tensor_tensor(
                srt[5][:], srt[5][:], srt[3][:], ALU.mult),
                sem="dv", waits=[tr])
            t_mask1 = tsc1_tok
            tden = P.op("vector", lambda v: v.tensor_tensor(
                sregs2[:, 4:8], sregs[:, 0:4], sregs2[:, 0:4], ALU.add),
                sem="dv", waits=[t_mask2])
            tden = P.op("vector", lambda v: v.reciprocal(
                sregs2[:, 4:8], sregs2[:, 4:8]), sem="dv", waits=[tden])
            tw1 = P.op("vector", lambda v: v.tensor_tensor(
                swab[:, 0:4], sregs[:, 0:4], sregs2[:, 4:8], ALU.mult),
                sem="dv", waits=[tden])
            t_wab = P.op("vector", lambda v: v.tensor_tensor(
                swab[:, 4:8], sregs2[:, 0:4], sregs2[:, 4:8], ALU.mult),
                sem="dv", waits=[tw1])
            bank_r[7] = t_wab

            t_on = P.op("vector", lambda v: v.memset(
                su[1][0:8, 0:512], 1.0), sem="dv", waits=[su_silu_tok[1]])
            t_s12 = [None, None]
            for r in range(2):
                mk = srt[2] if r == 0 else srt[5]
                t_mk = t_mask1 if r == 0 else t_mask2
                bkx = 6 + r
                for t in range(NT):
                    tw = P.op("tensor", lambda tn, t=t, mk=mk, bkx=bkx:
                              tn.transpose(
                        pb[bkx][0:8, 128 * t:128 * (t + 1)],
                        mk[:, 8 * t:8 * (t + 1)], sidf[:]),
                        sem="pe" if t == NT - 1 else None,
                        waits=([t_mk, bank_r[bkx], t_smaskT_war if r == 0
                                else None] if t == 0 else []))
                bank_w[bkx] = tw
                t_mT = evac("vector", lambda v, bkx=bkx: v.tensor_copy(
                    smaskT, pb[bkx][0:8, 0:512]), bkx)
                tk = P.op("vector", lambda v: v.tensor_tensor_scan(
                    scum, su[1][0:8, 0:512], smaskT,
                    0.0, ALU.mult, ALU.add), sem="dv",
                    waits=[t_on, t_mT])
                if r == 0:
                    tk = P.op("vector", lambda v: v.tensor_tensor(
                        sb8[0:8, 1:2], scum[:, 511:512], sb8[0:8, 0:1],
                        ALU.add), sem="dv", waits=[tk, t_b8])
                tk = P.op("vector", lambda v, r=r: v.tensor_scalar(
                    scum, scum, sb8[0:8, r:r + 1], None,
                    ALU.add), sem="dv", waits=[tk])
                tk = P.op("vector", lambda v: v.tensor_scalar(
                    smaskT, scum, sb8[0:8, 2:3], None,
                    ALU.is_gt), sem="dv", waits=[tk])
                tk = P.op("vector", lambda v: v.scalar_tensor_tensor(
                    scum, smaskT, float(1 << 20), scum,
                    ALU.mult, ALU.add), sem="dv", waits=[tk])
                bky = 4 + r
                for t in range(NT):
                    tw = P.op("tensor", lambda tn, t=t, bky=bky:
                              tn.transpose(
                        pb[bky][0:128, 8 * t:8 * (t + 1)],
                        scum[:, 128 * t:128 * (t + 1)], sidf[0:8, 0:8]),
                        sem="pe" if t == NT - 1 else None,
                        waits=([tk, bank_r[bky]] if t == 0 else []))
                bank_w[bky] = tw
                t_s12[r] = evac("vector", lambda v, r=r, bky=bky: v.tensor_copy(
                    sslotm[:, 32 * r:32 * (r + 1)], pb[bky][0:128, 0:32]),
                    bky)
            su_scrib["t"] = tk
            tk2 = None
            for r in range(2):
                mk = srt[2] if r == 0 else srt[5]
                tmu = P.op("vector", lambda v, mk=mk, r=r: v.tensor_tensor(
                    srt[0][:], mk[:],
                    sslotm[:, 32 * r:32 * (r + 1)], ALU.mult), sem="dv",
                    waits=([t_s12[0], t_s12[1]] if r == 0 else [tk2]))
                tk2 = P.op("vector", lambda v, r=r: v.tensor_reduce(
                    sidxf[:, 4 * r:4 * (r + 1)],
                    srt[0][:].rearrange("p (t e) -> p t e", t=NT),
                    AX.X, ALU.add), sem="dv", waits=[tmu])
            P.op("vector", lambda v: v.tensor_scalar(
                sidxf[:, 8:12], siotl[:], 0.0, None, ALU.add),
                waits=[t_iotl])
            tk3 = P.op("vector", lambda v: v.tensor_scalar(
                sidxf[:, 12:16], siotl[:], 512.0, None, ALU.add), sem="dv")
            t_idxu = P.op("vector", lambda v: v.tensor_copy(
                sidxu[:, 0:8], sidxf[:, 0:8]), sem="dv", waits=[tk2, tk3])
            t_iog = P.op("gpsimd", lambda g, c=c: g.iota(
                siotg[:], [[128, 4]], base=(c % 2) * CH,
                channel_multiplier=1, allow_small_or_imprecise_dtypes=True),
                sem="gp", waits=[t_iog_reader])
            t_init1 = P.op("gpsimd", lambda g: g.dma_start(
                bass.AP(scr_idx.tensor, 0, [[16, 128], [1, 16]]), shug[:]),
                sem="gx", amt=16, waits=[t_hug, t_tab_reader])
            t_init2 = P.op("gpsimd", lambda g: g.dma_start(
                bass.AP(scr_idx2.tensor, 0, [[16, 128], [1, 16]]), shug[:]),
                sem="gx", amt=16)
            for r in range(2):
                for t in range(NT):
                    t_i1 = P.op("gpsimd", lambda g, r=r, t=t:
                                g.indirect_dma_start(
                        bass.AP(scr_idx.tensor, 0, [[1, 2048], [1, 1]]),
                        bass.IndirectOffsetOnAxis(
                            ap=sidxu[:, 4 * r + t:4 * r + t + 1], axis=0),
                        siotg[:, t:t + 1], None,
                        bounds_check=regs["b_idx"], oob_is_err=False),
                        sem="gx", amt=16,
                        waits=[t_idxu, t_iog, t_init1, t_init2])
                    t_i2 = P.op("gpsimd", lambda g, r=r, t=t:
                                g.indirect_dma_start(
                        bass.AP(scr_idx2.tensor, 0, [[1, 2048], [1, 1]]),
                        bass.IndirectOffsetOnAxis(
                            ap=sidxu[:, 4 * r + t:4 * r + t + 1], axis=0),
                        sidxf[:, 8 + 4 * r + t:9 + 4 * r + t], None,
                        bounds_check=regs["b_idx"], oob_is_err=False),
                        sem="gx", amt=16)
            stab16 = ssbc
            t_tl = P.op("sync", lambda s: s.dma_start(
                stab16[0:16, 0:128],
                bass.AP(scr_idx.tensor, 0, [[128, 16], [1, 128]])),
                sem="gt", amt=16,
                waits=[t_i1, t_i2, t_xgt_cur, t_yscat_all,
                       yp_tok[0], yp_tok[1], t_stab_reader])
            t_tl = P.op("sync", lambda s: s.dma_start(
                stab16[0:16, 128:256],
                bass.AP(scr_idx2.tensor, 0, [[128, 16], [1, 128]])),
                sem="gt", amt=16)
            for h2 in range(2):
                tw = P.op("tensor", lambda tn, h2=h2: tn.transpose(
                    pb[6 + h2][0:128, 448:464],
                    stab16[0:16, 128 * h2:128 * (h2 + 1)],
                    sidf[0:16, 0:16]), sem="pe",
                    waits=[t_tl, bank_r[6 + h2]])
                bank_w[6 + h2] = tw
                evac("vector", lambda v, h2=h2: v.tensor_copy(
                    stabT[:, 16 * h2:16 * (h2 + 1)],
                    pb[6 + h2][0:128, 448:464]), 6 + h2)
            t_tab = P.op("vector", lambda v: v.tensor_copy(
                stabTi[:], stabT[:]), sem="dv",
                waits=[bank_r[6], bank_r[7], last_ga[0], last_ga[1],
                       yp_tok[0], yp_tok[1]])
            t_stab_reader = tw
            t_tab_reader = t_tl
            t_iog_reader = t_i1
            stabTu = stabTi
            if dbg and c == 0:
                t_dmp = dump(dbg_idx, sidxf[:], [t_idxu, t_dmp])
                t_dmp = dump(dbg_wab, swab[:], [t_wab, t_dmp])
                t_dmp = dump(dbg_m12a, srt[2][:], [t_mask1, t_dmp])
                t_dmp = dump(dbg_m12b, srt[5][:], [t_mask2, t_dmp])
                t_dmp = dump(dbg_slot, sslotm[:, 0:64],
                             [t_s12[0], t_s12[1], t_dmp])
                t_dmp = dump(dbg_tab, stabT[:], [t_tab, t_dmp])
            P.waits("tensor", [t_vr, bank_r[4]])
            for kc in range(32):
                P.op("tensor", lambda t, kc=kc: t.matmul(
                    pb[4][0:NT, 0:T], svr[:, 4 * kc:4 * (kc + 1)],
                    btz[:, T * kc:T * (kc + 1)], start=(kc == 0),
                    stop=False), waits=[t_btz] if kc == 0 else ())
            t_smm = P.op("tensor", lambda t, c=c: t.matmul(
                pb[4][0:NT, 0:T], sHT[:, c * NT:c * NT + NT],
                cmatT[:, 1:1 + T], start=False, stop=True), sem="pe",
                waits=[t_hcols])
            bank_w[4] = t_smm
            t_svr_reader = t_smm
            t_se = evac("vector", lambda v: v.tensor_copy(
                sS[:], pb[4][0:NT, 0:T]), 4)
            t_srow = P.op("sync", lambda s: s.dma_start(
                s_row[0:1, :], sS[:]), sem="ar", amt=16, waits=[t_se])
            mm_group([(pb[6][0:128, 0:512], sones[0:1, :], s_row[0:1, :])],
                     6, extra_waits=[t_srow, t_ones])
            t_sbc = evac("vector", lambda v: v.tensor_copy(
                ssbc[:], pb[6][0:128, 0:512]), 6)

            if dbg and c == 0:
                t_dmp = dump(dbg_sbc, ssbc[:], [t_sbc, t_dmp])
                t_dmp = dump(dbg_ht, sHT[0:32, 0:33], [t_hcols, t_dmp])
            if c < n_chunks - 1:
                t_x_next = emit_xload(
                    c + 1, [t_svr_reader, t_scrbtz if c == 0 else None])
            t_mm2_last = None
            t_out_stt = [None] * NT

            def emit_xgt(e):
                nonlocal_toks = []
                for ti in range(2):
                    g = 2 * e + ti
                    xgs = sxg[:, 2048 + 1024 * ti:2048 + 1024 * (ti + 1)]
                    t_ga = P.op("gpsimd", lambda gq, g=g, xgs=xgs:
                                gq.indirect_dma_start(
                        xgs, None, scr_xn,
                        bass.IndirectOffsetOnAxis(
                            ap=stabTu[:, g:g + 1], axis=0),
                        bounds_check=regs["b_tok"], oob_is_err=False),
                        sem=f"gi{ti}", amt=16,
                        waits=[t_tab, xg_tok[ti], t_arena1_z,
                               yab_tok[2 * ti], yab_tok[2 * ti + 1],
                               front_tok["t_xnst"]])
                    last_ga[ti] = t_ga
                    for g2 in range(2):
                        bk = 6 + g2
                        for dq in range(4):
                            dblk = 4 * g2 + dq
                            tw_ = P.op("tensor", lambda tn, xgs=xgs,
                                       dblk=dblk, dq=dq, bk=bk:
                                       tn.transpose(
                                pb[bk][0:128, 128 * dq:128 * (dq + 1)],
                                xgs[:, 128 * dblk:128 * (dblk + 1)],
                                sidf[:]),
                                sem="pe" if dq == 3 else None,
                                waits=([t_ga, bank_r[bk]]
                                       if dq == 0 else []))
                        bank_w[bk] = tw_
                        tok_ = evac(
                            "vector", lambda v, e=e, ti=ti, g2=g2, bk=bk:
                            v.tensor_copy(
                                swit[e % 2][:].rearrange(
                                    "p (d cc) -> p d cc", d=8)
                                [:, 4 * g2:4 * (g2 + 1),
                                 128 * ti:128 * (ti + 1)],
                                pb[bk][0:128, 0:512].rearrange(
                                    "p (d cc) -> p d cc", d=4)),
                            bk, extra_waits=[xgt_tok[e % 2],
                                             wit_mm_tok[e % 2]])
                    nonlocal_toks.append(tok_)
                    xg_tok[ti] = tw_
                return nonlocal_toks[-1]

            def emit_mm1s(e, slot_, tok_, t_xgt):
                t_relu_last = None
                t_h = None
                for fb in range(16):
                    bk = 4 + fb % 2
                    t_h = mm_group(
                        [(pb[bk][0:128, 0:256],
                          pool[slot_][:, 2048 * dc + 128 * fb:
                                      2048 * dc + 128 * (fb + 1)],
                          swit[e % 2][:, 256 * dc:256 * (dc + 1)])
                         for dc in range(8)],
                        bk, extra_waits=[tok_, t_xgt])
                    t_relu_last = evac(
                        "scalar", lambda s, fb=fb, bk=bk: s.activation(
                            shts[:, 256 * fb:256 * (fb + 1)],
                            pb[bk][0:128, 0:256], AF.Relu), bk,
                        extra_waits=([t_mm2_prev] if fb == 0 else ()))
                pool_tok[slot_] = t_h
                xgt_tok[e % 2] = t_h
                return t_relu_last

            def emit_mm2s(e, slot_, tok_, t_relu, t_smix_dep):
                nonlocal t_yscat_all
                last = None
                for ti in range(2):
                    bks = (0, 1) if ti == 0 else (2, 3)
                    for h2 in range(2):
                        bk = bks[h2]
                        last = mm_group(
                            [(pb[bk][0:128, 0:512],
                              shts[:, 256 * fc + 128 * ti:
                                   256 * fc + 128 * (ti + 1)],
                              pool[slot_][:, 1024 * fc + 512 * h2:
                                          1024 * fc + 512 * (h2 + 1)])
                             for fc in range(16)],
                            bk, extra_waits=[t_relu, tok_] +
                            ([t_smix_dep] if e == 0 else []))
                        tk_ = evac("vector",
                                   lambda v, ti=ti, h2=h2, bk=bk:
                                   v.tensor_copy(
                            syp[:, 1024 * ti + 512 * h2:
                                1024 * ti + 512 * (h2 + 1)],
                            pb[bk][0:128, 0:512]), bk,
                            extra_waits=[yp_tok[ti]] if h2 == 0 else ())
                    g = 2 * e + ti
                    _ioa = bass.IndirectOffsetOnAxis(
                        ap=stabTu[:, 16 + g:17 + g], axis=0)
                    _vap = syp[:, 1024 * ti:1024 * (ti + 1)]
                    t_sc = P.op("gpsimd", lambda gq, _ioa=_ioa, _vap=_vap:
                                gq.indirect_dma_start(
                        scr_yab, _ioa, _vap, None,
                        bounds_check=regs["b_slot"], oob_is_err=False),
                        sem=f"gs{ti}", amt=16,
                        waits=[tk_, t_tab, yab_fin[0], yab_fin[1]])
                    yp_tok[ti] = t_sc
                    t_yscat_all = t_sc
                pool_tok[slot_] = last
                return last

            def load_w1(e):
                if c == 0:
                    s_, t_, w1_scr_tok[e] = pool_load(
                        w1_ap(e, w1), store_to=w1_ap(e, scr_w1))
                else:
                    s_, t_, _ = pool_load(w1_ap(e, scr_w1),
                                          [w1_scr_tok[e]],
                                          eng="sync")
                return s_, t_

            def load_w2(e):
                if c == 0:
                    s_, t_, w2_scr_tok[e] = pool_load(
                        w2_ap(e, w2), store_to=w2_ap(e, scr_w2))
                else:
                    s_, t_, _ = pool_load(w2_ap(e, scr_w2),
                                          [w2_scr_tok[e]],
                                          eng="sync")
                return s_, t_

            w2_slot, t_w2 = load_w2(0)
            t_xgt_cur = emit_xgt(0)
            t_xgt_next = None
            t_yin = None
            for hb in range(16):
                t_yin = P.op("vector", lambda v, hb=hb:
                             v.scalar_tensor_tensor(
                    sp1[:, CH * hb:CH * (hb + 1)], ssbc[:],
                    sDVT[:, hb:hb + 1], sp1[:, CH * hb:CH * (hb + 1)],
                    ALU.add, ALU.mult), sem="dv" if hb == 15 else None,
                    waits=[t_sbc, t_dvt] if hb == 0 else ())
            t_y = None
            t_smix_done = None
            for t in range(NT):
                bks = (0, 1) if t % 2 == 0 else (2, 3)
                for h2 in range(2):
                    bk = bks[h2]
                    t_y = mm_group(
                        [(pb[bk][0:128, 0:512],
                          sp1[:, CH * hb2 + T * t:CH * hb2 + T * (t + 1)],
                          pool[wout_slot][:, 1024 * hb2 + 512 * h2:
                                          1024 * hb2 + 512 * (h2 + 1)])
                         for hb2 in range(16)],
                        bk, extra_waits=[t_yin, t_wout])
                    evac("vector", lambda v, bk=bk, h2=h2: v.tensor_tensor(
                        stmp[:], pb[bk][0:128, 0:512],
                        sBOUT[:, 512 * h2:512 * (h2 + 1)], ALU.add), bk)
                    t_smix_done = P.op(
                        "vector", lambda v, t=t, h2=h2: v.tensor_tensor(
                            smix[:, 1024 * t + 512 * h2:
                                 1024 * t + 512 * (h2 + 1)],
                            stmp[:],
                            sxn[:, 1024 * t + 512 * h2:
                                1024 * t + 512 * (h2 + 1)], ALU.add),
                        sem="dv")
            if dbg and c == 0:
                t_dmp = dump(dbg_yin, sp1[:], [t_yin, t_dmp])
                t_dmp = dump(dbg_smix, smix[:], [t_smix_done, t_dmp])
            pool_tok[wout_slot] = t_y
            t_p1_reader = t_y if not (dbg and c == 0) else t_dmp

            cur_xgt = t_xgt_cur
            t_relu = emit_mm1s(0, w1_slot, t_w1, cur_xgt)
            nw1_slot, nt_w1 = load_w1(1)
            t_xgt_next = emit_xgt(1)
            for e in range(E):
                if e > 0:
                    t_relu = emit_mm1s(e, w1_slot, t_w1, cur_xgt)
                w2_next = None
                if e < E - 1:
                    w1_slot, t_w1 = nw1_slot, nt_w1
                    cur_xgt = t_xgt_next
                    w2_next = load_w2(e + 1)
                    if e < E - 2:
                        t_xgt_next = emit_xgt(e + 2)
                t_mm2_last = emit_mm2s(e, w2_slot, t_w2, t_relu,
                                       t_smix_done)
                if e < E - 1:
                    w2_slot, t_w2 = w2_next
                    if e < E - 2:
                        nw1_slot, nt_w1 = load_w1(e + 2)
            t_mm2_prev = t_mm2_last
            t_sht_reader = t_mm2_last
            scrxn_reader[c % 2] = (last_ga[0], last_ga[1])
            if c < n_chunks - 1:
                emit_front_stats(c + 1,
                                 [t_smix_done, su_scrib["t"], su_tok[2],
                                  su_tok[3], sub_silu_tok[2],
                                  sub_silu_tok[3]],
                                 t_x_next)
            if dbg and c == 0:
                t_dmp = dump(dbg_yab, scr_yab[:],
                             [t_yscat_all, yp_tok[0], yp_tok[1], t_dmp])
            if c < n_chunks - 1:
                for _t in (0, 2, 1, 3):
                    emit_front_tile(c + 1, _t, first=(_t == 0))
            t_store = None
            t_fin = None
            t_g = None
            t_fin_tiles = [None] * NT
            sxe_tok = [None, None]
            soutb_tok = [None, None]
            t_su_free = [su_silu_tok[0], su_silu_tok[1], t_smix_done,
                         su_scrib["t"], su_tok[2], su_tok[3],
                         sub_silu_tok[2], sub_silu_tok[3]]
            yab = ar1b
            first_e = True
            for t in (0, 2, 1, 3):
                w = t % 2
                r0 = c * CH + t * T
                ya_s = yab[:, 4096 + 2048 * w:4096 + 2048 * w + 1024]
                yb_s = yab[:, 4096 + 2048 * w + 1024:4096 + 2048 * (w + 1)]
                t_ya = P.op("scalar", lambda s, t=t, ya_s=ya_s: s.dma_start(
                    ya_s, scr_yab[128 * t:128 * (t + 1), :]),
                    sem=f"gy{w}", amt=16,
                    waits=[t_yscat_all, yp_tok[0], yp_tok[1],
                           yab_tok[2 * w], xg_tok[0], xg_tok[1]])
                t_yb = P.op("scalar", lambda s, t=t, yb_s=yb_s: s.dma_start(
                    yb_s, scr_yab[512 + 128 * t:512 + 128 * (t + 1), :]),
                    sem=f"gy{w}", amt=16, waits=[yab_tok[2 * w + 1]])
                t_xe = P.op("scalar", lambda s, r0=r0, w=w: s.dma_start(
                    sxe[:, 1024 * w:1024 * (w + 1)], x_d[r0:r0 + T, :]),
                    sem=f"gy{w}", amt=16,
                    waits=[sxe_tok[w], t_p1_reader])
                t_mul = [None, None]
                tko = None
                for h in range(2):
                    P.op("vector", lambda v, t=t, h=h, ya_s=ya_s:
                         v.tensor_scalar(
                        su[h][:], ya_s[:, 512 * h:512 * (h + 1)],
                        swab[:, t:t + 1], None, ALU.mult),
                        waits=[t_ya, t_yb, t_xe, t_wab] +
                        (t_su_free if first_e else []))
                    tko = P.op("vector", lambda v, t=t, h=h, yb_s=yb_s:
                               v.scalar_tensor_tensor(
                        su[h][:], yb_s[:, 512 * h:512 * (h + 1)],
                        swab[:, 4 + t:5 + t], su[h][:],
                        ALU.mult, ALU.add), sem="dv")
                first_e = False
                yab_tok[2 * w] = tko
                yab_tok[2 * w + 1] = tko
                t_out_stt[t] = tko
                for h in range(2):
                    t_g = P.op("scalar", lambda s, h=h: s.activation(
                        su[h][:], su[h][:], AF.Silu),
                        sem="ac", waits=[tko, t_sout_reader
                                         if (dbg and c == 0) else None])
                    su_silu_tok[h] = t_g
                    t_mul[h] = P.op("vector", lambda v, t=t, w=w, h=h:
                                    v.tensor_tensor(
                        soutb[:, 2048 + 1024 * w + 512 * h:
                              2048 + 1024 * w + 512 * (h + 1)],
                        smix[:, 1024 * t + 512 * h:
                             1024 * t + 512 * (h + 1)],
                        su[h][:], ALU.mult), sem="dv",
                        waits=[t_g, soutb_tok[0] if h == 0 else None,
                               soutb_tok[1] if h == 0 else None,
                               t_smix_done])
                t_fin = P.op("vector", lambda v, w=w: v.tensor_tensor(
                    soutb[:, 2048 + 1024 * w:2048 + 1024 * (w + 1)],
                    soutb[:, 2048 + 1024 * w:2048 + 1024 * (w + 1)],
                    sxe[:, 1024 * w:1024 * (w + 1)], ALU.add),
                    sem="dv", waits=[t_xe, t_mul[0], t_mul[1]])
                sxe_tok[w] = t_fin
                yab_fin[w] = t_xe
                t_fin_tiles[t] = t_fin
                t_store = P.op("scalar", lambda s, r0=r0, w=w: s.dma_start(
                    out_d[r0:r0 + T, :],
                    soutb[:, 2048 + 1024 * w:2048 + 1024 * (w + 1)]),
                    sem="go", amt=16, waits=[t_fin])
                soutb_tok[w] = t_store
            t_sout_reader = t_fin
            t_e_bufs_reader = t_store

            swit_tok[0] = xgt_tok[0]
            swit_tok[1] = xgt_tok[1]
            front_tok["xnt_reader"] = t_xnt_reader

        P.streams["sync"].append(
            lambda s: s.wait_ge(sems["sd"], P.cnt["sd"]))
        for _sn in ("gt", "gy0", "gy1", "ap0", "ap1", "ap2",
                    "aw0", "aw1", "ar"):
            if P.cnt[_sn]:
                P.streams["sync"].append(
                    lambda s, _sn=_sn: s.wait_ge(sems[_sn], P.cnt[_sn]))
        P.streams["gpsimd"].append(
            lambda g: g.wait_ge(sems["gd"], P.cnt["gd"]))
        for _sn in ("gi0", "gi1", "gs0", "gs1", "gx", "gn0", "gn1", "gz"):
            if P.cnt[_sn]:
                P.streams["gpsimd"].append(
                    lambda g, _sn=_sn: g.wait_ge(sems[_sn], P.cnt[_sn]))
        P.streams["scalar"].append(
            lambda s: s.wait_ge(sems["ad"], P.cnt["ad"]))
        P.streams["scalar"].append(
            lambda s: s.wait_ge(sems["go"], P.cnt["go"]))
        P.replay(block)
    return nc


_NC_CACHE = {}


def _get_nc():
    if "nc" not in _NC_CACHE:
        _NC_CACHE["nc"] = build_nc()
    return _NC_CACHE["nc"]


def kernel(L0_attn_x, L0_attn_W_in, L0_attn_b_in, L0_attn_conv_w,
           L0_attn_conv_b, L0_attn_W_ssm, L0_attn_b_ssm, L0_attn_A,
           L0_attn_Bv, L0_attn_Cv, L0_attn_Dv, L0_attn_W_out, L0_attn_b_out,
           L0_ffn_Wg, L0_ffn_W1, L0_ffn_W2):
    nc = _get_nc()
    wmap = {
        "w_in": np.ascontiguousarray(L0_attn_W_in, dtype=np.float32),
        "b_in": np.ascontiguousarray(L0_attn_b_in, dtype=np.float32),
        "conv_w": np.ascontiguousarray(L0_attn_conv_w, dtype=np.float32),
        "conv_b": np.ascontiguousarray(L0_attn_conv_b, dtype=np.float32),
        "w_ssm": np.ascontiguousarray(L0_attn_W_ssm, dtype=np.float32),
        "b_ssm": np.ascontiguousarray(L0_attn_b_ssm, dtype=np.float32),
        "A": np.ascontiguousarray(L0_attn_A, dtype=np.float32),
        "Bv": np.ascontiguousarray(L0_attn_Bv, dtype=np.float32),
        "Cv": np.ascontiguousarray(L0_attn_Cv, dtype=np.float32),
        "Dv": np.ascontiguousarray(L0_attn_Dv, dtype=np.float32),
        "w_out": np.ascontiguousarray(L0_attn_W_out, dtype=np.float32),
        "b_out": np.ascontiguousarray(L0_attn_b_out, dtype=np.float32),
        "wg": np.ascontiguousarray(L0_ffn_Wg, dtype=np.float32),
        "w1": np.ascontiguousarray(L0_ffn_W1, dtype=np.float32),
        "w2": np.ascontiguousarray(L0_ffn_W2, dtype=np.float32),
    }
    xs = np.ascontiguousarray(L0_attn_x, dtype=np.float32)
    in_maps = [dict(wmap, x=xs[i]) for i in range(8)]
    res = run_bass_kernel_spmd(nc, in_maps, core_ids=list(range(8)))
    return np.stack([r["out"] for r in res.results], axis=0)


# version 27
# speedup vs baseline: 1.1071x; 1.0732x over previous
import contextlib
import numpy as np

import concourse.bass as bass
import concourse.mybir as mybir
from concourse.alu_op_type import AluOpType as ALU
from concourse.bass_utils import run_bass_kernel_spmd

F32 = mybir.dt.float32
BF16 = mybir.dt.bfloat16
I32 = mybir.dt.int32
AF = mybir.ActivationFunctionType
AX = mybir.AxisListType

S = 4096
D = 1024
H = 2048
Z = 2 * H
N = 32
E = 8
FF = 2048
CH = 512
T = 128
NT = CH // T
NCH = S // CH


class Sched:

    ENGS = ("sync", "gpsimd", "tensor", "vector", "scalar")

    def __init__(self, nc, sems):
        self.nc = nc
        self.sems = sems
        self.cnt = {k: 0 for k in sems}
        self.streams = {e: [] for e in self.ENGS}
        self.last_wait = {e: {} for e in self.ENGS}

    def wait(self, eng, tok):
        if tok is None:
            return
        sem, val = tok
        if self.last_wait[eng].get(sem, 0) >= val:
            return
        self.last_wait[eng][sem] = val
        h = self.sems[sem]
        self.streams[eng].append(lambda e, h=h, val=val: e.wait_ge(h, val))

    def waits(self, eng, toks):
        best = {}
        for t in toks:
            if t is None:
                continue
            sem, val = t
            if val > best.get(sem, 0):
                best[sem] = val
        for sem, val in best.items():
            self.wait(eng, (sem, val))

    def op(self, eng, fn, sem=None, amt=1, waits=()):
        self.waits(eng, waits)
        if sem is None:
            self.streams[eng].append(lambda e, fn=fn: fn(e))
            return None
        self.cnt[sem] += amt
        tok = (sem, self.cnt[sem])
        h = self.sems[sem]

        def wrapped(e, fn=fn, h=h, amt=amt):
            fn(e).then_inc(h, amt)
        self.streams[eng].append(wrapped)
        return tok

    def replay(self, block):
        for eng in self.ENGS:
            stream = self.streams[eng]

            def make(stream):
                def body(h):
                    for fn in stream:
                        fn(h)
                return body
            getattr(block, eng)(make(stream))


def build_nc(n_chunks=NCH, dbg=False):
    nc = bass.Bass("TRN2", target_bir_lowering=False)
    S_ = n_chunks * CH

    x_d = nc.dram_tensor("x", [S_, D], F32, kind="ExternalInput").ap()
    w_in = nc.dram_tensor("w_in", [Z, D], F32, kind="ExternalInput").ap()
    b_in = nc.dram_tensor("b_in", [Z], F32, kind="ExternalInput").ap()
    conv_w = nc.dram_tensor("conv_w", [H, 1, 4], F32, kind="ExternalInput").ap()
    conv_b = nc.dram_tensor("conv_b", [H], F32, kind="ExternalInput").ap()
    w_ssm = nc.dram_tensor("w_ssm", [N, H], F32, kind="ExternalInput").ap()
    b_ssm = nc.dram_tensor("b_ssm", [N], F32, kind="ExternalInput").ap()
    A_d = nc.dram_tensor("A", [N, N], F32, kind="ExternalInput").ap()
    Bv_d = nc.dram_tensor("Bv", [N], F32, kind="ExternalInput").ap()
    Cv_d = nc.dram_tensor("Cv", [N], F32, kind="ExternalInput").ap()
    Dv_d = nc.dram_tensor("Dv", [H], F32, kind="ExternalInput").ap()
    w_out = nc.dram_tensor("w_out", [D, H], F32, kind="ExternalInput").ap()
    b_out = nc.dram_tensor("b_out", [D], F32, kind="ExternalInput").ap()
    wg = nc.dram_tensor("wg", [D, E], F32, kind="ExternalInput").ap()
    w1 = nc.dram_tensor("w1", [E, D, FF], F32, kind="ExternalInput").ap()
    w2 = nc.dram_tensor("w2", [E, FF, D], F32, kind="ExternalInput").ap()
    out_d = nc.dram_tensor("out", [S_, D], F32, kind="ExternalOutput").ap()

    if dbg:
        dbg_xn = nc.dram_tensor("dbg_xn", [128, 4096], F32, kind="ExternalOutput").ap()
        dbg_xnt = nc.dram_tensor("dbg_xnt", [128, 4096], F32, kind="ExternalOutput").ap()
        dbg_p1 = nc.dram_tensor("dbg_p1", [128, 8192], F32, kind="ExternalOutput").ap()
        dbg_vt = nc.dram_tensor("dbg_vt", [32, 512], F32, kind="ExternalOutput").ap()
        dbg_sbc = nc.dram_tensor("dbg_sbc", [128, 512], F32, kind="ExternalOutput").ap()
        dbg_ht = nc.dram_tensor("dbg_ht", [32, 33], F32, kind="ExternalOutput").ap()
        dbg_smix = nc.dram_tensor("dbg_smix", [128, 4096], F32, kind="ExternalOutput").ap()
        dbg_cw = nc.dram_tensor("dbg_cw", [128, 32], F32, kind="ExternalOutput").ap()
        dbg_out = nc.dram_tensor("dbg_out", [128, 4096], F32, kind="ExternalOutput").ap()
        dbg_yin = nc.dram_tensor("dbg_yin", [128, 8192], F32, kind="ExternalOutput").ap()
        dbg_btz = nc.dram_tensor("dbg_btz", [128, 4096], F32, kind="ExternalOutput").ap()
        dbg_mr = nc.dram_tensor("dbg_mr", [128, 1024], F32, kind="ExternalOutput").ap()
        dbg_cmat = nc.dram_tensor("dbg_cmat", [32, 130], F32, kind="ExternalOutput").ap()
        dbg_sx = nc.dram_tensor("dbg_sx", [128, 4096], F32, kind="ExternalOutput").ap()
        dbg_regs = nc.dram_tensor("dbg_regs", [128, 8], F32, kind="ExternalOutput").ap()
        dbg_idx = nc.dram_tensor("dbg_idx", [128, 16], F32, kind="ExternalOutput").ap()
        dbg_wab = nc.dram_tensor("dbg_wab", [128, 8], F32, kind="ExternalOutput").ap()
        dbg_m12a = nc.dram_tensor("dbg_m12a", [128, 32], F32, kind="ExternalOutput").ap()
        dbg_m12b = nc.dram_tensor("dbg_m12b", [128, 32], F32, kind="ExternalOutput").ap()
        dbg_slot = nc.dram_tensor("dbg_slot", [128, 64], F32, kind="ExternalOutput").ap()
        dbg_tab = nc.dram_tensor("dbg_tab", [128, 32], F32, kind="ExternalOutput").ap()
        dbg_yab = nc.dram_tensor("dbg_yab", [1024, 1024], F32, kind="ExternalOutput").ap()
    scr_wit = nc.dram_tensor("scr_wit", [16, 128, 2048], BF16,
                             kind="Internal").ap()
    scr_wout = nc.dram_tensor("scr_wout", [16, 128, D], BF16,
                              kind="Internal").ap()
    scr_w1 = nc.dram_tensor("scr_w1", [E, D, FF], BF16, kind="Internal").ap()
    scr_w2 = nc.dram_tensor("scr_w2", [E, FF, D], BF16, kind="Internal").ap()
    scr_btz = nc.dram_tensor("scr_btz", [128, 4096], BF16, kind="Internal").ap()
    scr_idx = nc.dram_tensor("scr_idx", [2048], mybir.dt.float32,
                             kind="Internal").ap()
    scr_idx2 = nc.dram_tensor("scr_idx2", [2048], mybir.dt.float32,
                              kind="Internal").ap()
    scr_yab = nc.dram_tensor("scr_yab", [1024, D], BF16, kind="Internal").ap()
    scr_xn = nc.dram_tensor("scr_xn", [2 * CH, D], BF16, kind="Internal").ap()

    with contextlib.ExitStack() as st:
        def sb(name, shape, dt):
            return st.enter_context(nc.sbuf_tensor(name, shape, dt))

        def ps(name, shape, dt):
            return st.enter_context(nc.psum_tensor(name, shape, dt))

        pool = [sb(f"pool{i}", [128, 16384], BF16) for i in range(3)]
        btz = sb("btz", [128, 32 * T], BF16)
        mr = sb("mr", [128, 32 * N], BF16)
        wssmT = sb("wssmT", [128, 16 * N], BF16)
        sBOUT = sb("sBOUT", [128, D], BF16)
        xnt = sb("xnt", [128, 8 * CH], BF16)
        smix = sb("smix", [128, NT * D], BF16)
        swit = [sb(f"swit{i}", [128, 2048], BF16) for i in range(2)]
        ssbc = sb("ssbc", [128, CH], F32)
        ar1 = sb("ar1", [128, 4096], F32)
        ar2 = sb("ar2", [128, 4096], F32)
        ar3 = sb("ar3", [128, 4096], F32)
        U32 = mybir.dt.uint32
        ar1b = ar1.bitcast(BF16)
        shts = ar1b
        sxg = ar1
        syr = ar1b
        sxn = ar2
        sout = ar2
        sp1 = ar3.bitcast(BF16)
        sxe = ar3
        soutb = ar3
        syp = ar3.bitcast(BF16)
        spad = [sb(f"spad{i}", [128, CH + 8], F32) for i in range(2)]
        ssg = [sb(f"ssg{i}", [128, CH], BF16) for i in range(2)]
        su = [sb(f"su{i}", [128, CH], F32) for i in range(2)]
        sub = [sb(f"sub{i}", [128, CH], BF16) for i in range(4)]
        stmp = su[0]
        sidf = sb("sidf", [128, 128], F32)
        sidb = sb("sidb", [128, 128], BF16)
        cmatT = sb("cmatT", [N, T + 2], F32)
        cb = [sb(f"cb{i}", [N, N + 1], F32) for i in range(2)]
        sPm = sb("sPm", [N, N], F32)
        sA = sb("sA", [N, N], F32)
        sHT = sb("sHT", [N, NCH * NT + 1], F32)
        sVT = sb("sVT", [N, CH], F32)
        svr = sb("svr", [128, T], BF16)
        sS = sb("sS", [NT, T], F32)
        s_row = sb("s_row", [1, CH], F32)
        sU = sb("sU", [NT, N], F32)
        s32 = sb("s32", [N, 16], F32)
        sUT = s32[:, 8:8 + NT]
        sBINT = sb("sBINT", [128, 32], F32)
        sCBT = sb("sCBT", [128, 16], F32)
        sDVT = sb("sDVT", [128, 16], F32)
        sCWS = sb("sCWS", [128, 64], F32)
        sWg = sb("sWg", [128, 64], BF16)
        shalo = sb("shalo", [128, 16 * 3], F32)
        sones = sb("sones", [1, 128], F32)
        sones8 = sb("sones8", [128, 8], F32)
        sbv = s32[:, 0:1]
        scv = s32[:, 1:2]
        sbs = s32[:, 2:3]
        sbsbv = s32[:, 3:4]
        smixf = smix.bitcast(F32)
        sbrow = smixf
        srawT = su[1]
        swsraw = pool[2].bitcast(F32)
        srt = [sb(f"srt{i}", [128, 32], F32) for i in range(6)]

        sslotm = sb("sslotm", [128, 64], F32)
        sidxf = sb("sidxf", [128, 16], F32)
        sidxu = sb("sidxu", [128, 16], U32)
        siotg = sb("siotg", [128, 4], F32)
        siotl = sb("siotl", [128, 4], F32)
        shug = sb("shug", [128, 16], F32)
        stabT = sb("stabT", [128, 32], F32)
        stabTi = sb("stabTi", [128, 32], U32)
        swab = sb("swab", [128, 8], F32)
        sb8 = sb("sb8", [8, 4], F32)
        smaskT = ssbc[0:8, 0:512]
        scum = su[0][0:8, 0:512]
        sregs = sb("sregs", [128, 8], F32)
        sregs2 = sb("sregs2", [128, 8], F32)
        srm = [sregs[:, i:i + 1] for i in range(4)]

        pb = [ps(f"pb{i}", [128, 512], F32) for i in range(8)]
        pbb = [b.bitcast(BF16) for b in pb]

        sems = {}
        for s_name in ("pe", "dv", "ac", "gp", "sd", "gd", "ad",
                       "gi0", "gi1", "gs0", "gs1", "gx", "gt",
                       "gy0", "gy1", "gn0", "gn1", "go", "gz",
                       "ap0", "ap1", "ap2", "aw0", "aw1", "ar"):
            sems[s_name] = st.enter_context(nc.semaphore(s_name))
        block = st.enter_context(nc.Block())
        P = Sched(nc, sems)

        bank_w = [None] * 8
        bank_r = [None] * 8

        def mm_group(items, bank_i, extra_waits=(), sem="pe"):
            P.waits("tensor", list(extra_waits) + [bank_r[bank_i]])
            n = len(items)
            tok = None
            for i, (o, l, r) in enumerate(items):
                s0, s1 = (i == 0), (i == n - 1)
                if s1:
                    tok = P.op("tensor",
                               lambda t, o=o, l=l, r=r, s0=s0, s1=s1:
                               t.matmul(o, l, r, start=s0, stop=s1), sem=sem)
                else:
                    P.op("tensor",
                         lambda t, o=o, l=l, r=r, s0=s0, s1=s1:
                         t.matmul(o, l, r, start=s0, stop=s1))
            bank_w[bank_i] = tok
            return tok

        def evac(eng, fn, bank_i, extra_waits=(), sem=None):
            sem = sem or ("dv" if eng == "vector" else "ac")
            tok = P.op(eng, fn, sem=sem,
                       waits=list(extra_waits) + [bank_w[bank_i]])
            bank_r[bank_i] = tok
            return tok

        regs = {}

        def _alloc_regs(g):
            b_idx = g.alloc_register(name="b_idx")
            regs["b_idx"] = b_idx
            g.reg_mov(regs["b_idx"], 2047)
            b_tok = g.alloc_register(name="b_tok")
            regs["b_tok"] = b_tok
            g.reg_mov(regs["b_tok"], 2 * CH - 1)
            b_slot = g.alloc_register(name="b_slot")
            regs["b_slot"] = b_slot
            g.reg_mov(regs["b_slot"], 1023)
        P.streams["gpsimd"].append(_alloc_regs)

        t_iota = P.op("gpsimd", lambda g: g.iota(
            sidf[:], [[-1, 128]], base=0, channel_multiplier=1,
            allow_small_or_imprecise_dtypes=True), sem="gp")
        t_ones = P.op("gpsimd", lambda g: g.memset(sones[:], 1.0), sem="gp")
        t_ones8 = P.op("gpsimd", lambda g: g.memset(sones8[:], 1.0), sem="gp")
        t_halo0 = P.op("gpsimd", lambda g: g.memset(shalo[:], 0.0), sem="gp")
        t_b8 = P.op("gpsimd", lambda g: g.iota(
            sb8[0:8, 0:1], [[1, 1]], base=-1, channel_multiplier=256,
            allow_small_or_imprecise_dtypes=True), sem="gp")
        t_b8 = P.op("gpsimd", lambda g: g.iota(
            sb8[0:8, 2:3], [[1, 1]], base=255, channel_multiplier=256,
            allow_small_or_imprecise_dtypes=True), sem="gp")
        t_iotl = P.op("gpsimd", lambda g: g.iota(
            siotl[:], [[128, 4]], base=0, channel_multiplier=1,
            allow_small_or_imprecise_dtypes=True), sem="gp")
        t_hug = P.op("gpsimd", lambda g: g.memset(
            shug[:], float(1 << 20)), sem="gp")
        t_arena1_z = P.op("gpsimd", lambda g: g.memset(ar1[:], 0.0),
                          sem="gp")


        t_idf = P.op("vector", lambda v: v.tensor_scalar(
            sidf[:], sidf[:], 0, None, ALU.is_equal), sem="dv",
            waits=[t_iota])
        t_idb = P.op("vector", lambda v: v.tensor_copy(
            sidb[:], sidf[:]), sem="dv", waits=[t_idf])
        t_ht0 = P.op("vector", lambda v: v.memset(sHT[:, 0:1], 0.0), sem="dv")

        def ld(dst, src, waits=()):
            return P.op("sync", lambda s, d=dst, sr=src: s.dma_start(d, sr),
                        sem="sd", amt=16, waits=waits)

        t_a = ld(sA[:], A_d)
        t_bv = ld(sbv, bass.AP(Bv_d.tensor, 0, [[1, N], [1, 1]]))
        t_cv = ld(scv, bass.AP(Cv_d.tensor, 0, [[1, N], [1, 1]]))
        t_bs = ld(sbs, bass.AP(b_ssm.tensor, 0, [[1, N], [1, 1]]))
        t_binr = ld(srawT[0:32, 0:128],
                    bass.AP(b_in.tensor, 0, [[128, 32], [1, 128]]))
        t_cws = ld(sCWS[:],
                   bass.AP(conv_w.tensor, 0, [[4, 128], [512, 16], [1, 4]]))
        t_brow = ld(sbrow[0:1, 0:D], bass.AP(b_out.tensor, 0, [[D, 1], [1, D]]))
        t_wsr = ld(swsraw[0:N, 0:H], w_ssm)
        t_wg = P.op("gpsimd", lambda g: g.dma_start(
            sWg[:], bass.AP(wg.tensor, 0, [[8, 128], [1024, 8], [1, 8]])),
            sem="gd", amt=16)

        t_bsbv = P.op("vector", lambda v: v.tensor_tensor(
            sbsbv, sbs, sbv, ALU.mult), sem="dv",
            waits=[t_bv, t_bs])

        def small_transpose(dst_ap, src_ap, in_p, load_tok):
            tw = P.op("tensor", lambda t, src_ap=src_ap, in_p=in_p:
                      t.transpose(pb[6][0:128, 0:in_p], src_ap,
                                  sidf[0:in_p, 0:in_p]),
                      sem="pe", waits=[load_tok, t_idf, bank_r[6]])
            bank_w[6] = tw
            return evac("vector", lambda v, dst_ap=dst_ap, in_p=in_p:
                        v.tensor_copy(dst_ap, pb[6][0:128, 0:in_p]), 6)

        t_bint = small_transpose(sBINT[:], srawT[0:32, 0:128], 32, t_binr)
        t_cbr = ld(srawT[0:16, 0:128],
                   bass.AP(conv_b.tensor, 0, [[128, 16], [1, 128]]),
                   waits=[t_bint])
        t_cbt = small_transpose(sCBT[:], srawT[0:16, 0:128], 16, t_cbr)
        t_dvr = ld(srawT[0:16, 0:128],
                   bass.AP(Dv_d.tensor, 0, [[128, 16], [1, 128]]),
                   waits=[t_cbt])
        t_dvt = small_transpose(sDVT[:], srawT[0:16, 0:128], 16, t_dvr)

        t_wssm_pe = None
        for hb in range(16):
            tw = P.op("tensor", lambda t, hb=hb: t.transpose(
                pb[6][0:128, 0:32], swsraw[0:N, 128 * hb:128 * (hb + 1)],
                sidf[0:32, 0:32]), sem="pe",
                waits=[t_wsr, t_idf, bank_r[6]])
            bank_w[6] = tw
            t_wssm_pe = tw
            evac("vector", lambda v, hb=hb: v.tensor_copy(
                wssmT[:, 32 * hb:32 * (hb + 1)], pb[6][0:128, 0:32]), 6)

        for h2 in range(2):
            mm_group([(pb[h2][0:128, 0:512], sones[0:1, :],
                       sbrow[0:1, 512 * h2:512 * (h2 + 1)])], h2,
                     extra_waits=[t_brow, t_ones])
            evac("vector", lambda v, h2=h2: v.tensor_copy(
                sBOUT[:, 512 * h2:512 * (h2 + 1)], pb[h2][0:128, 0:512]), h2)

        t0 = P.op("vector", lambda v: v.tensor_copy(
            cb[0][:, 0:N], sidf[0:N, 0:N]), waits=[t_idf])
        t0b = P.op("vector", lambda v: v.tensor_copy(
            cb[0][:, N:N + 1], scv), sem="dv", waits=[t_cv])
        P.op("vector", lambda v: v.tensor_copy(cmatT[:, 0:1], scv))
        P.op("vector", lambda v: v.tensor_copy(
            mr[96:128, 31 * N:32 * N], sidf[0:N, 0:N]))
        chain_state = {"prev": t0b}

        def chain_step(k):
            tw = P.op("tensor", lambda t, k=k: t.matmul(
                pb[7][0:N, 0:N + 1], sA[:], cb[k % 2][:],
                start=True, stop=True),
                sem="pe", waits=[t_a, chain_state["prev"], bank_r[7]])
            bank_w[7] = tw
            te = evac("vector", lambda v, k=k: v.tensor_copy(
                cb[(k + 1) % 2][:], pb[7][0:N, 0:N + 1]), 7)
            kk = k + 1
            P.op("vector", lambda v, kk=kk: v.tensor_copy(
                cmatT[:, kk:kk + 1], cb[kk % 2][:, N:N + 1]), waits=[te])
            if kk <= T - 1:
                q = T - 1 - kk
                kc, j = q // 4, q % 4
                P.op("vector", lambda v, kk=kk, kc=kc, j=j: v.tensor_copy(
                    mr[32 * j:32 * (j + 1), N * kc:N * (kc + 1)],
                    cb[kk % 2][:, 0:N]))
            elif kk == T:
                P.op("vector", lambda v, kk=kk: v.tensor_copy(
                    sPm[:], cb[kk % 2][:, 0:N]))
            chain_state["prev"] = te

        chain_steps = [lambda k=k: chain_step(k) for k in range(T + 1)]

        def emit_btz(t_zero):
            P.wait("vector", t_zero)
            P.wait("vector", chain_state["prev"])
            t_btz = None
            for q in range(T):
                kc, j = q // 4, q % 4
                t_btz = P.op("vector", lambda v, kc=kc, j=j, q=q:
                             v.tensor_copy(
                    btz[32 * j:32 * (j + 1), T * kc + q:T * (kc + 1)],
                    cmatT[:, 0:T - q]), sem="dv" if q == T - 1 else None)
            return t_btz

        stage_a = pool[2].bitcast(F32)
        stage_b = pool[2]
        bst = {"store": [None, None], "first": True}
        wit_tok = [None] * 16
        wit_ready = [None] * 16
        wit_mm_tok = [None, None]
        wout_done = [None]

        wit_steps = []
        for hb_ in range(16):
            for half_ in range(2):
                def wstep(hb=hb_, half=half_):
                    w = half
                    zt = hb + 16 * half
                    extra = [t_wssm_pe] if bst["first"] else []
                    bst["first"] = False
                    t_l = P.op("sync", lambda s, zt=zt, w=w: s.dma_start(
                        stage_a[:, 2048 * w:2048 * w + 1024],
                        w_in[128 * zt:128 * (zt + 1), :]), sem="sd",
                        amt=16, waits=[bst["store"][w]] + extra)
                    ring = hb % 2
                    for g2 in range(2):
                        bk = 4 + g2
                        for dq in range(4):
                            dblk = 4 * g2 + dq
                            tw = P.op("tensor", lambda t, w=w, dblk=dblk,
                                      dq=dq, bk=bk: t.transpose(
                                pb[bk][0:128, 128 * dq:128 * (dq + 1)],
                                stage_a[:, 2048 * w + 128 * dblk:
                                        2048 * w + 128 * (dblk + 1)],
                                sidf[:]),
                                sem="pe" if dq == 3 else None,
                                waits=([t_l, t_idf, bank_r[bk]]
                                       if dq == 0 else []))
                        bank_w[bk] = tw
                        t_e = evac("vector", lambda v, ring=ring, half=half,
                                   bk=bk, g2=g2: v.tensor_copy(
                            swit[ring][:, 1024 * half + 512 * g2:
                                       1024 * half + 512 * (g2 + 1)],
                            pb[bk][0:128, 0:512]), bk,
                            extra_waits=[wit_mm_tok[ring]] if g2 == 0 else ())
                    t_st = P.op("sync", lambda s, hb=hb, half=half, ring=ring:
                                s.dma_start(
                        bass.AP(scr_wit.tensor,
                                hb * 128 * 2048 + half * 1024,
                                [[2048, 128], [1, 1024]]),
                        swit[ring][:, 1024 * half:1024 * half + 1024]),
                        sem="sd", amt=16, waits=[t_e])
                    bst["store"][w] = t_st
                    if half == 1:
                        wit_tok[hb] = t_st
                        wit_ready[hb] = t_e
                wit_steps.append(wstep)

        wout_steps = []
        for dt_ in range(8):
            def wostep(dt=dt_):
                w = dt % 2
                t_l = P.op("sync", lambda s, dt=dt, w=w: s.dma_start(
                    stage_a[:, 2048 * w:2048 * (w + 1)],
                    w_out[128 * dt:128 * (dt + 1), :]), sem="sd", amt=16,
                    waits=[bst["store"][w]])
                t_e = None
                for g4 in range(4):
                    bk = 4 + g4 % 2
                    for dq in range(4):
                        hb = 4 * g4 + dq
                        tw = P.op("tensor", lambda t, hb=hb, dq=dq, bk=bk,
                                  w=w: t.transpose(
                            pb[bk][0:128, 128 * dq:128 * (dq + 1)],
                            stage_a[:, 2048 * w + 128 * hb:
                                    2048 * w + 128 * (hb + 1)], sidf[:]),
                            sem="pe" if dq == 3 else None,
                            waits=([t_l, bank_r[bk]] if dq == 0 else []))
                    bank_w[bk] = tw
                    t_e = evac("vector", lambda v, w=w, bk=bk, g4=g4:
                               v.tensor_copy(
                        stage_b[:, 8192 + 2048 * w + 512 * g4:
                                8192 + 2048 * w + 512 * (g4 + 1)],
                        pb[bk][0:128, 0:512]), bk)
                t_st = P.op("sync", lambda s, dt=dt, w=w: s.dma_start(
                    bass.AP(scr_wout.tensor, dt * 128,
                            [[1024, 128], [128 * 1024, 16], [1, 128]]),
                    stage_b[:, 8192 + 2048 * w:8192 + 2048 * (w + 1)]),
                    sem="sd", amt=16, waits=[t_e])
                bst["store"][w] = t_st
                if dt == 7:
                    wout_done[0] = t_st
            wout_steps.append(wostep)

        pump_state = {"wit": 0, "wout": 0, "chain": 0}

        def pump(name, steps, n):
            i = pump_state[name]
            while n > 0 and i < len(steps):
                steps[i]()
                i += 1
                n -= 1
            pump_state[name] = i

        pool_rot = [0]
        pool_tok = [None, None, None]
        pool_stok = [None, None, None]
        w1_scr_tok = [None] * E
        w2_scr_tok = [None] * E
        swit_tok = [None] * 3
        t_sht_reader = None
        t_sout_reader = None
        t_p1_reader = None
        t_e_bufs_reader = None
        t_xnt_reader = None
        t_svr_reader = None
        t_ex_prev = None

        WOUT_AP = bass.AP(scr_wout.tensor, 0,
                          [[1024, 128], [128 * 1024, 16], [1, 1024]])

        def w1_ap(e, t):
            return bass.AP(t.tensor, e * D * FF,
                           [[2048, 128], [128 * 2048, 8], [1, 2048]])

        def w2_ap(e, t):
            return bass.AP(t.tensor, e * FF * D,
                           [[1024, 128], [128 * 1024, 16], [1, 1024]])

        def pool_load(src_ap, waits=(), store_to=None, eng="gpsimd",
                      sem=None):
            slot = pool_rot[0] % 3
            pool_rot[0] += 1
            tok = P.op(eng, lambda g, s=slot, src=src_ap: g.dma_start(
                pool[s][:, 0:16384], src), sem=f"ap{slot}", amt=16,
                waits=list(waits) + [pool_tok[slot], pool_stok[slot]])
            stok = None
            if store_to is not None:
                stok = P.op("sync", lambda s, sl=slot, d=store_to:
                            s.dma_start(d, pool[sl][:, 0:16384]),
                            sem="sd", amt=16, waits=[tok])
            pool_stok[slot] = stok
            return slot, tok, stok

        def dump(dst, src_ap, waits):
            return P.op("gpsimd", lambda g, d=dst, s=src_ap: g.dma_start(d, s),
                        sem="gd", amt=16, waits=waits)

        front_tok = {}
        scrxn_reader = {}

        def emit_xload(c, btz_reader_waits):
            return P.op("gpsimd", lambda g, c=c: g.dma_start(
                btz[:].rearrange("p (t d) -> p t d", t=NT),
                bass.AP(x_d.tensor, c * CH * D,
                        [[1024, 128], [128 * 1024, NT], [1, 1024]])),
                sem="gz", amt=16, waits=list(btz_reader_waits))

        def emit_front_stats(c, sq_waits, t_x):
            t_red = None
            t_sqa = None
            for t in range(NT):
                xs = btz[:, 1024 * t:1024 * (t + 1)]
                t_red = P.op("vector", lambda v, xs=xs, t=t: v.tensor_reduce(
                    sregs[:, t:t + 1], xs, AX.X, ALU.add),
                    sem="dv" if t == NT - 1 else None,
                    waits=[t_x])
                P.op("scalar", lambda s, xs=xs, t=t: s.activation(
                    su[0][:], xs[:, 0:512], AF.Square,
                    accum_out=sregs[:, 4 + t:5 + t]),
                    waits=[t_x] + (list(sq_waits) if t == 0 else []))
                t_sqa = P.op("scalar", lambda s, xs=xs, t=t: s.activation(
                    su[1][:], xs[:, 512:1024], AF.Square,
                    accum_out=sregs2[:, 4 + t:5 + t]),
                    sem="ac" if t == NT - 1 else None)
            su_silu_tok[0] = t_sqa
            su_silu_tok[1] = t_sqa
            tc = P.op("vector", lambda v: v.tensor_scalar(
                sregs2[:, 0:4], sregs[:, 0:4], 1.0 / D, None, ALU.mult),
                sem="dv", waits=[t_red])
            tc = P.op("vector", lambda v: v.scalar_tensor_tensor(
                sregs[:, 0:4], sregs2[:, 0:4], -1.0, sregs2[:, 0:4],
                ALU.mult, ALU.mult), sem="dv", waits=[tc])
            tc2 = P.op("vector", lambda v: v.tensor_tensor(
                sregs[:, 4:8], sregs[:, 4:8], sregs2[:, 4:8], ALU.add),
                sem="dv", waits=[t_sqa])
            tc2 = P.op("vector", lambda v: v.tensor_scalar(
                sregs[:, 4:8], sregs[:, 4:8], 1.0 / D, None, ALU.mult),
                sem="dv", waits=[tc2])
            tc = P.op("vector", lambda v: v.tensor_tensor(
                sregs2[:, 4:8], sregs[:, 4:8], sregs[:, 0:4], ALU.add),
                sem="dv", waits=[tc, tc2])
            tc = P.op("vector", lambda v: v.tensor_scalar(
                sregs2[:, 4:8], sregs2[:, 4:8], 1e-5, None, ALU.add),
                sem="dv", waits=[tc])
            tc = P.op("scalar", lambda s: s.activation(
                sregs[:, 4:8], sregs2[:, 4:8], AF.Sqrt), sem="ac",
                waits=[tc])
            t_rstd = P.op("vector", lambda v: v.reciprocal(
                sregs[:, 4:8], sregs[:, 4:8]), sem="dv", waits=[tc])
            front_tok["t_x"] = t_x
            front_tok["t_sq_last"] = t_sqa
            front_tok["t_rstd"] = t_rstd

        def emit_front_tile(c, t, first=False):
            t_rstd = front_tok["t_rstd"]
            par = c % 2
            xs = btz[:, 1024 * t:1024 * (t + 1)]
            t_xn = P.op("vector", lambda v, xs=xs, t=t: v.tensor_scalar(
                sxn[:, 1024 * t:1024 * (t + 1)], xs,
                sregs2[:, t:t + 1], sregs[:, 4 + t:5 + t],
                ALU.subtract, ALU.mult), sem="dv",
                waits=[t_rstd] +
                ([front_tok.get("t_xnst")] if first else []))
            t_xnst = P.op("gpsimd", lambda g, t=t, par=par: g.dma_start(
                scr_xn[par * CH + T * t:par * CH + T * (t + 1), :],
                sxn[:, 1024 * t:1024 * (t + 1)]),
                sem=f"gn{par}", amt=16,
                waits=[t_xn] + list(scrxn_reader.get(par, ())))
            t_xnt_all = None
            for g2 in range(2):
                bk = 4 + g2
                for dq in range(4):
                    dblk = 4 * g2 + dq
                    tw = P.op("tensor", lambda tn, t=t, dblk=dblk,
                              dq=dq, bk=bk: tn.transpose(
                        pb[bk][0:128, 128 * dq:128 * (dq + 1)],
                        sxn[:, 1024 * t + 128 * dblk:
                            1024 * t + 128 * (dblk + 1)], sidf[:]),
                        sem="pe" if dq == 3 else None,
                        waits=([t_xn, bank_r[bk]] if dq == 0 else []))
                bank_w[bk] = tw
                t_xnt_all = evac(
                    "vector", lambda v, t=t, bk=bk, g2=g2: v.tensor_copy(
                        xnt[:].rearrange("p (d ch) -> p d ch", d=8)
                        [:, 4 * g2:4 * (g2 + 1), T * t:T * (t + 1)],
                        pb[bk][0:128, 0:512].rearrange(
                            "p (d ch) -> p d ch", d=4)),
                    bk, extra_waits=[front_tok.get("xnt_reader")])
            front_tok["t_xn_last"] = t_xn
            front_tok["t_xnt_all"] = t_xnt_all
            front_tok["t_xnst"] = t_xnst

        t_btz = None
        t_mm2_prev = None
        t_iog_reader = None
        t_tab_reader = None
        t_stab_reader = None
        t_xgt_cur = None
        xg_tok = [None, None]
        t_yab_last = None
        yab_fin = [None, None]
        t_smaskT_war = None
        su_scrib = {"t": None}
        su_silu_tok = [None, None]
        xgt_tok = [None, None]
        last_ga = [None, None]
        yp_tok = [None, None]
        yab_tok = [None] * 4
        t_yscat_all = None
        for c in range(n_chunks):
            if c == 0:
                t_x0 = emit_xload(0, [None])
                emit_front_stats(0, [None], t_x0)
                for _t in range(NT):
                    emit_front_tile(0, _t, first=(_t == 0))
            t_xnt_all = front_tok["t_xnt_all"]
            t_ln_last = front_tok["t_xn_last"]
            t_sq_last = front_tok["t_sq_last"]
            if c == 0:
                w1_slot, t_w1, w1_scr_tok[0] = pool_load(
                    w1_ap(0, w1), store_to=w1_ap(0, scr_w1))
            else:
                wout_slot, t_wout, _ = pool_load(
                    WOUT_AP, [wout_done[0]], eng="scalar")
                w1_slot, t_w1, _ = pool_load(
                    w1_ap(0, scr_w1), [w1_scr_tok[0]],
                    eng="scalar")
                t_btz = P.op("sync", lambda s: s.dma_start(
                    btz[:], scr_btz), sem="sd", amt=16,
                    waits=[t_ln_last, t_sq_last, t_scrbtz])

            t_ssm_group = None
            t_p1_last = None
            su_tok = [None] * 4
            sub_tok = [None] * 4
            sub_silu_tok = [None] * 4
            spad_tok = [None, None]
            ssg_tok = [None, None]
            for hb in range(16):
                if c == 0:
                    pump("wit", wit_steps, 2)
                    pump("chain", chain_steps, 9)
                slot = hb % 2
                if c == 0:
                    t_wl = wit_ready[hb]
                else:
                    t_wl = P.op("sync", lambda s, hb=hb, slot=slot:
                                s.dma_start(
                        swit[slot][:],
                        bass.AP(scr_wit.tensor, hb * 128 * 2048,
                                [[2048, 128], [1, 2048]])),
                        sem=f"aw{slot}", amt=16,
                        waits=[swit_tok[slot], wit_tok[hb]])
                pba, pbg = (0, 1) if hb % 2 == 0 else (2, 3)
                t_xh = mm_group(
                    [(pb[pba][0:128, 0:512],
                      swit[slot][:, 128 * dblk:128 * (dblk + 1)],
                      xnt[:, CH * dblk:CH * (dblk + 1)])
                     for dblk in range(8)],
                    pba, extra_waits=[t_wl, t_xnt_all])
                t_gt = mm_group(
                    [(pb[pbg][0:128, 0:512],
                      swit[slot][:, 1024 + 128 * dblk:
                                 1024 + 128 * (dblk + 1)],
                      xnt[:, CH * dblk:CH * (dblk + 1)])
                     for dblk in range(8)],
                    pbg, extra_waits=[t_wl, t_xnt_all])
                swit_tok[slot] = t_gt
                wit_mm_tok[slot] = t_gt
                w = hb % 2
                t_xhe = evac("scalar", lambda s, hb=hb, w=w, pba=pba:
                             s.activation(
                    spad[w][:, 3:3 + CH], pb[pba][0:128, 0:512],
                    AF.Identity, bias=sBINT[:, hb:hb + 1]), pba,
                    extra_waits=[spad_tok[w], t_bint])
                t_sge = evac("scalar", lambda s, hb=hb, w=w, pbg=pbg:
                             s.activation(
                    ssg[w][:], pb[pbg][0:128, 0:512],
                    AF.Silu, bias=sBINT[:, 16 + hb:17 + hb]), pbg,
                    extra_waits=[ssg_tok[w]])
                P.op("vector", lambda v, hb=hb, w=w: v.tensor_copy(
                    spad[w][:, 0:3], shalo[:, 3 * hb:3 * (hb + 1)]),
                    waits=[t_xhe, t_halo0])
                P.op("vector", lambda v, hb=hb, w=w: v.tensor_copy(
                    shalo[:, 3 * hb:3 * (hb + 1)], spad[w][:, CH:CH + 3]))
                P.op("vector", lambda v, hb=hb, w=w: v.tensor_scalar(
                    su[w][:], spad[w][:, 0:CH], sCWS[:, 4 * hb:4 * hb + 1],
                    None, ALU.mult), waits=[t_cws, su_silu_tok[w]])
                t_conv = None
                for k in range(1, 4):
                    t_conv = P.op("vector", lambda v, hb=hb, w=w, k=k:
                                  v.scalar_tensor_tensor(
                        su[w][:], spad[w][:, k:k + CH],
                        sCWS[:, 4 * hb + k:4 * hb + k + 1], su[w][:],
                        ALU.mult, ALU.add),
                        sem="dv" if k == 3 else None)
                spad_tok[w] = t_conv
                t_us = P.op("scalar", lambda s, hb=hb, w=w: s.activation(
                    sub[hb % 4][:], su[w][:], AF.Silu,
                    bias=sCBT[:, hb:hb + 1]), sem="ac",
                    waits=[t_conv, t_cbt, sub_tok[hb % 4], su_tok[hb % 4]])
                if hb >= 2:
                    hs = hb - 2
                    t_ssm_group = P.op("tensor", lambda t, hs=hs: t.matmul(
                        pb[6][0:N, 0:512], wssmT[:, N * hs:N * (hs + 1)],
                        sub[hs % 4][:], start=(hs == 0), stop=False),
                        sem="pe",
                        waits=[sub_silu_tok[hs % 4]] +
                        ([bank_r[6]] if hs == 0 else []))
                    sub_tok[hs % 4] = t_ssm_group
                t_p1w = P.op("vector", lambda v, hb=hb, w=w: v.tensor_tensor(
                    sp1[:, CH * hb:CH * (hb + 1)], sub[hb % 4][:], ssg[w][:],
                    ALU.mult), sem="dv",
                    waits=[t_us, t_sge] +
                    [t_e_bufs_reader, t_p1_reader])
                su_tok[hb % 4] = t_p1w
                sub_silu_tok[hb % 4] = t_us
                su_silu_tok[w] = t_us
                ssg_tok[w] = t_p1w
                t_p1_last = t_p1w

            for hs in (14, 15):
                t_ssm_group = P.op("tensor", lambda t, hs=hs: t.matmul(
                    pb[6][0:N, 0:512], wssmT[:, N * hs:N * (hs + 1)],
                    sub[hs % 4][:], start=False, stop=(hs == 15)),
                    sem="pe", waits=[sub_silu_tok[hs % 4]])
                sub_tok[hs % 4] = t_ssm_group
            bank_w[6] = t_ssm_group
            t_v = evac("scalar", lambda s: s.activation(
                sVT[:], pb[6][0:N, 0:512], AF.Identity,
                scale=sbv, bias=sbsbv), 6, extra_waits=[t_bsbv])
            if c == 0:
                pump("chain", chain_steps, T + 1)
                t_z = P.op("gpsimd", lambda g: g.memset(btz[:], 0.0),
                           sem="gp", waits=[t_ln_last, t_sq_last])
                t_btz = emit_btz(t_z)
                t_scrbtz = P.op("sync", lambda s: s.dma_start(
                    scr_btz, btz[:]), sem="sd", amt=16, waits=[t_btz])
                pump("wout", wout_steps, 8)
                pool_tok[2] = wout_done[0]
                wout_slot, t_wout, _ = pool_load(WOUT_AP, [wout_done[0]])
            if dbg and c == 0:
                t_dmp = None
                t_dmp = dump(dbg_p1, sp1[:], [t_p1_last, t_dmp])
                t_dmp = dump(dbg_vt, sVT[0:32, 0:512], [t_v, t_dmp])

            t_vr = None
            for j in range(4):
                t_vr = P.op("vector", lambda v, j=j: v.tensor_copy(
                    svr[32 * j:32 * (j + 1), :].rearrange(
                        "p (kc i) -> p kc i", kc=32),
                    bass.AP(sVT, j, [[CH, 32], [4, 32], [T, 4]])),
                    sem="dv" if j == 3 else None,
                    waits=[t_v] + ([t_svr_reader] if j == 0 else []))
            t_umm = mm_group(
                [(pb[5][0:NT, 0:N], svr[:, 4 * kc:4 * (kc + 1)],
                  mr[:, N * kc:N * (kc + 1)]) for kc in range(32)],
                5, extra_waits=[t_vr])

            t_ue = evac("vector", lambda v: v.tensor_copy(
                sU[:], pb[5][0:NT, 0:N]), 5)
            tw = P.op("tensor", lambda t: t.transpose(
                pb[5][0:N, 256:256 + NT], sU[:], sidf[0:NT, 0:NT]),
                sem="pe", waits=[t_ue, bank_r[5]])
            bank_w[5] = tw
            t_ut = evac("vector", lambda v: v.tensor_copy(
                sUT[:, 0:NT], pb[5][0:N, 256:256 + NT]), 5)
            P.wait("vector", t_ut)
            for i in range(NT):
                g = c * NT + i
                tw = P.op("tensor", lambda t, g=g: t.matmul(
                    pb[5][0:N, 300:301], sPm[:], sHT[:, g:g + 1],
                    start=True, stop=True), sem="pe",
                    waits=[t_ut, t_ht0, bank_r[5]])
                bank_w[5] = tw
                evac("vector", lambda v, g=g, i=i: v.tensor_tensor(
                    sHT[:, g + 1:g + 2], pb[5][0:N, 300:301],
                    sUT[:, i:i + 1], ALU.add), 5)
            t_hcols = bank_r[5]
            t_lg = None
            P.waits("tensor", [t_ex_prev, bank_r[7]])
            for t in range(NT):
                t_lg = mm_group(
                    [(pb[7][0:128, 8 * t:8 * (t + 1)],
                      xnt[:, CH * dblk + T * t:CH * dblk + T * (t + 1)],
                      sWg[:, 8 * dblk:8 * (dblk + 1)]) for dblk in range(8)],
                    7, extra_waits=[t_wg, t_xnt_all])
            pb7v = pb[7][0:128, 0:32].rearrange("p (t e) -> p t e", t=NT)
            tr = P.op("vector", lambda v: v.tensor_reduce(
                sregs[:, 0:4], pb7v, AX.X, ALU.max, negate=True),
                sem="dv", waits=[t_lg])
            neg_b = sregs[:, 0:4].broadcast_to((128, NT, 8))
            tr = P.op("vector", lambda v: v.tensor_tensor(
                srt[0][:].rearrange("p (t e) -> p t e", t=NT), pb7v,
                neg_b, ALU.add), sem="dv", waits=[tr])
            t_ex = P.op("scalar", lambda s: s.activation(
                srt[1][:], srt[0][:], AF.Exp), sem="ac", waits=[tr])
            t_ex_prev = t_ex
            bank_r[7] = t_ex
            tr = P.op("vector", lambda v: v.tensor_reduce(
                sregs[:, 0:4], srt[1][:].rearrange("p (t e) -> p t e", t=NT),
                AX.X, ALU.add), sem="dv", waits=[t_ex])
            tr = P.op("vector", lambda v: v.reciprocal(
                sregs[:, 0:4], sregs[:, 0:4]), sem="dv", waits=[tr])
            rec_b = sregs[:, 0:4].broadcast_to((128, NT, 8))
            tr = P.op("vector", lambda v: v.tensor_tensor(
                srt[1][:].rearrange("p (t e) -> p t e", t=NT),
                srt[1][:].rearrange("p (t e) -> p t e", t=NT),
                rec_b, ALU.mult), sem="dv", waits=[tr])
            tr = P.op("vector", lambda v: v.tensor_reduce(
                sregs[:, 0:4], srt[1][:].rearrange("p (t e) -> p t e", t=NT),
                AX.X, ALU.max), sem="dv", waits=[tr])
            m1_b = sregs[:, 0:4].broadcast_to((128, NT, 8))
            tr = P.op("vector", lambda v: v.tensor_tensor(
                srt[2][:].rearrange("p (t e) -> p t e", t=NT),
                srt[1][:].rearrange("p (t e) -> p t e", t=NT),
                m1_b, ALU.is_equal), sem="dv", waits=[tr])
            tsc = None
            for t in range(NT):
                tsc = P.op("vector", lambda v, t=t: v.tensor_tensor_scan(
                    srt[3][:, 8 * t:8 * (t + 1)], sones8[:],
                    srt[2][:, 8 * t:8 * (t + 1)], 0.0, ALU.mult, ALU.add),
                    sem="dv" if t == NT - 1 else None,
                    waits=[tr, t_ones8])
            tr = P.op("vector", lambda v: v.tensor_scalar(
                srt[3][:], srt[3][:], 1.0, None, ALU.is_equal),
                sem="dv", waits=[tsc])
            tr = P.op("vector", lambda v: v.tensor_tensor(
                srt[2][:], srt[2][:], srt[3][:], ALU.mult),
                sem="dv", waits=[tr])
            tsc1_tok = tr
            tr = P.op("vector", lambda v: v.scalar_tensor_tensor(
                srt[4][:], srt[2][:], -1e30, srt[1][:], ALU.mult, ALU.add),
                sem="dv", waits=[tr])
            tr = P.op("vector", lambda v: v.tensor_reduce(
                sregs2[:, 0:4], srt[4][:].rearrange("p (t e) -> p t e", t=NT),
                AX.X, ALU.max), sem="dv", waits=[tr])
            m2_b = sregs2[:, 0:4].broadcast_to((128, NT, 8))
            tr = P.op("vector", lambda v: v.tensor_tensor(
                srt[5][:].rearrange("p (t e) -> p t e", t=NT),
                srt[4][:].rearrange("p (t e) -> p t e", t=NT),
                m2_b, ALU.is_equal), sem="dv", waits=[tr])
            tsc = None
            for t in range(NT):
                tsc = P.op("vector", lambda v, t=t: v.tensor_tensor_scan(
                    srt[3][:, 8 * t:8 * (t + 1)], sones8[:],
                    srt[5][:, 8 * t:8 * (t + 1)], 0.0, ALU.mult, ALU.add),
                    sem="dv" if t == NT - 1 else None, waits=[tr])
            tr = P.op("vector", lambda v: v.tensor_scalar(
                srt[3][:], srt[3][:], 1.0, None, ALU.is_equal),
                sem="dv", waits=[tsc])
            t_mask2 = P.op("vector", lambda v: v.tensor_tensor(
                srt[5][:], srt[5][:], srt[3][:], ALU.mult),
                sem="dv", waits=[tr])
            t_mask1 = tsc1_tok
            tden = P.op("vector", lambda v: v.tensor_tensor(
                sregs2[:, 4:8], sregs[:, 0:4], sregs2[:, 0:4], ALU.add),
                sem="dv", waits=[t_mask2])
            tden = P.op("vector", lambda v: v.reciprocal(
                sregs2[:, 4:8], sregs2[:, 4:8]), sem="dv", waits=[tden])
            tw1 = P.op("vector", lambda v: v.tensor_tensor(
                swab[:, 0:4], sregs[:, 0:4], sregs2[:, 4:8], ALU.mult),
                sem="dv", waits=[tden])
            t_wab = P.op("vector", lambda v: v.tensor_tensor(
                swab[:, 4:8], sregs2[:, 0:4], sregs2[:, 4:8], ALU.mult),
                sem="dv", waits=[tw1])
            bank_r[7] = t_wab

            t_on = P.op("vector", lambda v: v.memset(
                su[1][0:8, 0:512], 1.0), sem="dv", waits=[su_silu_tok[1]])
            t_s12 = [None, None]
            for r in range(2):
                mk = srt[2] if r == 0 else srt[5]
                t_mk = t_mask1 if r == 0 else t_mask2
                bkx = 6 + r
                for t in range(NT):
                    tw = P.op("tensor", lambda tn, t=t, mk=mk, bkx=bkx:
                              tn.transpose(
                        pb[bkx][0:8, 128 * t:128 * (t + 1)],
                        mk[:, 8 * t:8 * (t + 1)], sidf[:]),
                        sem="pe" if t == NT - 1 else None,
                        waits=([t_mk, bank_r[bkx], t_smaskT_war if r == 0
                                else None] if t == 0 else []))
                bank_w[bkx] = tw
                t_mT = evac("vector", lambda v, bkx=bkx: v.tensor_copy(
                    smaskT, pb[bkx][0:8, 0:512]), bkx)
                tk = P.op("vector", lambda v: v.tensor_tensor_scan(
                    scum, su[1][0:8, 0:512], smaskT,
                    0.0, ALU.mult, ALU.add), sem="dv",
                    waits=[t_on, t_mT])
                if r == 0:
                    tk = P.op("vector", lambda v: v.tensor_tensor(
                        sb8[0:8, 1:2], scum[:, 511:512], sb8[0:8, 0:1],
                        ALU.add), sem="dv", waits=[tk, t_b8])
                tk = P.op("vector", lambda v, r=r: v.tensor_scalar(
                    scum, scum, sb8[0:8, r:r + 1], None,
                    ALU.add), sem="dv", waits=[tk])
                tk = P.op("vector", lambda v: v.tensor_scalar(
                    smaskT, scum, sb8[0:8, 2:3], None,
                    ALU.is_gt), sem="dv", waits=[tk])
                tk = P.op("vector", lambda v: v.scalar_tensor_tensor(
                    scum, smaskT, float(1 << 20), scum,
                    ALU.mult, ALU.add), sem="dv", waits=[tk])
                bky = 4 + r
                for t in range(NT):
                    tw = P.op("tensor", lambda tn, t=t, bky=bky:
                              tn.transpose(
                        pb[bky][0:128, 8 * t:8 * (t + 1)],
                        scum[:, 128 * t:128 * (t + 1)], sidf[0:8, 0:8]),
                        sem="pe" if t == NT - 1 else None,
                        waits=([tk, bank_r[bky]] if t == 0 else []))
                bank_w[bky] = tw
                t_s12[r] = evac("vector", lambda v, r=r, bky=bky: v.tensor_copy(
                    sslotm[:, 32 * r:32 * (r + 1)], pb[bky][0:128, 0:32]),
                    bky)
            su_scrib["t"] = tk
            P.op("vector", lambda v: v.tensor_scalar(
                sidxf[:, 8:12], siotl[:], 0.0, None, ALU.add),
                waits=[t_iotl])
            tk3 = P.op("vector", lambda v: v.tensor_scalar(
                sidxf[:, 12:16], siotl[:], 512.0, None, ALU.add), sem="dv")
            t_iog = P.op("gpsimd", lambda g, c=c: g.iota(
                siotg[:], [[128, 4]], base=(c % 2) * CH,
                channel_multiplier=1, allow_small_or_imprecise_dtypes=True),
                sem="gp", waits=[t_iog_reader])
            t_init1 = P.op("gpsimd", lambda g: g.dma_start(
                bass.AP(scr_idx.tensor, 0, [[16, 128], [1, 16]]), shug[:]),
                sem="gx", amt=16, waits=[t_hug, t_tab_reader])
            t_init2 = P.op("gpsimd", lambda g: g.dma_start(
                bass.AP(scr_idx2.tensor, 0, [[16, 128], [1, 16]]), shug[:]),
                sem="gx", amt=16)
            tk2 = None
            t_idxu = None
            for r in range(2):
                mk = srt[2] if r == 0 else srt[5]
                tmu = P.op("vector", lambda v, mk=mk, r=r: v.tensor_tensor(
                    srt[0][:], mk[:],
                    sslotm[:, 32 * r:32 * (r + 1)], ALU.mult), sem="dv",
                    waits=([t_s12[0], t_s12[1]] if r == 0 else [tk2]))
                tk2 = P.op("vector", lambda v, r=r: v.tensor_reduce(
                    sidxf[:, 4 * r:4 * (r + 1)],
                    srt[0][:].rearrange("p (t e) -> p t e", t=NT),
                    AX.X, ALU.add), sem="dv", waits=[tmu])
                t_idxu = P.op("vector", lambda v, r=r: v.tensor_copy(
                    sidxu[:, 4 * r:4 * (r + 1)],
                    sidxf[:, 4 * r:4 * (r + 1)]), sem="dv",
                    waits=[tk2, tk3])
                for t in range(NT):
                    t_i1 = P.op("gpsimd", lambda g, r=r, t=t:
                                g.indirect_dma_start(
                        bass.AP(scr_idx.tensor, 0, [[1, 2048], [1, 1]]),
                        bass.IndirectOffsetOnAxis(
                            ap=sidxu[:, 4 * r + t:4 * r + t + 1], axis=0),
                        siotg[:, t:t + 1], None,
                        bounds_check=regs["b_idx"], oob_is_err=False),
                        sem="gx", amt=16,
                        waits=[t_idxu, t_iog, t_init1, t_init2])
                    t_i2 = P.op("gpsimd", lambda g, r=r, t=t:
                                g.indirect_dma_start(
                        bass.AP(scr_idx2.tensor, 0, [[1, 2048], [1, 1]]),
                        bass.IndirectOffsetOnAxis(
                            ap=sidxu[:, 4 * r + t:4 * r + t + 1], axis=0),
                        sidxf[:, 8 + 4 * r + t:9 + 4 * r + t], None,
                        bounds_check=regs["b_idx"], oob_is_err=False),
                        sem="gx", amt=16)
            stab16 = ssbc
            t_tl = P.op("sync", lambda s: s.dma_start(
                stab16[0:16, 0:128],
                bass.AP(scr_idx.tensor, 0, [[128, 16], [1, 128]])),
                sem="gt", amt=16,
                waits=[t_i1, t_i2, t_xgt_cur, t_yscat_all,
                       yp_tok[0], yp_tok[1], t_stab_reader])
            t_tl = P.op("sync", lambda s: s.dma_start(
                stab16[0:16, 128:256],
                bass.AP(scr_idx2.tensor, 0, [[128, 16], [1, 128]])),
                sem="gt", amt=16)
            for h2 in range(2):
                tw = P.op("tensor", lambda tn, h2=h2: tn.transpose(
                    pb[6 + h2][0:128, 448:464],
                    stab16[0:16, 128 * h2:128 * (h2 + 1)],
                    sidf[0:16, 0:16]), sem="pe",
                    waits=[t_tl, bank_r[6 + h2]])
                bank_w[6 + h2] = tw
                evac("vector", lambda v, h2=h2: v.tensor_copy(
                    stabT[:, 16 * h2:16 * (h2 + 1)],
                    pb[6 + h2][0:128, 448:464]), 6 + h2)
            t_tab = P.op("vector", lambda v: v.tensor_copy(
                stabTi[:], stabT[:]), sem="dv",
                waits=[bank_r[6], bank_r[7], last_ga[0], last_ga[1],
                       yp_tok[0], yp_tok[1]])
            t_stab_reader = tw
            t_tab_reader = t_tl
            t_iog_reader = t_i1
            stabTu = stabTi
            if dbg and c == 0:
                t_dmp = dump(dbg_idx, sidxf[:], [t_idxu, t_dmp])
                t_dmp = dump(dbg_wab, swab[:], [t_wab, t_dmp])
                t_dmp = dump(dbg_m12a, srt[2][:], [t_mask1, t_dmp])
                t_dmp = dump(dbg_m12b, srt[5][:], [t_mask2, t_dmp])
                t_dmp = dump(dbg_slot, sslotm[:, 0:64],
                             [t_s12[0], t_s12[1], t_dmp])
                t_dmp = dump(dbg_tab, stabT[:], [t_tab, t_dmp])
            P.waits("tensor", [t_vr, bank_r[4]])
            for kc in range(32):
                P.op("tensor", lambda t, kc=kc: t.matmul(
                    pb[4][0:NT, 0:T], svr[:, 4 * kc:4 * (kc + 1)],
                    btz[:, T * kc:T * (kc + 1)], start=(kc == 0),
                    stop=False), waits=[t_btz] if kc == 0 else ())
            t_smm = P.op("tensor", lambda t, c=c: t.matmul(
                pb[4][0:NT, 0:T], sHT[:, c * NT:c * NT + NT],
                cmatT[:, 1:1 + T], start=False, stop=True), sem="pe",
                waits=[t_hcols])
            bank_w[4] = t_smm
            t_svr_reader = t_smm
            t_se = evac("vector", lambda v: v.tensor_copy(
                sS[:], pb[4][0:NT, 0:T]), 4)
            t_srow = P.op("sync", lambda s: s.dma_start(
                s_row[0:1, :], sS[:]), sem="ar", amt=16, waits=[t_se])
            mm_group([(pb[6][0:128, 0:512], sones[0:1, :], s_row[0:1, :])],
                     6, extra_waits=[t_srow, t_ones])
            t_sbc = evac("vector", lambda v: v.tensor_copy(
                ssbc[:], pb[6][0:128, 0:512]), 6)

            if dbg and c == 0:
                t_dmp = dump(dbg_sbc, ssbc[:], [t_sbc, t_dmp])
                t_dmp = dump(dbg_ht, sHT[0:32, 0:33], [t_hcols, t_dmp])
            if c < n_chunks - 1:
                t_x_next = emit_xload(
                    c + 1, [t_svr_reader, t_scrbtz if c == 0 else None])
            t_mm2_last = None
            t_out_stt = [None] * NT

            def emit_xgt(e):
                nonlocal_toks = []
                for ti in range(2):
                    g = 2 * e + ti
                    xgs = sxg[:, 2048 + 1024 * ti:2048 + 1024 * (ti + 1)]
                    t_ga = P.op("gpsimd", lambda gq, g=g, xgs=xgs:
                                gq.indirect_dma_start(
                        xgs, None, scr_xn,
                        bass.IndirectOffsetOnAxis(
                            ap=stabTu[:, g:g + 1], axis=0),
                        bounds_check=regs["b_tok"], oob_is_err=False),
                        sem=f"gi{ti}", amt=16,
                        waits=[t_tab, xg_tok[ti], t_arena1_z,
                               yab_tok[2 * ti], yab_tok[2 * ti + 1],
                               front_tok["t_xnst"]])
                    last_ga[ti] = t_ga
                    for g2 in range(2):
                        bk = 6 + g2
                        for dq in range(4):
                            dblk = 4 * g2 + dq
                            tw_ = P.op("tensor", lambda tn, xgs=xgs,
                                       dblk=dblk, dq=dq, bk=bk:
                                       tn.transpose(
                                pb[bk][0:128, 128 * dq:128 * (dq + 1)],
                                xgs[:, 128 * dblk:128 * (dblk + 1)],
                                sidf[:]),
                                sem="pe" if dq == 3 else None,
                                waits=([t_ga, bank_r[bk]]
                                       if dq == 0 else []))
                        bank_w[bk] = tw_
                        tok_ = evac(
                            "vector", lambda v, e=e, ti=ti, g2=g2, bk=bk:
                            v.tensor_copy(
                                swit[e % 2][:].rearrange(
                                    "p (d cc) -> p d cc", d=8)
                                [:, 4 * g2:4 * (g2 + 1),
                                 128 * ti:128 * (ti + 1)],
                                pb[bk][0:128, 0:512].rearrange(
                                    "p (d cc) -> p d cc", d=4)),
                            bk, extra_waits=[xgt_tok[e % 2],
                                             wit_mm_tok[e % 2]])
                    nonlocal_toks.append(tok_)
                    xg_tok[ti] = tw_
                return nonlocal_toks[-1]

            def emit_mm1s(e, slot_, tok_, t_xgt):
                t_relu_last = None
                t_h = None
                for fb in range(16):
                    bk = 4 + fb % 2
                    t_h = mm_group(
                        [(pb[bk][0:128, 0:256],
                          pool[slot_][:, 2048 * dc + 128 * fb:
                                      2048 * dc + 128 * (fb + 1)],
                          swit[e % 2][:, 256 * dc:256 * (dc + 1)])
                         for dc in range(8)],
                        bk, extra_waits=[tok_, t_xgt])
                    t_relu_last = evac(
                        "scalar", lambda s, fb=fb, bk=bk: s.activation(
                            shts[:, 256 * fb:256 * (fb + 1)],
                            pb[bk][0:128, 0:256], AF.Relu), bk,
                        extra_waits=([t_mm2_prev] if fb == 0 else ()))
                pool_tok[slot_] = t_h
                xgt_tok[e % 2] = t_h
                return t_relu_last

            def emit_mm2s(e, slot_, tok_, t_relu, t_smix_dep):
                nonlocal t_yscat_all
                last = None
                for ti in range(2):
                    bks = (0, 1) if ti == 0 else (2, 3)
                    for h2 in range(2):
                        bk = bks[h2]
                        last = mm_group(
                            [(pb[bk][0:128, 0:512],
                              shts[:, 256 * fc + 128 * ti:
                                   256 * fc + 128 * (ti + 1)],
                              pool[slot_][:, 1024 * fc + 512 * h2:
                                          1024 * fc + 512 * (h2 + 1)])
                             for fc in range(16)],
                            bk, extra_waits=[t_relu, tok_] +
                            ([t_smix_dep] if e == 0 else []))
                        tk_ = evac("vector",
                                   lambda v, ti=ti, h2=h2, bk=bk:
                                   v.tensor_copy(
                            syp[:, 1024 * ti + 512 * h2:
                                1024 * ti + 512 * (h2 + 1)],
                            pb[bk][0:128, 0:512]), bk,
                            extra_waits=[yp_tok[ti]] if h2 == 0 else ())
                    g = 2 * e + ti
                    _ioa = bass.IndirectOffsetOnAxis(
                        ap=stabTu[:, 16 + g:17 + g], axis=0)
                    _vap = syp[:, 1024 * ti:1024 * (ti + 1)]
                    t_sc = P.op("gpsimd", lambda gq, _ioa=_ioa, _vap=_vap:
                                gq.indirect_dma_start(
                        scr_yab, _ioa, _vap, None,
                        bounds_check=regs["b_slot"], oob_is_err=False),
                        sem=f"gs{ti}", amt=16,
                        waits=[tk_, t_tab, yab_fin[0], yab_fin[1]])
                    yp_tok[ti] = t_sc
                    t_yscat_all = t_sc
                pool_tok[slot_] = last
                return last

            def load_w1(e):
                if c == 0:
                    s_, t_, w1_scr_tok[e] = pool_load(
                        w1_ap(e, w1), store_to=w1_ap(e, scr_w1))
                else:
                    s_, t_, _ = pool_load(w1_ap(e, scr_w1),
                                          [w1_scr_tok[e]],
                                          eng="sync")
                return s_, t_

            def load_w2(e):
                if c == 0:
                    s_, t_, w2_scr_tok[e] = pool_load(
                        w2_ap(e, w2), store_to=w2_ap(e, scr_w2))
                else:
                    s_, t_, _ = pool_load(w2_ap(e, scr_w2),
                                          [w2_scr_tok[e]],
                                          eng="sync")
                return s_, t_

            w2_slot, t_w2 = load_w2(0)
            t_xgt_cur = emit_xgt(0)
            t_xgt_next = None
            t_yin = None
            for hb in range(16):
                t_yin = P.op("vector", lambda v, hb=hb:
                             v.scalar_tensor_tensor(
                    sp1[:, CH * hb:CH * (hb + 1)], ssbc[:],
                    sDVT[:, hb:hb + 1], sp1[:, CH * hb:CH * (hb + 1)],
                    ALU.add, ALU.mult), sem="dv" if hb == 15 else None,
                    waits=[t_sbc, t_dvt] if hb == 0 else ())
            t_y = None
            t_smix_done = None
            for t in range(NT):
                bks = (0, 1) if t % 2 == 0 else (2, 3)
                for h2 in range(2):
                    bk = bks[h2]
                    t_y = mm_group(
                        [(pb[bk][0:128, 0:512],
                          sp1[:, CH * hb2 + T * t:CH * hb2 + T * (t + 1)],
                          pool[wout_slot][:, 1024 * hb2 + 512 * h2:
                                          1024 * hb2 + 512 * (h2 + 1)])
                         for hb2 in range(16)],
                        bk, extra_waits=[t_yin, t_wout])
                    evac("vector", lambda v, bk=bk, h2=h2: v.tensor_tensor(
                        stmp[:], pb[bk][0:128, 0:512],
                        sBOUT[:, 512 * h2:512 * (h2 + 1)], ALU.add), bk)
                    t_smix_done = P.op(
                        "vector", lambda v, t=t, h2=h2: v.tensor_tensor(
                            smix[:, 1024 * t + 512 * h2:
                                 1024 * t + 512 * (h2 + 1)],
                            stmp[:],
                            sxn[:, 1024 * t + 512 * h2:
                                1024 * t + 512 * (h2 + 1)], ALU.add),
                        sem="dv")
            if dbg and c == 0:
                t_dmp = dump(dbg_yin, sp1[:], [t_yin, t_dmp])
                t_dmp = dump(dbg_smix, smix[:], [t_smix_done, t_dmp])
            pool_tok[wout_slot] = t_y
            t_p1_reader = t_y if not (dbg and c == 0) else t_dmp

            cur_xgt = t_xgt_cur
            t_relu = emit_mm1s(0, w1_slot, t_w1, cur_xgt)
            nw1_slot, nt_w1 = load_w1(1)
            t_xgt_next = emit_xgt(1)
            for e in range(E):
                if e > 0:
                    t_relu = emit_mm1s(e, w1_slot, t_w1, cur_xgt)
                w2_next = None
                if e < E - 1:
                    w1_slot, t_w1 = nw1_slot, nt_w1
                    cur_xgt = t_xgt_next
                    w2_next = load_w2(e + 1)
                    if e < E - 2:
                        t_xgt_next = emit_xgt(e + 2)
                t_mm2_last = emit_mm2s(e, w2_slot, t_w2, t_relu,
                                       t_smix_done)
                if e < E - 1:
                    w2_slot, t_w2 = w2_next
                    if e < E - 2:
                        nw1_slot, nt_w1 = load_w1(e + 2)
            t_mm2_prev = t_mm2_last
            t_sht_reader = t_mm2_last
            scrxn_reader[c % 2] = (last_ga[0], last_ga[1])
            if c < n_chunks - 1:
                emit_front_stats(c + 1,
                                 [t_smix_done, su_scrib["t"], su_tok[2],
                                  su_tok[3], sub_silu_tok[2],
                                  sub_silu_tok[3]],
                                 t_x_next)
            if dbg and c == 0:
                t_dmp = dump(dbg_yab, scr_yab[:],
                             [t_yscat_all, yp_tok[0], yp_tok[1], t_dmp])
            if c < n_chunks - 1:
                for _t in (0, 2, 1, 3):
                    emit_front_tile(c + 1, _t, first=(_t == 0))
            t_store = None
            t_fin = None
            t_g = None
            t_fin_tiles = [None] * NT
            sxe_tok = [None, None]
            soutb_tok = [None, None]
            t_su_free = [su_silu_tok[0], su_silu_tok[1], t_smix_done,
                         su_scrib["t"], su_tok[2], su_tok[3],
                         sub_silu_tok[2], sub_silu_tok[3]]
            yab = ar1b
            first_e = True
            for t in (0, 2, 1, 3):
                w = t % 2
                r0 = c * CH + t * T
                ya_s = yab[:, 4096 + 2048 * w:4096 + 2048 * w + 1024]
                yb_s = yab[:, 4096 + 2048 * w + 1024:4096 + 2048 * (w + 1)]
                t_ya = P.op("scalar", lambda s, t=t, ya_s=ya_s: s.dma_start(
                    ya_s, scr_yab[128 * t:128 * (t + 1), :]),
                    sem=f"gy{w}", amt=16,
                    waits=[t_yscat_all, yp_tok[0], yp_tok[1],
                           yab_tok[2 * w], xg_tok[0], xg_tok[1]])
                t_yb = P.op("scalar", lambda s, t=t, yb_s=yb_s: s.dma_start(
                    yb_s, scr_yab[512 + 128 * t:512 + 128 * (t + 1), :]),
                    sem=f"gy{w}", amt=16, waits=[yab_tok[2 * w + 1]])
                t_xe = P.op("scalar", lambda s, r0=r0, w=w: s.dma_start(
                    sxe[:, 1024 * w:1024 * (w + 1)], x_d[r0:r0 + T, :]),
                    sem=f"gy{w}", amt=16,
                    waits=[sxe_tok[w], t_p1_reader])
                t_mul = [None, None]
                tko = None
                for h in range(2):
                    P.op("vector", lambda v, t=t, h=h, ya_s=ya_s:
                         v.tensor_scalar(
                        su[h][:], ya_s[:, 512 * h:512 * (h + 1)],
                        swab[:, t:t + 1], None, ALU.mult),
                        waits=[t_ya, t_yb, t_xe, t_wab] +
                        (t_su_free if first_e else []))
                    tko = P.op("vector", lambda v, t=t, h=h, yb_s=yb_s:
                               v.scalar_tensor_tensor(
                        su[h][:], yb_s[:, 512 * h:512 * (h + 1)],
                        swab[:, 4 + t:5 + t], su[h][:],
                        ALU.mult, ALU.add), sem="dv")
                first_e = False
                yab_tok[2 * w] = tko
                yab_tok[2 * w + 1] = tko
                t_out_stt[t] = tko
                for h in range(2):
                    t_g = P.op("scalar", lambda s, h=h: s.activation(
                        su[h][:], su[h][:], AF.Silu),
                        sem="ac", waits=[tko, t_sout_reader
                                         if (dbg and c == 0) else None])
                    su_silu_tok[h] = t_g
                    t_mul[h] = P.op("vector", lambda v, t=t, w=w, h=h:
                                    v.tensor_tensor(
                        soutb[:, 2048 + 1024 * w + 512 * h:
                              2048 + 1024 * w + 512 * (h + 1)],
                        smix[:, 1024 * t + 512 * h:
                             1024 * t + 512 * (h + 1)],
                        su[h][:], ALU.mult), sem="dv",
                        waits=[t_g, soutb_tok[0] if h == 0 else None,
                               soutb_tok[1] if h == 0 else None,
                               t_smix_done])
                t_fin = P.op("vector", lambda v, w=w: v.tensor_tensor(
                    soutb[:, 2048 + 1024 * w:2048 + 1024 * (w + 1)],
                    soutb[:, 2048 + 1024 * w:2048 + 1024 * (w + 1)],
                    sxe[:, 1024 * w:1024 * (w + 1)], ALU.add),
                    sem="dv", waits=[t_xe, t_mul[0], t_mul[1]])
                sxe_tok[w] = t_fin
                yab_fin[w] = t_xe
                t_fin_tiles[t] = t_fin
                t_store = P.op("scalar", lambda s, r0=r0, w=w: s.dma_start(
                    out_d[r0:r0 + T, :],
                    soutb[:, 2048 + 1024 * w:2048 + 1024 * (w + 1)]),
                    sem="go", amt=16, waits=[t_fin])
                soutb_tok[w] = t_store
            t_sout_reader = t_fin
            t_e_bufs_reader = t_store

            swit_tok[0] = xgt_tok[0]
            swit_tok[1] = xgt_tok[1]
            front_tok["xnt_reader"] = t_xnt_reader

        P.streams["sync"].append(
            lambda s: s.wait_ge(sems["sd"], P.cnt["sd"]))
        for _sn in ("gt", "gy0", "gy1", "ap0", "ap1", "ap2",
                    "aw0", "aw1", "ar"):
            if P.cnt[_sn]:
                P.streams["sync"].append(
                    lambda s, _sn=_sn: s.wait_ge(sems[_sn], P.cnt[_sn]))
        P.streams["gpsimd"].append(
            lambda g: g.wait_ge(sems["gd"], P.cnt["gd"]))
        for _sn in ("gi0", "gi1", "gs0", "gs1", "gx", "gn0", "gn1", "gz"):
            if P.cnt[_sn]:
                P.streams["gpsimd"].append(
                    lambda g, _sn=_sn: g.wait_ge(sems[_sn], P.cnt[_sn]))
        P.streams["scalar"].append(
            lambda s: s.wait_ge(sems["ad"], P.cnt["ad"]))
        P.streams["scalar"].append(
            lambda s: s.wait_ge(sems["go"], P.cnt["go"]))
        P.replay(block)
    return nc


_NC_CACHE = {}


def _get_nc():
    if "nc" not in _NC_CACHE:
        _NC_CACHE["nc"] = build_nc()
    return _NC_CACHE["nc"]


def kernel(L0_attn_x, L0_attn_W_in, L0_attn_b_in, L0_attn_conv_w,
           L0_attn_conv_b, L0_attn_W_ssm, L0_attn_b_ssm, L0_attn_A,
           L0_attn_Bv, L0_attn_Cv, L0_attn_Dv, L0_attn_W_out, L0_attn_b_out,
           L0_ffn_Wg, L0_ffn_W1, L0_ffn_W2):
    nc = _get_nc()
    wmap = {
        "w_in": np.ascontiguousarray(L0_attn_W_in, dtype=np.float32),
        "b_in": np.ascontiguousarray(L0_attn_b_in, dtype=np.float32),
        "conv_w": np.ascontiguousarray(L0_attn_conv_w, dtype=np.float32),
        "conv_b": np.ascontiguousarray(L0_attn_conv_b, dtype=np.float32),
        "w_ssm": np.ascontiguousarray(L0_attn_W_ssm, dtype=np.float32),
        "b_ssm": np.ascontiguousarray(L0_attn_b_ssm, dtype=np.float32),
        "A": np.ascontiguousarray(L0_attn_A, dtype=np.float32),
        "Bv": np.ascontiguousarray(L0_attn_Bv, dtype=np.float32),
        "Cv": np.ascontiguousarray(L0_attn_Cv, dtype=np.float32),
        "Dv": np.ascontiguousarray(L0_attn_Dv, dtype=np.float32),
        "w_out": np.ascontiguousarray(L0_attn_W_out, dtype=np.float32),
        "b_out": np.ascontiguousarray(L0_attn_b_out, dtype=np.float32),
        "wg": np.ascontiguousarray(L0_ffn_Wg, dtype=np.float32),
        "w1": np.ascontiguousarray(L0_ffn_W1, dtype=np.float32),
        "w2": np.ascontiguousarray(L0_ffn_W2, dtype=np.float32),
    }
    xs = np.ascontiguousarray(L0_attn_x, dtype=np.float32)
    in_maps = [dict(wmap, x=xs[i]) for i in range(8)]
    res = run_bass_kernel_spmd(nc, in_maps, core_ids=list(range(8)))
    return np.stack([r["out"] for r in res.results], axis=0)


# version 28
# speedup vs baseline: 1.1865x; 1.0717x over previous
import contextlib
import numpy as np

import concourse.bass as bass
import concourse.mybir as mybir
from concourse.alu_op_type import AluOpType as ALU
from concourse.bass_utils import run_bass_kernel_spmd

F32 = mybir.dt.float32
BF16 = mybir.dt.bfloat16
I32 = mybir.dt.int32
AF = mybir.ActivationFunctionType
AX = mybir.AxisListType

S = 4096
D = 1024
H = 2048
Z = 2 * H
N = 32
E = 8
FF = 2048
CH = 512
T = 128
NT = CH // T
NCH = S // CH


class Sched:

    ENGS = ("sync", "gpsimd", "tensor", "vector", "scalar")

    def __init__(self, nc, sems):
        self.nc = nc
        self.sems = sems
        self.cnt = {k: 0 for k in sems}
        self.streams = {e: [] for e in self.ENGS}
        self.last_wait = {e: {} for e in self.ENGS}

    def wait(self, eng, tok):
        if tok is None:
            return
        sem, val = tok
        if self.last_wait[eng].get(sem, 0) >= val:
            return
        self.last_wait[eng][sem] = val
        h = self.sems[sem]
        self.streams[eng].append(lambda e, h=h, val=val: e.wait_ge(h, val))

    def waits(self, eng, toks):
        best = {}
        for t in toks:
            if t is None:
                continue
            sem, val = t
            if val > best.get(sem, 0):
                best[sem] = val
        for sem, val in best.items():
            self.wait(eng, (sem, val))

    def op(self, eng, fn, sem=None, amt=1, waits=()):
        self.waits(eng, waits)
        if sem is None:
            self.streams[eng].append(lambda e, fn=fn: fn(e))
            return None
        self.cnt[sem] += amt
        tok = (sem, self.cnt[sem])
        h = self.sems[sem]

        def wrapped(e, fn=fn, h=h, amt=amt):
            fn(e).then_inc(h, amt)
        self.streams[eng].append(wrapped)
        return tok

    def replay(self, block):
        for eng in self.ENGS:
            stream = self.streams[eng]

            def make(stream):
                def body(h):
                    for fn in stream:
                        fn(h)
                return body
            getattr(block, eng)(make(stream))


def build_nc(n_chunks=NCH, dbg=False):
    nc = bass.Bass("TRN2", target_bir_lowering=False)
    S_ = n_chunks * CH

    x_d = nc.dram_tensor("x", [S_, D], F32, kind="ExternalInput").ap()
    w_in = nc.dram_tensor("w_in", [Z, D], F32, kind="ExternalInput").ap()
    b_in = nc.dram_tensor("b_in", [Z], F32, kind="ExternalInput").ap()
    conv_w = nc.dram_tensor("conv_w", [H, 1, 4], F32, kind="ExternalInput").ap()
    conv_b = nc.dram_tensor("conv_b", [H], F32, kind="ExternalInput").ap()
    w_ssm = nc.dram_tensor("w_ssm", [N, H], F32, kind="ExternalInput").ap()
    b_ssm = nc.dram_tensor("b_ssm", [N], F32, kind="ExternalInput").ap()
    A_d = nc.dram_tensor("A", [N, N], F32, kind="ExternalInput").ap()
    Bv_d = nc.dram_tensor("Bv", [N], F32, kind="ExternalInput").ap()
    Cv_d = nc.dram_tensor("Cv", [N], F32, kind="ExternalInput").ap()
    Dv_d = nc.dram_tensor("Dv", [H], F32, kind="ExternalInput").ap()
    w_out = nc.dram_tensor("w_out", [D, H], F32, kind="ExternalInput").ap()
    b_out = nc.dram_tensor("b_out", [D], F32, kind="ExternalInput").ap()
    wg = nc.dram_tensor("wg", [D, E], F32, kind="ExternalInput").ap()
    w1 = nc.dram_tensor("w1", [E, D, FF], F32, kind="ExternalInput").ap()
    w2 = nc.dram_tensor("w2", [E, FF, D], F32, kind="ExternalInput").ap()
    out_d = nc.dram_tensor("out", [S_, D], F32, kind="ExternalOutput").ap()

    if dbg:
        dbg_xn = nc.dram_tensor("dbg_xn", [128, 4096], F32, kind="ExternalOutput").ap()
        dbg_xnt = nc.dram_tensor("dbg_xnt", [128, 4096], F32, kind="ExternalOutput").ap()
        dbg_p1 = nc.dram_tensor("dbg_p1", [128, 8192], F32, kind="ExternalOutput").ap()
        dbg_vt = nc.dram_tensor("dbg_vt", [32, 512], F32, kind="ExternalOutput").ap()
        dbg_sbc = nc.dram_tensor("dbg_sbc", [128, 512], F32, kind="ExternalOutput").ap()
        dbg_ht = nc.dram_tensor("dbg_ht", [32, 33], F32, kind="ExternalOutput").ap()
        dbg_smix = nc.dram_tensor("dbg_smix", [128, 4096], F32, kind="ExternalOutput").ap()
        dbg_cw = nc.dram_tensor("dbg_cw", [128, 32], F32, kind="ExternalOutput").ap()
        dbg_out = nc.dram_tensor("dbg_out", [128, 4096], F32, kind="ExternalOutput").ap()
        dbg_yin = nc.dram_tensor("dbg_yin", [128, 8192], F32, kind="ExternalOutput").ap()
        dbg_btz = nc.dram_tensor("dbg_btz", [128, 4096], F32, kind="ExternalOutput").ap()
        dbg_mr = nc.dram_tensor("dbg_mr", [128, 1024], F32, kind="ExternalOutput").ap()
        dbg_cmat = nc.dram_tensor("dbg_cmat", [32, 130], F32, kind="ExternalOutput").ap()
        dbg_sx = nc.dram_tensor("dbg_sx", [128, 4096], F32, kind="ExternalOutput").ap()
        dbg_regs = nc.dram_tensor("dbg_regs", [128, 8], F32, kind="ExternalOutput").ap()
        dbg_idx = nc.dram_tensor("dbg_idx", [128, 16], F32, kind="ExternalOutput").ap()
        dbg_wab = nc.dram_tensor("dbg_wab", [128, 8], F32, kind="ExternalOutput").ap()
        dbg_m12a = nc.dram_tensor("dbg_m12a", [128, 32], F32, kind="ExternalOutput").ap()
        dbg_m12b = nc.dram_tensor("dbg_m12b", [128, 32], F32, kind="ExternalOutput").ap()
        dbg_slot = nc.dram_tensor("dbg_slot", [128, 64], F32, kind="ExternalOutput").ap()
        dbg_tab = nc.dram_tensor("dbg_tab", [128, 32], F32, kind="ExternalOutput").ap()
        dbg_yab = nc.dram_tensor("dbg_yab", [1024, 1024], F32, kind="ExternalOutput").ap()
    scr_wit = nc.dram_tensor("scr_wit", [16, 128, 2048], BF16,
                             kind="Internal").ap()
    scr_wout = nc.dram_tensor("scr_wout", [16, 128, D], BF16,
                              kind="Internal").ap()
    scr_w1 = nc.dram_tensor("scr_w1", [E, D, FF], BF16, kind="Internal").ap()
    scr_w2 = nc.dram_tensor("scr_w2", [E, FF, D], BF16, kind="Internal").ap()
    scr_btz = nc.dram_tensor("scr_btz", [128, 4096], BF16, kind="Internal").ap()
    scr_idx = nc.dram_tensor("scr_idx", [2048], mybir.dt.float32,
                             kind="Internal").ap()
    scr_idx2 = nc.dram_tensor("scr_idx2", [2048], mybir.dt.float32,
                              kind="Internal").ap()
    scr_yab = nc.dram_tensor("scr_yab", [1024, D], BF16, kind="Internal").ap()
    scr_xn = nc.dram_tensor("scr_xn", [2 * CH, D], BF16, kind="Internal").ap()

    with contextlib.ExitStack() as st:
        def sb(name, shape, dt):
            return st.enter_context(nc.sbuf_tensor(name, shape, dt))

        def ps(name, shape, dt):
            return st.enter_context(nc.psum_tensor(name, shape, dt))

        pool = [sb(f"pool{i}", [128, 16384], BF16) for i in range(3)]
        btz = sb("btz", [128, 32 * T], BF16)
        mr = sb("mr", [128, 32 * N], BF16)
        wssmT = sb("wssmT", [128, 16 * N], BF16)
        sBOUT = sb("sBOUT", [128, D], BF16)
        xnt = sb("xnt", [128, 8 * CH], BF16)
        smix = sb("smix", [128, NT * D], BF16)
        swit = [sb(f"swit{i}", [128, 2048], BF16) for i in range(2)]
        ssbc = sb("ssbc", [128, CH], F32)
        ar1 = sb("ar1", [128, 4096], F32)
        ar2 = sb("ar2", [128, 4096], F32)
        ar3 = sb("ar3", [128, 4096], F32)
        U32 = mybir.dt.uint32
        ar1b = ar1.bitcast(BF16)
        shts = ar1b
        sxg = ar1
        syr = ar1b
        sxn = ar2
        sout = ar2
        sp1 = ar3.bitcast(BF16)
        sxe = ar3
        soutb = ar3
        syp = ar3.bitcast(BF16)
        spad = [sb(f"spad{i}", [128, CH + 8], F32) for i in range(2)]
        ssg = [sb(f"ssg{i}", [128, CH], BF16) for i in range(2)]
        su = [sb(f"su{i}", [128, CH], F32) for i in range(2)]
        sub = [sb(f"sub{i}", [128, CH], BF16) for i in range(4)]
        stmp = su[0]
        sidf = sb("sidf", [128, 128], F32)
        sidb = sb("sidb", [128, 128], BF16)
        cmatT = sb("cmatT", [N, T + 2], F32)
        cb = [sb(f"cb{i}", [N, N + 1], F32) for i in range(2)]
        sPm = sb("sPm", [N, N], F32)
        sA = sb("sA", [N, N], F32)
        sHT = sb("sHT", [N, NCH * NT + 1], F32)
        sVT = sb("sVT", [N, CH], F32)
        svr = sb("svr", [128, T], BF16)
        sS = sb("sS", [NT, T], F32)
        s_row = sb("s_row", [1, CH], F32)
        sU = sb("sU", [NT, N], F32)
        s32 = sb("s32", [N, 16], F32)
        sUT = s32[:, 8:8 + NT]
        sBINT = sb("sBINT", [128, 32], F32)
        sCBT = sb("sCBT", [128, 16], F32)
        sDVT = sb("sDVT", [128, 16], F32)
        sCWS = sb("sCWS", [128, 64], F32)
        sWg = sb("sWg", [128, 64], BF16)
        shalo = sb("shalo", [128, 16 * 3], F32)
        sones = sb("sones", [1, 128], F32)
        sones8 = sb("sones8", [128, 8], F32)
        sbv = s32[:, 0:1]
        scv = s32[:, 1:2]
        sbs = s32[:, 2:3]
        sbsbv = s32[:, 3:4]
        smixf = smix.bitcast(F32)
        sbrow = smixf
        srawT = su[1]
        swsraw = pool[2].bitcast(F32)
        srt = [sb(f"srt{i}", [128, 32], F32) for i in range(6)]

        sslotm = sb("sslotm", [128, 64], F32)
        sidxf = sb("sidxf", [128, 16], F32)
        sidxu = sb("sidxu", [128, 16], U32)
        siotg = sb("siotg", [128, 4], F32)
        siotl = sb("siotl", [128, 4], F32)
        shug = sb("shug", [128, 16], F32)
        stabT = sb("stabT", [128, 32], F32)
        stabTi = sb("stabTi", [128, 32], U32)
        swab = sb("swab", [128, 8], F32)
        sb8 = sb("sb8", [8, 4], F32)
        smaskT = ssbc[0:8, 0:512]
        scum = su[0][0:8, 0:512]
        sregs = sb("sregs", [128, 8], F32)
        sregs2 = sb("sregs2", [128, 8], F32)
        srm = [sregs[:, i:i + 1] for i in range(4)]

        pb = [ps(f"pb{i}", [128, 512], F32) for i in range(8)]
        pbb = [b.bitcast(BF16) for b in pb]

        sems = {}
        for s_name in ("pe", "dv", "ac", "gp", "sd", "gd", "ad",
                       "gi0", "gi1", "gs0", "gs1", "gx", "gt",
                       "gy0", "gy1", "gn0", "gn1", "go", "gz",
                       "ap0", "ap1", "ap2", "aw0", "aw1", "ar"):
            sems[s_name] = st.enter_context(nc.semaphore(s_name))
        block = st.enter_context(nc.Block())
        P = Sched(nc, sems)

        bank_w = [None] * 8
        bank_r = [None] * 8

        def mm_group(items, bank_i, extra_waits=(), sem="pe"):
            P.waits("tensor", list(extra_waits) + [bank_r[bank_i]])
            n = len(items)
            tok = None
            for i, (o, l, r) in enumerate(items):
                s0, s1 = (i == 0), (i == n - 1)
                if s1:
                    tok = P.op("tensor",
                               lambda t, o=o, l=l, r=r, s0=s0, s1=s1:
                               t.matmul(o, l, r, start=s0, stop=s1), sem=sem)
                else:
                    P.op("tensor",
                         lambda t, o=o, l=l, r=r, s0=s0, s1=s1:
                         t.matmul(o, l, r, start=s0, stop=s1))
            bank_w[bank_i] = tok
            return tok

        def evac(eng, fn, bank_i, extra_waits=(), sem=None):
            sem = sem or ("dv" if eng == "vector" else "ac")
            tok = P.op(eng, fn, sem=sem,
                       waits=list(extra_waits) + [bank_w[bank_i]])
            bank_r[bank_i] = tok
            return tok

        regs = {}

        def _alloc_regs(g):
            b_idx = g.alloc_register(name="b_idx")
            regs["b_idx"] = b_idx
            g.reg_mov(regs["b_idx"], 2047)
            b_tok = g.alloc_register(name="b_tok")
            regs["b_tok"] = b_tok
            g.reg_mov(regs["b_tok"], 2 * CH - 1)
            b_slot = g.alloc_register(name="b_slot")
            regs["b_slot"] = b_slot
            g.reg_mov(regs["b_slot"], 1023)
        P.streams["gpsimd"].append(_alloc_regs)

        t_iota = P.op("gpsimd", lambda g: g.iota(
            sidf[:], [[-1, 128]], base=0, channel_multiplier=1,
            allow_small_or_imprecise_dtypes=True), sem="gp")
        t_ones = P.op("gpsimd", lambda g: g.memset(sones[:], 1.0), sem="gp")
        t_ones8 = P.op("gpsimd", lambda g: g.memset(sones8[:], 1.0), sem="gp")
        t_halo0 = P.op("gpsimd", lambda g: g.memset(shalo[:], 0.0), sem="gp")
        t_b8 = P.op("gpsimd", lambda g: g.iota(
            sb8[0:8, 0:1], [[1, 1]], base=-1, channel_multiplier=256,
            allow_small_or_imprecise_dtypes=True), sem="gp")
        t_b8 = P.op("gpsimd", lambda g: g.iota(
            sb8[0:8, 2:3], [[1, 1]], base=255, channel_multiplier=256,
            allow_small_or_imprecise_dtypes=True), sem="gp")
        t_iotl = P.op("gpsimd", lambda g: g.iota(
            siotl[:], [[128, 4]], base=0, channel_multiplier=1,
            allow_small_or_imprecise_dtypes=True), sem="gp")
        t_hug = P.op("gpsimd", lambda g: g.memset(
            shug[:], float(1 << 20)), sem="gp")
        t_arena1_z = P.op("gpsimd", lambda g: g.memset(ar1[:], 0.0),
                          sem="gp")


        t_idf = P.op("vector", lambda v: v.tensor_scalar(
            sidf[:], sidf[:], 0, None, ALU.is_equal), sem="dv",
            waits=[t_iota])
        t_idb = P.op("vector", lambda v: v.tensor_copy(
            sidb[:], sidf[:]), sem="dv", waits=[t_idf])
        t_ht0 = P.op("vector", lambda v: v.memset(sHT[:, 0:1], 0.0), sem="dv")

        def ld(dst, src, waits=()):
            return P.op("sync", lambda s, d=dst, sr=src: s.dma_start(d, sr),
                        sem="sd", amt=16, waits=waits)

        t_a = ld(sA[:], A_d)
        t_bv = ld(sbv, bass.AP(Bv_d.tensor, 0, [[1, N], [1, 1]]))
        t_cv = ld(scv, bass.AP(Cv_d.tensor, 0, [[1, N], [1, 1]]))
        t_bs = ld(sbs, bass.AP(b_ssm.tensor, 0, [[1, N], [1, 1]]))
        t_binr = ld(srawT[0:32, 0:128],
                    bass.AP(b_in.tensor, 0, [[128, 32], [1, 128]]))
        t_cws = ld(sCWS[:],
                   bass.AP(conv_w.tensor, 0, [[4, 128], [512, 16], [1, 4]]))
        t_brow = ld(sbrow[0:1, 0:D], bass.AP(b_out.tensor, 0, [[D, 1], [1, D]]))
        t_wsr = ld(swsraw[0:N, 0:H], w_ssm)
        t_wg = P.op("gpsimd", lambda g: g.dma_start(
            sWg[:], bass.AP(wg.tensor, 0, [[8, 128], [1024, 8], [1, 8]])),
            sem="gd", amt=16)

        t_bsbv = P.op("vector", lambda v: v.tensor_tensor(
            sbsbv, sbs, sbv, ALU.mult), sem="dv",
            waits=[t_bv, t_bs])

        def small_transpose(dst_ap, src_ap, in_p, load_tok):
            tw = P.op("tensor", lambda t, src_ap=src_ap, in_p=in_p:
                      t.transpose(pb[6][0:128, 0:in_p], src_ap,
                                  sidf[0:in_p, 0:in_p]),
                      sem="pe", waits=[load_tok, t_idf, bank_r[6]])
            bank_w[6] = tw
            return evac("vector", lambda v, dst_ap=dst_ap, in_p=in_p:
                        v.tensor_copy(dst_ap, pb[6][0:128, 0:in_p]), 6)

        t_bint = small_transpose(sBINT[:], srawT[0:32, 0:128], 32, t_binr)
        t_cbr = ld(srawT[0:16, 0:128],
                   bass.AP(conv_b.tensor, 0, [[128, 16], [1, 128]]),
                   waits=[t_bint])
        t_cbt = small_transpose(sCBT[:], srawT[0:16, 0:128], 16, t_cbr)
        t_dvr = ld(srawT[0:16, 0:128],
                   bass.AP(Dv_d.tensor, 0, [[128, 16], [1, 128]]),
                   waits=[t_cbt])
        t_dvt = small_transpose(sDVT[:], srawT[0:16, 0:128], 16, t_dvr)

        t_wssm_pe = None
        for hb in range(16):
            tw = P.op("tensor", lambda t, hb=hb: t.transpose(
                pb[6][0:128, 0:32], swsraw[0:N, 128 * hb:128 * (hb + 1)],
                sidf[0:32, 0:32]), sem="pe",
                waits=[t_wsr, t_idf, bank_r[6]])
            bank_w[6] = tw
            t_wssm_pe = tw
            evac("vector", lambda v, hb=hb: v.tensor_copy(
                wssmT[:, 32 * hb:32 * (hb + 1)], pb[6][0:128, 0:32]), 6)

        for h2 in range(2):
            mm_group([(pb[h2][0:128, 0:512], sones[0:1, :],
                       sbrow[0:1, 512 * h2:512 * (h2 + 1)])], h2,
                     extra_waits=[t_brow, t_ones])
            evac("vector", lambda v, h2=h2: v.tensor_copy(
                sBOUT[:, 512 * h2:512 * (h2 + 1)], pb[h2][0:128, 0:512]), h2)

        t0 = P.op("vector", lambda v: v.tensor_copy(
            cb[0][:, 0:N], sidf[0:N, 0:N]), waits=[t_idf])
        t0b = P.op("vector", lambda v: v.tensor_copy(
            cb[0][:, N:N + 1], scv), sem="dv", waits=[t_cv])
        P.op("vector", lambda v: v.tensor_copy(cmatT[:, 0:1], scv))
        P.op("vector", lambda v: v.tensor_copy(
            mr[96:128, 31 * N:32 * N], sidf[0:N, 0:N]))
        chain_state = {"prev": t0b}

        def chain_step(k):
            tw = P.op("tensor", lambda t, k=k: t.matmul(
                pb[7][0:N, 0:N + 1], sA[:], cb[k % 2][:],
                start=True, stop=True),
                sem="pe", waits=[t_a, chain_state["prev"], bank_r[7]])
            bank_w[7] = tw
            te = evac("vector", lambda v, k=k: v.tensor_copy(
                cb[(k + 1) % 2][:], pb[7][0:N, 0:N + 1]), 7)
            kk = k + 1
            P.op("vector", lambda v, kk=kk: v.tensor_copy(
                cmatT[:, kk:kk + 1], cb[kk % 2][:, N:N + 1]), waits=[te])
            if kk <= T - 1:
                q = T - 1 - kk
                kc, j = q // 4, q % 4
                P.op("vector", lambda v, kk=kk, kc=kc, j=j: v.tensor_copy(
                    mr[32 * j:32 * (j + 1), N * kc:N * (kc + 1)],
                    cb[kk % 2][:, 0:N]))
            elif kk == T:
                P.op("vector", lambda v, kk=kk: v.tensor_copy(
                    sPm[:], cb[kk % 2][:, 0:N]))
            chain_state["prev"] = te

        chain_steps = [lambda k=k: chain_step(k) for k in range(T + 1)]

        def emit_btz(t_zero):
            P.wait("vector", t_zero)
            P.wait("vector", chain_state["prev"])
            t_btz = None
            for q in range(T):
                kc, j = q // 4, q % 4
                t_btz = P.op("vector", lambda v, kc=kc, j=j, q=q:
                             v.tensor_copy(
                    btz[32 * j:32 * (j + 1), T * kc + q:T * (kc + 1)],
                    cmatT[:, 0:T - q]), sem="dv" if q == T - 1 else None)
            return t_btz

        stage_a = pool[2].bitcast(F32)
        stage_b = pool[2]
        bst = {"store": [None, None], "first": True}
        wit_tok = [None] * 16
        wit_ready = [None] * 16
        wit_mm_tok = [None, None]
        wout_done = [None]

        wit_steps = []
        for hb_ in range(16):
            for half_ in range(2):
                def wstep(hb=hb_, half=half_):
                    w = half
                    zt = hb + 16 * half
                    extra = [t_wssm_pe] if bst["first"] else []
                    bst["first"] = False
                    t_l = P.op("sync", lambda s, zt=zt, w=w: s.dma_start(
                        stage_a[:, 2048 * w:2048 * w + 1024],
                        w_in[128 * zt:128 * (zt + 1), :]), sem="sd",
                        amt=16, waits=[bst["store"][w]] + extra)
                    ring = hb % 2
                    for g2 in range(2):
                        bk = 4 + g2
                        for dq in range(4):
                            dblk = 4 * g2 + dq
                            tw = P.op("tensor", lambda t, w=w, dblk=dblk,
                                      dq=dq, bk=bk: t.transpose(
                                pb[bk][0:128, 128 * dq:128 * (dq + 1)],
                                stage_a[:, 2048 * w + 128 * dblk:
                                        2048 * w + 128 * (dblk + 1)],
                                sidf[:]),
                                sem="pe" if dq == 3 else None,
                                waits=([t_l, t_idf, bank_r[bk]]
                                       if dq == 0 else []))
                        bank_w[bk] = tw
                        t_e = evac("vector", lambda v, ring=ring, half=half,
                                   bk=bk, g2=g2: v.tensor_copy(
                            swit[ring][:, 1024 * half + 512 * g2:
                                       1024 * half + 512 * (g2 + 1)],
                            pb[bk][0:128, 0:512]), bk,
                            extra_waits=[wit_mm_tok[ring]] if g2 == 0 else ())
                    t_st = P.op("sync", lambda s, hb=hb, half=half, ring=ring:
                                s.dma_start(
                        bass.AP(scr_wit.tensor,
                                hb * 128 * 2048 + half * 1024,
                                [[2048, 128], [1, 1024]]),
                        swit[ring][:, 1024 * half:1024 * half + 1024]),
                        sem="sd", amt=16, waits=[t_e])
                    bst["store"][w] = t_st
                    if half == 1:
                        wit_tok[hb] = t_st
                        wit_ready[hb] = t_e
                wit_steps.append(wstep)

        wout_steps = []
        for dt_ in range(8):
            def wostep(dt=dt_):
                w = dt % 2
                t_l = P.op("sync", lambda s, dt=dt, w=w: s.dma_start(
                    stage_a[:, 2048 * w:2048 * (w + 1)],
                    w_out[128 * dt:128 * (dt + 1), :]), sem="sd", amt=16,
                    waits=[bst["store"][w]])
                t_e = None
                for g4 in range(4):
                    bk = 4 + g4 % 2
                    for dq in range(4):
                        hb = 4 * g4 + dq
                        tw = P.op("tensor", lambda t, hb=hb, dq=dq, bk=bk,
                                  w=w: t.transpose(
                            pb[bk][0:128, 128 * dq:128 * (dq + 1)],
                            stage_a[:, 2048 * w + 128 * hb:
                                    2048 * w + 128 * (hb + 1)], sidf[:]),
                            sem="pe" if dq == 3 else None,
                            waits=([t_l, bank_r[bk]] if dq == 0 else []))
                    bank_w[bk] = tw
                    t_e = evac("vector", lambda v, w=w, bk=bk, g4=g4:
                               v.tensor_copy(
                        stage_b[:, 8192 + 2048 * w + 512 * g4:
                                8192 + 2048 * w + 512 * (g4 + 1)],
                        pb[bk][0:128, 0:512]), bk)
                t_st = P.op("sync", lambda s, dt=dt, w=w: s.dma_start(
                    bass.AP(scr_wout.tensor, dt * 128,
                            [[1024, 128], [128 * 1024, 16], [1, 128]]),
                    stage_b[:, 8192 + 2048 * w:8192 + 2048 * (w + 1)]),
                    sem="sd", amt=16, waits=[t_e])
                bst["store"][w] = t_st
                if dt == 7:
                    wout_done[0] = t_st
            wout_steps.append(wostep)

        pump_state = {"wit": 0, "wout": 0, "chain": 0}

        def pump(name, steps, n):
            i = pump_state[name]
            while n > 0 and i < len(steps):
                steps[i]()
                i += 1
                n -= 1
            pump_state[name] = i

        pool_rot = [0]
        pool_tok = [None, None, None]
        pool_stok = [None, None, None]
        w1_scr_tok = [None] * E
        w2_scr_tok = [None] * E
        swit_tok = [None] * 3
        t_sht_reader = None
        t_sout_reader = None
        t_p1_reader = None
        t_e_bufs_reader = None
        t_xnt_reader = None
        t_svr_reader = None
        t_ex_prev = None

        WOUT_AP = bass.AP(scr_wout.tensor, 0,
                          [[1024, 128], [128 * 1024, 16], [1, 1024]])

        def w1_ap(e, t):
            return bass.AP(t.tensor, e * D * FF,
                           [[2048, 128], [128 * 2048, 8], [1, 2048]])

        def w2_ap(e, t):
            return bass.AP(t.tensor, e * FF * D,
                           [[1024, 128], [128 * 1024, 16], [1, 1024]])

        def pool_load(src_ap, waits=(), store_to=None, eng="gpsimd",
                      sem=None):
            slot = pool_rot[0] % 3
            pool_rot[0] += 1
            tok = P.op(eng, lambda g, s=slot, src=src_ap: g.dma_start(
                pool[s][:, 0:16384], src), sem=f"ap{slot}", amt=16,
                waits=list(waits) + [pool_tok[slot], pool_stok[slot]])
            stok = None
            if store_to is not None:
                stok = P.op("sync", lambda s, sl=slot, d=store_to:
                            s.dma_start(d, pool[sl][:, 0:16384]),
                            sem="sd", amt=16, waits=[tok])
            pool_stok[slot] = stok
            return slot, tok, stok

        def dump(dst, src_ap, waits):
            return P.op("gpsimd", lambda g, d=dst, s=src_ap: g.dma_start(d, s),
                        sem="gd", amt=16, waits=waits)

        front_tok = {}
        scrxn_reader = {}

        def emit_xload(c, btz_reader_waits):
            return P.op("gpsimd", lambda g, c=c: g.dma_start(
                btz[:].rearrange("p (t d) -> p t d", t=NT),
                bass.AP(x_d.tensor, c * CH * D,
                        [[1024, 128], [128 * 1024, NT], [1, 1024]])),
                sem="gz", amt=16, waits=list(btz_reader_waits))

        def emit_front_stats(c, sq_waits, t_x):
            t_red = None
            t_sqa = None
            for t in range(NT):
                xs = btz[:, 1024 * t:1024 * (t + 1)]
                t_red = P.op("vector", lambda v, xs=xs, t=t: v.tensor_reduce(
                    sregs[:, t:t + 1], xs, AX.X, ALU.add),
                    sem="dv" if t == NT - 1 else None,
                    waits=[t_x])
                P.op("scalar", lambda s, xs=xs, t=t: s.activation(
                    su[0][:], xs[:, 0:512], AF.Square,
                    accum_out=sregs[:, 4 + t:5 + t]),
                    waits=[t_x] + (list(sq_waits) if t == 0 else []))
                t_sqa = P.op("scalar", lambda s, xs=xs, t=t: s.activation(
                    su[1][:], xs[:, 512:1024], AF.Square,
                    accum_out=sregs2[:, 4 + t:5 + t]),
                    sem="ac" if t == NT - 1 else None)
            su_silu_tok[0] = t_sqa
            su_silu_tok[1] = t_sqa
            tc = P.op("vector", lambda v: v.tensor_scalar(
                sregs2[:, 0:4], sregs[:, 0:4], 1.0 / D, None, ALU.mult),
                sem="dv", waits=[t_red])
            tc = P.op("vector", lambda v: v.scalar_tensor_tensor(
                sregs[:, 0:4], sregs2[:, 0:4], -1.0, sregs2[:, 0:4],
                ALU.mult, ALU.mult), sem="dv", waits=[tc])
            tc2 = P.op("vector", lambda v: v.tensor_tensor(
                sregs[:, 4:8], sregs[:, 4:8], sregs2[:, 4:8], ALU.add),
                sem="dv", waits=[t_sqa])
            tc2 = P.op("vector", lambda v: v.tensor_scalar(
                sregs[:, 4:8], sregs[:, 4:8], 1.0 / D, None, ALU.mult),
                sem="dv", waits=[tc2])
            tc = P.op("vector", lambda v: v.tensor_tensor(
                sregs2[:, 4:8], sregs[:, 4:8], sregs[:, 0:4], ALU.add),
                sem="dv", waits=[tc, tc2])
            tc = P.op("vector", lambda v: v.tensor_scalar(
                sregs2[:, 4:8], sregs2[:, 4:8], 1e-5, None, ALU.add),
                sem="dv", waits=[tc])
            tc = P.op("scalar", lambda s: s.activation(
                sregs[:, 4:8], sregs2[:, 4:8], AF.Sqrt), sem="ac",
                waits=[tc])
            t_rstd = P.op("vector", lambda v: v.reciprocal(
                sregs[:, 4:8], sregs[:, 4:8]), sem="dv", waits=[tc])
            front_tok["t_x"] = t_x
            front_tok["t_sq_last"] = t_sqa
            front_tok["t_rstd"] = t_rstd

        def emit_front_tile(c, t, first=False):
            t_rstd = front_tok["t_rstd"]
            par = c % 2
            xs = btz[:, 1024 * t:1024 * (t + 1)]
            t_xn = P.op("vector", lambda v, xs=xs, t=t: v.tensor_scalar(
                sxn[:, 1024 * t:1024 * (t + 1)], xs,
                sregs2[:, t:t + 1], sregs[:, 4 + t:5 + t],
                ALU.subtract, ALU.mult), sem="dv",
                waits=[t_rstd] +
                ([front_tok.get("t_xnst")] if first else []))
            t_xnst = P.op("gpsimd", lambda g, t=t, par=par: g.dma_start(
                scr_xn[par * CH + T * t:par * CH + T * (t + 1), :],
                sxn[:, 1024 * t:1024 * (t + 1)]),
                sem=f"gn{par}", amt=16,
                waits=[t_xn] + list(scrxn_reader.get(par, ())))
            t_xnt_all = None
            for g2 in range(2):
                bk = 4 + g2
                for dq in range(4):
                    dblk = 4 * g2 + dq
                    tw = P.op("tensor", lambda tn, t=t, dblk=dblk,
                              dq=dq, bk=bk: tn.transpose(
                        pb[bk][0:128, 128 * dq:128 * (dq + 1)],
                        sxn[:, 1024 * t + 128 * dblk:
                            1024 * t + 128 * (dblk + 1)], sidf[:]),
                        sem="pe" if dq == 3 else None,
                        waits=([t_xn, bank_r[bk]] if dq == 0 else []))
                bank_w[bk] = tw
                t_xnt_all = evac(
                    "vector", lambda v, t=t, bk=bk, g2=g2: v.tensor_copy(
                        xnt[:].rearrange("p (d ch) -> p d ch", d=8)
                        [:, 4 * g2:4 * (g2 + 1), T * t:T * (t + 1)],
                        pb[bk][0:128, 0:512].rearrange(
                            "p (d ch) -> p d ch", d=4)),
                    bk, extra_waits=[front_tok.get("xnt_reader")])
            front_tok["t_xn_last"] = t_xn
            front_tok["t_xnt_all"] = t_xnt_all
            front_tok["t_xnst"] = t_xnst

        t_btz = None
        t_mm2_prev = None
        t_iog_reader = None
        t_tab_reader = None
        t_stab_reader = None
        t_xgt_cur = None
        xg_tok = [None, None]
        t_yab_last = None
        yab_fin = [None, None]
        t_smaskT_war = None
        su_scrib = {"t": None}
        su_silu_tok = [None, None]
        xgt_tok = [None, None]
        last_ga = [None, None]
        yp_tok = [None, None]
        yab_tok = [None] * 4
        t_yscat_all = None
        for c in range(n_chunks):
            if c == 0:
                t_x0 = emit_xload(0, [None])
                emit_front_stats(0, [None], t_x0)
                for _t in range(NT):
                    emit_front_tile(0, _t, first=(_t == 0))
            t_xnt_all = front_tok["t_xnt_all"]
            t_ln_last = front_tok["t_xn_last"]
            t_sq_last = front_tok["t_sq_last"]
            if c == 0:
                w1_slot, t_w1, w1_scr_tok[0] = pool_load(
                    w1_ap(0, w1), store_to=w1_ap(0, scr_w1))
            else:
                wout_slot, t_wout, _ = pool_load(
                    WOUT_AP, [wout_done[0]], eng="scalar")
                w1_slot, t_w1, _ = pool_load(
                    w1_ap(0, scr_w1), [w1_scr_tok[0]],
                    eng="scalar")
                t_btz = P.op("sync", lambda s: s.dma_start(
                    btz[:], scr_btz), sem="sd", amt=16,
                    waits=[t_ln_last, t_sq_last, t_scrbtz])

            t_ssm_group = None
            t_p1_last = None
            su_tok = [None] * 4
            sub_tok = [None] * 4
            sub_silu_tok = [None] * 4
            spad_tok = [None, None]
            ssg_tok = [None, None]
            for hb in range(16):
                if c == 0:
                    pump("wit", wit_steps, 2)
                    pump("chain", chain_steps, 9)
                slot = hb % 2
                if c == 0:
                    t_wl = wit_ready[hb]
                else:
                    t_wl = P.op("sync" if slot == 0 else "gpsimd",
                                lambda s, hb=hb, slot=slot:
                                s.dma_start(
                        swit[slot][:],
                        bass.AP(scr_wit.tensor, hb * 128 * 2048,
                                [[2048, 128], [1, 2048]])),
                        sem=f"aw{slot}", amt=16,
                        waits=[swit_tok[slot], wit_tok[hb]])
                pba, pbg = (0, 1) if hb % 2 == 0 else (2, 3)
                t_xh = mm_group(
                    [(pb[pba][0:128, 0:512],
                      swit[slot][:, 128 * dblk:128 * (dblk + 1)],
                      xnt[:, CH * dblk:CH * (dblk + 1)])
                     for dblk in range(8)],
                    pba, extra_waits=[t_wl, t_xnt_all])
                t_gt = mm_group(
                    [(pb[pbg][0:128, 0:512],
                      swit[slot][:, 1024 + 128 * dblk:
                                 1024 + 128 * (dblk + 1)],
                      xnt[:, CH * dblk:CH * (dblk + 1)])
                     for dblk in range(8)],
                    pbg, extra_waits=[t_wl, t_xnt_all])
                swit_tok[slot] = t_gt
                wit_mm_tok[slot] = t_gt
                w = hb % 2
                t_xhe = evac("scalar", lambda s, hb=hb, w=w, pba=pba:
                             s.activation(
                    spad[w][:, 3:3 + CH], pb[pba][0:128, 0:512],
                    AF.Identity, bias=sBINT[:, hb:hb + 1]), pba,
                    extra_waits=[spad_tok[w], t_bint])
                t_sge = evac("scalar", lambda s, hb=hb, w=w, pbg=pbg:
                             s.activation(
                    ssg[w][:], pb[pbg][0:128, 0:512],
                    AF.Silu, bias=sBINT[:, 16 + hb:17 + hb]), pbg,
                    extra_waits=[ssg_tok[w]])
                P.op("vector", lambda v, hb=hb, w=w: v.tensor_copy(
                    spad[w][:, 0:3], shalo[:, 3 * hb:3 * (hb + 1)]),
                    waits=[t_xhe, t_halo0])
                P.op("vector", lambda v, hb=hb, w=w: v.tensor_copy(
                    shalo[:, 3 * hb:3 * (hb + 1)], spad[w][:, CH:CH + 3]))
                P.op("vector", lambda v, hb=hb, w=w: v.tensor_scalar(
                    su[w][:], spad[w][:, 0:CH], sCWS[:, 4 * hb:4 * hb + 1],
                    None, ALU.mult), waits=[t_cws, su_silu_tok[w]])
                t_conv = None
                for k in range(1, 4):
                    t_conv = P.op("vector", lambda v, hb=hb, w=w, k=k:
                                  v.scalar_tensor_tensor(
                        su[w][:], spad[w][:, k:k + CH],
                        sCWS[:, 4 * hb + k:4 * hb + k + 1], su[w][:],
                        ALU.mult, ALU.add),
                        sem="dv" if k == 3 else None)
                spad_tok[w] = t_conv
                t_us = P.op("scalar", lambda s, hb=hb, w=w: s.activation(
                    sub[hb % 4][:], su[w][:], AF.Silu,
                    bias=sCBT[:, hb:hb + 1]), sem="ac",
                    waits=[t_conv, t_cbt, sub_tok[hb % 4], su_tok[hb % 4]])
                if hb >= 2:
                    hs = hb - 2
                    t_ssm_group = P.op("tensor", lambda t, hs=hs: t.matmul(
                        pb[6][0:N, 0:512], wssmT[:, N * hs:N * (hs + 1)],
                        sub[hs % 4][:], start=(hs == 0), stop=False),
                        sem="pe",
                        waits=[sub_silu_tok[hs % 4]] +
                        ([bank_r[6]] if hs == 0 else []))
                    sub_tok[hs % 4] = t_ssm_group
                t_p1w = P.op("vector", lambda v, hb=hb, w=w: v.tensor_tensor(
                    sp1[:, CH * hb:CH * (hb + 1)], sub[hb % 4][:], ssg[w][:],
                    ALU.mult), sem="dv",
                    waits=[t_us, t_sge] +
                    [t_e_bufs_reader, t_p1_reader])
                su_tok[hb % 4] = t_p1w
                sub_silu_tok[hb % 4] = t_us
                su_silu_tok[w] = t_us
                ssg_tok[w] = t_p1w
                t_p1_last = t_p1w

            for hs in (14, 15):
                t_ssm_group = P.op("tensor", lambda t, hs=hs: t.matmul(
                    pb[6][0:N, 0:512], wssmT[:, N * hs:N * (hs + 1)],
                    sub[hs % 4][:], start=False, stop=(hs == 15)),
                    sem="pe", waits=[sub_silu_tok[hs % 4]])
                sub_tok[hs % 4] = t_ssm_group
            bank_w[6] = t_ssm_group
            t_v = evac("scalar", lambda s: s.activation(
                sVT[:], pb[6][0:N, 0:512], AF.Identity,
                scale=sbv, bias=sbsbv), 6, extra_waits=[t_bsbv])
            if c == 0:
                pump("chain", chain_steps, T + 1)
                t_z = P.op("gpsimd", lambda g: g.memset(btz[:], 0.0),
                           sem="gp", waits=[t_ln_last, t_sq_last])
                t_btz = emit_btz(t_z)
                t_scrbtz = P.op("sync", lambda s: s.dma_start(
                    scr_btz, btz[:]), sem="sd", amt=16, waits=[t_btz])
                pump("wout", wout_steps, 8)
                pool_tok[2] = wout_done[0]
                wout_slot, t_wout, _ = pool_load(WOUT_AP, [wout_done[0]])
            if dbg and c == 0:
                t_dmp = None
                t_dmp = dump(dbg_p1, sp1[:], [t_p1_last, t_dmp])
                t_dmp = dump(dbg_vt, sVT[0:32, 0:512], [t_v, t_dmp])

            t_vr = None
            for j in range(4):
                t_vr = P.op("vector", lambda v, j=j: v.tensor_copy(
                    svr[32 * j:32 * (j + 1), :].rearrange(
                        "p (kc i) -> p kc i", kc=32),
                    bass.AP(sVT, j, [[CH, 32], [4, 32], [T, 4]])),
                    sem="dv" if j == 3 else None,
                    waits=[t_v] + ([t_svr_reader] if j == 0 else []))
            t_umm = mm_group(
                [(pb[5][0:NT, 0:N], svr[:, 4 * kc:4 * (kc + 1)],
                  mr[:, N * kc:N * (kc + 1)]) for kc in range(32)],
                5, extra_waits=[t_vr])

            t_ue = evac("vector", lambda v: v.tensor_copy(
                sU[:], pb[5][0:NT, 0:N]), 5)
            tw = P.op("tensor", lambda t: t.transpose(
                pb[5][0:N, 256:256 + NT], sU[:], sidf[0:NT, 0:NT]),
                sem="pe", waits=[t_ue, bank_r[5]])
            bank_w[5] = tw
            t_ut = evac("vector", lambda v: v.tensor_copy(
                sUT[:, 0:NT], pb[5][0:N, 256:256 + NT]), 5)
            P.wait("vector", t_ut)
            for i in range(NT):
                g = c * NT + i
                tw = P.op("tensor", lambda t, g=g: t.matmul(
                    pb[5][0:N, 300:301], sPm[:], sHT[:, g:g + 1],
                    start=True, stop=True), sem="pe",
                    waits=[t_ut, t_ht0, bank_r[5]])
                bank_w[5] = tw
                evac("vector", lambda v, g=g, i=i: v.tensor_tensor(
                    sHT[:, g + 1:g + 2], pb[5][0:N, 300:301],
                    sUT[:, i:i + 1], ALU.add), 5)
            t_hcols = bank_r[5]
            t_lg = None
            P.waits("tensor", [t_ex_prev, bank_r[7]])
            for t in range(NT):
                t_lg = mm_group(
                    [(pb[7][0:128, 8 * t:8 * (t + 1)],
                      xnt[:, CH * dblk + T * t:CH * dblk + T * (t + 1)],
                      sWg[:, 8 * dblk:8 * (dblk + 1)]) for dblk in range(8)],
                    7, extra_waits=[t_wg, t_xnt_all])
            pb7v = pb[7][0:128, 0:32].rearrange("p (t e) -> p t e", t=NT)
            tr = P.op("vector", lambda v: v.tensor_reduce(
                sregs[:, 0:4], pb7v, AX.X, ALU.max, negate=True),
                sem="dv", waits=[t_lg])
            neg_b = sregs[:, 0:4].broadcast_to((128, NT, 8))
            tr = P.op("vector", lambda v: v.tensor_tensor(
                srt[0][:].rearrange("p (t e) -> p t e", t=NT), pb7v,
                neg_b, ALU.add), sem="dv", waits=[tr])
            t_ex = P.op("scalar", lambda s: s.activation(
                srt[1][:], srt[0][:], AF.Exp), sem="ac", waits=[tr])
            t_ex_prev = t_ex
            bank_r[7] = t_ex
            tr = P.op("vector", lambda v: v.tensor_reduce(
                sregs[:, 0:4], srt[1][:].rearrange("p (t e) -> p t e", t=NT),
                AX.X, ALU.add), sem="dv", waits=[t_ex])
            tr = P.op("vector", lambda v: v.reciprocal(
                sregs[:, 0:4], sregs[:, 0:4]), sem="dv", waits=[tr])
            rec_b = sregs[:, 0:4].broadcast_to((128, NT, 8))
            tr = P.op("vector", lambda v: v.tensor_tensor(
                srt[1][:].rearrange("p (t e) -> p t e", t=NT),
                srt[1][:].rearrange("p (t e) -> p t e", t=NT),
                rec_b, ALU.mult), sem="dv", waits=[tr])
            tr = P.op("vector", lambda v: v.tensor_reduce(
                sregs[:, 0:4], srt[1][:].rearrange("p (t e) -> p t e", t=NT),
                AX.X, ALU.max), sem="dv", waits=[tr])
            m1_b = sregs[:, 0:4].broadcast_to((128, NT, 8))
            tr = P.op("vector", lambda v: v.tensor_tensor(
                srt[2][:].rearrange("p (t e) -> p t e", t=NT),
                srt[1][:].rearrange("p (t e) -> p t e", t=NT),
                m1_b, ALU.is_equal), sem="dv", waits=[tr])
            tsc = None
            for t in range(NT):
                tsc = P.op("vector", lambda v, t=t: v.tensor_tensor_scan(
                    srt[3][:, 8 * t:8 * (t + 1)], sones8[:],
                    srt[2][:, 8 * t:8 * (t + 1)], 0.0, ALU.mult, ALU.add),
                    sem="dv" if t == NT - 1 else None,
                    waits=[tr, t_ones8])
            tr = P.op("vector", lambda v: v.tensor_scalar(
                srt[3][:], srt[3][:], 1.0, None, ALU.is_equal),
                sem="dv", waits=[tsc])
            tr = P.op("vector", lambda v: v.tensor_tensor(
                srt[2][:], srt[2][:], srt[3][:], ALU.mult),
                sem="dv", waits=[tr])
            tsc1_tok = tr
            tr = P.op("vector", lambda v: v.scalar_tensor_tensor(
                srt[4][:], srt[2][:], -1e30, srt[1][:], ALU.mult, ALU.add),
                sem="dv", waits=[tr])
            tr = P.op("vector", lambda v: v.tensor_reduce(
                sregs2[:, 0:4], srt[4][:].rearrange("p (t e) -> p t e", t=NT),
                AX.X, ALU.max), sem="dv", waits=[tr])
            m2_b = sregs2[:, 0:4].broadcast_to((128, NT, 8))
            tr = P.op("vector", lambda v: v.tensor_tensor(
                srt[5][:].rearrange("p (t e) -> p t e", t=NT),
                srt[4][:].rearrange("p (t e) -> p t e", t=NT),
                m2_b, ALU.is_equal), sem="dv", waits=[tr])
            tsc = None
            for t in range(NT):
                tsc = P.op("vector", lambda v, t=t: v.tensor_tensor_scan(
                    srt[3][:, 8 * t:8 * (t + 1)], sones8[:],
                    srt[5][:, 8 * t:8 * (t + 1)], 0.0, ALU.mult, ALU.add),
                    sem="dv" if t == NT - 1 else None, waits=[tr])
            tr = P.op("vector", lambda v: v.tensor_scalar(
                srt[3][:], srt[3][:], 1.0, None, ALU.is_equal),
                sem="dv", waits=[tsc])
            t_mask2 = P.op("vector", lambda v: v.tensor_tensor(
                srt[5][:], srt[5][:], srt[3][:], ALU.mult),
                sem="dv", waits=[tr])
            t_mask1 = tsc1_tok
            tden = P.op("vector", lambda v: v.tensor_tensor(
                sregs2[:, 4:8], sregs[:, 0:4], sregs2[:, 0:4], ALU.add),
                sem="dv", waits=[t_mask2])
            tden = P.op("vector", lambda v: v.reciprocal(
                sregs2[:, 4:8], sregs2[:, 4:8]), sem="dv", waits=[tden])
            tw1 = P.op("vector", lambda v: v.tensor_tensor(
                swab[:, 0:4], sregs[:, 0:4], sregs2[:, 4:8], ALU.mult),
                sem="dv", waits=[tden])
            t_wab = P.op("vector", lambda v: v.tensor_tensor(
                swab[:, 4:8], sregs2[:, 0:4], sregs2[:, 4:8], ALU.mult),
                sem="dv", waits=[tw1])
            bank_r[7] = t_wab

            t_on = P.op("vector", lambda v: v.memset(
                su[1][0:8, 0:512], 1.0), sem="dv", waits=[su_silu_tok[1]])
            t_s12 = [None, None]
            for r in range(2):
                mk = srt[2] if r == 0 else srt[5]
                t_mk = t_mask1 if r == 0 else t_mask2
                bkx = 6 + r
                for t in range(NT):
                    tw = P.op("tensor", lambda tn, t=t, mk=mk, bkx=bkx:
                              tn.transpose(
                        pb[bkx][0:8, 128 * t:128 * (t + 1)],
                        mk[:, 8 * t:8 * (t + 1)], sidf[:]),
                        sem="pe" if t == NT - 1 else None,
                        waits=([t_mk, bank_r[bkx], t_smaskT_war if r == 0
                                else None] if t == 0 else []))
                bank_w[bkx] = tw
                t_mT = evac("vector", lambda v, bkx=bkx: v.tensor_copy(
                    smaskT, pb[bkx][0:8, 0:512]), bkx)
                tk = P.op("vector", lambda v: v.tensor_tensor_scan(
                    scum, su[1][0:8, 0:512], smaskT,
                    0.0, ALU.mult, ALU.add), sem="dv",
                    waits=[t_on, t_mT])
                if r == 0:
                    tk = P.op("vector", lambda v: v.tensor_tensor(
                        sb8[0:8, 1:2], scum[:, 511:512], sb8[0:8, 0:1],
                        ALU.add), sem="dv", waits=[tk, t_b8])
                tk = P.op("vector", lambda v, r=r: v.tensor_scalar(
                    scum, scum, sb8[0:8, r:r + 1], None,
                    ALU.add), sem="dv", waits=[tk])
                tk = P.op("vector", lambda v: v.tensor_scalar(
                    smaskT, scum, sb8[0:8, 2:3], None,
                    ALU.is_gt), sem="dv", waits=[tk])
                tk = P.op("vector", lambda v: v.scalar_tensor_tensor(
                    scum, smaskT, float(1 << 20), scum,
                    ALU.mult, ALU.add), sem="dv", waits=[tk])
                bky = 4 + r
                for t in range(NT):
                    tw = P.op("tensor", lambda tn, t=t, bky=bky:
                              tn.transpose(
                        pb[bky][0:128, 8 * t:8 * (t + 1)],
                        scum[:, 128 * t:128 * (t + 1)], sidf[0:8, 0:8]),
                        sem="pe" if t == NT - 1 else None,
                        waits=([tk, bank_r[bky]] if t == 0 else []))
                bank_w[bky] = tw
                t_s12[r] = evac("vector", lambda v, r=r, bky=bky: v.tensor_copy(
                    sslotm[:, 32 * r:32 * (r + 1)], pb[bky][0:128, 0:32]),
                    bky)
            su_scrib["t"] = tk
            P.op("vector", lambda v: v.tensor_scalar(
                sidxf[:, 8:12], siotl[:], 0.0, None, ALU.add),
                waits=[t_iotl])
            tk3 = P.op("vector", lambda v: v.tensor_scalar(
                sidxf[:, 12:16], siotl[:], 512.0, None, ALU.add), sem="dv")
            t_iog = P.op("gpsimd", lambda g, c=c: g.iota(
                siotg[:], [[128, 4]], base=(c % 2) * CH,
                channel_multiplier=1, allow_small_or_imprecise_dtypes=True),
                sem="gp", waits=[t_iog_reader])
            t_init1 = P.op("gpsimd", lambda g: g.dma_start(
                bass.AP(scr_idx.tensor, 0, [[16, 128], [1, 16]]), shug[:]),
                sem="gx", amt=16, waits=[t_hug, t_tab_reader])
            t_init2 = P.op("gpsimd", lambda g: g.dma_start(
                bass.AP(scr_idx2.tensor, 0, [[16, 128], [1, 16]]), shug[:]),
                sem="gx", amt=16)
            tk2 = None
            t_idxu = None
            for r in range(2):
                mk = srt[2] if r == 0 else srt[5]
                tmu = P.op("vector", lambda v, mk=mk, r=r: v.tensor_tensor(
                    srt[0][:], mk[:],
                    sslotm[:, 32 * r:32 * (r + 1)], ALU.mult), sem="dv",
                    waits=([t_s12[0], t_s12[1]] if r == 0 else [tk2]))
                tk2 = P.op("vector", lambda v, r=r: v.tensor_reduce(
                    sidxf[:, 4 * r:4 * (r + 1)],
                    srt[0][:].rearrange("p (t e) -> p t e", t=NT),
                    AX.X, ALU.add), sem="dv", waits=[tmu])
                t_idxu = P.op("vector", lambda v, r=r: v.tensor_copy(
                    sidxu[:, 4 * r:4 * (r + 1)],
                    sidxf[:, 4 * r:4 * (r + 1)]), sem="dv",
                    waits=[tk2, tk3])
                for t in range(NT):
                    t_i1 = P.op("gpsimd", lambda g, r=r, t=t:
                                g.indirect_dma_start(
                        bass.AP(scr_idx.tensor, 0, [[1, 2048], [1, 1]]),
                        bass.IndirectOffsetOnAxis(
                            ap=sidxu[:, 4 * r + t:4 * r + t + 1], axis=0),
                        siotg[:, t:t + 1], None,
                        bounds_check=regs["b_idx"], oob_is_err=False),
                        sem="gx", amt=16,
                        waits=[t_idxu, t_iog, t_init1, t_init2])
                    t_i2 = P.op("gpsimd", lambda g, r=r, t=t:
                                g.indirect_dma_start(
                        bass.AP(scr_idx2.tensor, 0, [[1, 2048], [1, 1]]),
                        bass.IndirectOffsetOnAxis(
                            ap=sidxu[:, 4 * r + t:4 * r + t + 1], axis=0),
                        sidxf[:, 8 + 4 * r + t:9 + 4 * r + t], None,
                        bounds_check=regs["b_idx"], oob_is_err=False),
                        sem="gx", amt=16)
            stab16 = ssbc
            t_tl = P.op("sync", lambda s: s.dma_start(
                stab16[0:16, 0:128],
                bass.AP(scr_idx.tensor, 0, [[128, 16], [1, 128]])),
                sem="gt", amt=16,
                waits=[t_i1, t_i2, t_xgt_cur, t_yscat_all,
                       yp_tok[0], yp_tok[1], t_stab_reader])
            t_tl = P.op("sync", lambda s: s.dma_start(
                stab16[0:16, 128:256],
                bass.AP(scr_idx2.tensor, 0, [[128, 16], [1, 128]])),
                sem="gt", amt=16)
            for h2 in range(2):
                tw = P.op("tensor", lambda tn, h2=h2: tn.transpose(
                    pb[6 + h2][0:128, 448:464],
                    stab16[0:16, 128 * h2:128 * (h2 + 1)],
                    sidf[0:16, 0:16]), sem="pe",
                    waits=[t_tl, bank_r[6 + h2]])
                bank_w[6 + h2] = tw
                evac("vector", lambda v, h2=h2: v.tensor_copy(
                    stabT[:, 16 * h2:16 * (h2 + 1)],
                    pb[6 + h2][0:128, 448:464]), 6 + h2)
            t_tab = P.op("vector", lambda v: v.tensor_copy(
                stabTi[:], stabT[:]), sem="dv",
                waits=[bank_r[6], bank_r[7], last_ga[0], last_ga[1],
                       yp_tok[0], yp_tok[1]])
            t_stab_reader = tw
            t_tab_reader = t_tl
            t_iog_reader = t_i1
            stabTu = stabTi
            if dbg and c == 0:
                t_dmp = dump(dbg_idx, sidxf[:], [t_idxu, t_dmp])
                t_dmp = dump(dbg_wab, swab[:], [t_wab, t_dmp])
                t_dmp = dump(dbg_m12a, srt[2][:], [t_mask1, t_dmp])
                t_dmp = dump(dbg_m12b, srt[5][:], [t_mask2, t_dmp])
                t_dmp = dump(dbg_slot, sslotm[:, 0:64],
                             [t_s12[0], t_s12[1], t_dmp])
                t_dmp = dump(dbg_tab, stabT[:], [t_tab, t_dmp])
            P.waits("tensor", [t_vr, bank_r[4]])
            for kc in range(32):
                P.op("tensor", lambda t, kc=kc: t.matmul(
                    pb[4][0:NT, 0:T], svr[:, 4 * kc:4 * (kc + 1)],
                    btz[:, T * kc:T * (kc + 1)], start=(kc == 0),
                    stop=False), waits=[t_btz] if kc == 0 else ())
            t_smm = P.op("tensor", lambda t, c=c: t.matmul(
                pb[4][0:NT, 0:T], sHT[:, c * NT:c * NT + NT],
                cmatT[:, 1:1 + T], start=False, stop=True), sem="pe",
                waits=[t_hcols])
            bank_w[4] = t_smm
            t_svr_reader = t_smm
            t_se = evac("vector", lambda v: v.tensor_copy(
                sS[:], pb[4][0:NT, 0:T]), 4)
            t_srow = P.op("sync", lambda s: s.dma_start(
                s_row[0:1, :], sS[:]), sem="ar", amt=16, waits=[t_se])
            mm_group([(pb[6][0:128, 0:512], sones[0:1, :], s_row[0:1, :])],
                     6, extra_waits=[t_srow, t_ones])
            t_sbc = evac("vector", lambda v: v.tensor_copy(
                ssbc[:], pb[6][0:128, 0:512]), 6)

            if dbg and c == 0:
                t_dmp = dump(dbg_sbc, ssbc[:], [t_sbc, t_dmp])
                t_dmp = dump(dbg_ht, sHT[0:32, 0:33], [t_hcols, t_dmp])
            if c < n_chunks - 1:
                t_x_next = emit_xload(
                    c + 1, [t_svr_reader, t_scrbtz if c == 0 else None])
            t_mm2_last = None
            t_out_stt = [None] * NT

            def emit_xgt(e):
                nonlocal_toks = []
                for ti in range(2):
                    g = 2 * e + ti
                    xgs = sxg[:, 2048 + 1024 * ti:2048 + 1024 * (ti + 1)]
                    t_ga = P.op("gpsimd", lambda gq, g=g, xgs=xgs:
                                gq.indirect_dma_start(
                        xgs, None, scr_xn,
                        bass.IndirectOffsetOnAxis(
                            ap=stabTu[:, g:g + 1], axis=0),
                        bounds_check=regs["b_tok"], oob_is_err=False),
                        sem=f"gi{ti}", amt=16,
                        waits=[t_tab, xg_tok[ti], t_arena1_z,
                               yab_tok[2 * ti], yab_tok[2 * ti + 1],
                               front_tok["t_xnst"]])
                    last_ga[ti] = t_ga
                    for g2 in range(2):
                        bk = 6 + g2
                        for dq in range(4):
                            dblk = 4 * g2 + dq
                            tw_ = P.op("tensor", lambda tn, xgs=xgs,
                                       dblk=dblk, dq=dq, bk=bk:
                                       tn.transpose(
                                pb[bk][0:128, 128 * dq:128 * (dq + 1)],
                                xgs[:, 128 * dblk:128 * (dblk + 1)],
                                sidf[:]),
                                sem="pe" if dq == 3 else None,
                                waits=([t_ga, bank_r[bk]]
                                       if dq == 0 else []))
                        bank_w[bk] = tw_
                        tok_ = evac(
                            "vector", lambda v, e=e, ti=ti, g2=g2, bk=bk:
                            v.tensor_copy(
                                swit[e % 2][:].rearrange(
                                    "p (d cc) -> p d cc", d=8)
                                [:, 4 * g2:4 * (g2 + 1),
                                 128 * ti:128 * (ti + 1)],
                                pb[bk][0:128, 0:512].rearrange(
                                    "p (d cc) -> p d cc", d=4)),
                            bk, extra_waits=[xgt_tok[e % 2],
                                             wit_mm_tok[e % 2]])
                    nonlocal_toks.append(tok_)
                    xg_tok[ti] = tw_
                return nonlocal_toks[-1]

            def emit_mm1s(e, slot_, tok_, t_xgt):
                t_relu_last = None
                t_h = None
                for fb in range(16):
                    bk = 4 + fb % 2
                    t_h = mm_group(
                        [(pb[bk][0:128, 0:256],
                          pool[slot_][:, 2048 * dc + 128 * fb:
                                      2048 * dc + 128 * (fb + 1)],
                          swit[e % 2][:, 256 * dc:256 * (dc + 1)])
                         for dc in range(8)],
                        bk, extra_waits=[tok_, t_xgt])
                    t_relu_last = evac(
                        "scalar", lambda s, fb=fb, bk=bk: s.activation(
                            shts[:, 256 * fb:256 * (fb + 1)],
                            pb[bk][0:128, 0:256], AF.Relu), bk,
                        extra_waits=([t_mm2_prev] if fb == 0 else ()))
                pool_tok[slot_] = t_h
                xgt_tok[e % 2] = t_h
                return t_relu_last

            def emit_mm2s(e, slot_, tok_, t_relu, t_smix_dep):
                nonlocal t_yscat_all
                last = None
                for ti in range(2):
                    bks = (0, 1) if ti == 0 else (2, 3)
                    for h2 in range(2):
                        bk = bks[h2]
                        last = mm_group(
                            [(pb[bk][0:128, 0:512],
                              shts[:, 256 * fc + 128 * ti:
                                   256 * fc + 128 * (ti + 1)],
                              pool[slot_][:, 1024 * fc + 512 * h2:
                                          1024 * fc + 512 * (h2 + 1)])
                             for fc in range(16)],
                            bk, extra_waits=[t_relu, tok_] +
                            ([t_smix_dep] if e == 0 else []))
                        tk_ = evac("vector",
                                   lambda v, ti=ti, h2=h2, bk=bk:
                                   v.tensor_copy(
                            syp[:, 1024 * ti + 512 * h2:
                                1024 * ti + 512 * (h2 + 1)],
                            pb[bk][0:128, 0:512]), bk,
                            extra_waits=[yp_tok[ti]] if h2 == 0 else ())
                    g = 2 * e + ti
                    _ioa = bass.IndirectOffsetOnAxis(
                        ap=stabTu[:, 16 + g:17 + g], axis=0)
                    _vap = syp[:, 1024 * ti:1024 * (ti + 1)]
                    t_sc = P.op("gpsimd", lambda gq, _ioa=_ioa, _vap=_vap:
                                gq.indirect_dma_start(
                        scr_yab, _ioa, _vap, None,
                        bounds_check=regs["b_slot"], oob_is_err=False),
                        sem=f"gs{ti}", amt=16,
                        waits=[tk_, t_tab, yab_fin[0], yab_fin[1]])
                    yp_tok[ti] = t_sc
                    t_yscat_all = t_sc
                pool_tok[slot_] = last
                return last

            def load_w1(e):
                if c == 0:
                    s_, t_, w1_scr_tok[e] = pool_load(
                        w1_ap(e, w1), store_to=w1_ap(e, scr_w1))
                else:
                    s_, t_, _ = pool_load(w1_ap(e, scr_w1),
                                          [w1_scr_tok[e]],
                                          eng="sync")
                return s_, t_

            def load_w2(e):
                if c == 0:
                    s_, t_, w2_scr_tok[e] = pool_load(
                        w2_ap(e, w2), store_to=w2_ap(e, scr_w2))
                else:
                    s_, t_, _ = pool_load(w2_ap(e, scr_w2),
                                          [w2_scr_tok[e]],
                                          eng="sync")
                return s_, t_

            w2_slot, t_w2 = load_w2(0)
            t_xgt_cur = emit_xgt(0)
            t_xgt_next = None
            t_yin = None
            for hb in range(16):
                t_yin = P.op("vector", lambda v, hb=hb:
                             v.scalar_tensor_tensor(
                    sp1[:, CH * hb:CH * (hb + 1)], ssbc[:],
                    sDVT[:, hb:hb + 1], sp1[:, CH * hb:CH * (hb + 1)],
                    ALU.add, ALU.mult), sem="dv" if hb == 15 else None,
                    waits=[t_sbc, t_dvt] if hb == 0 else ())
            t_y = None
            t_smix_done = None
            for t in range(NT):
                bks = (0, 1) if t % 2 == 0 else (2, 3)
                for h2 in range(2):
                    bk = bks[h2]
                    t_y = mm_group(
                        [(pb[bk][0:128, 0:512],
                          sp1[:, CH * hb2 + T * t:CH * hb2 + T * (t + 1)],
                          pool[wout_slot][:, 1024 * hb2 + 512 * h2:
                                          1024 * hb2 + 512 * (h2 + 1)])
                         for hb2 in range(16)],
                        bk, extra_waits=[t_yin, t_wout])
                    evac("vector", lambda v, bk=bk, h2=h2: v.tensor_tensor(
                        stmp[:], pb[bk][0:128, 0:512],
                        sBOUT[:, 512 * h2:512 * (h2 + 1)], ALU.add), bk)
                    t_smix_done = P.op(
                        "vector", lambda v, t=t, h2=h2: v.tensor_tensor(
                            smix[:, 1024 * t + 512 * h2:
                                 1024 * t + 512 * (h2 + 1)],
                            stmp[:],
                            sxn[:, 1024 * t + 512 * h2:
                                1024 * t + 512 * (h2 + 1)], ALU.add),
                        sem="dv")
            if dbg and c == 0:
                t_dmp = dump(dbg_yin, sp1[:], [t_yin, t_dmp])
                t_dmp = dump(dbg_smix, smix[:], [t_smix_done, t_dmp])
            pool_tok[wout_slot] = t_y
            t_p1_reader = t_y if not (dbg and c == 0) else t_dmp

            cur_xgt = t_xgt_cur
            t_relu = emit_mm1s(0, w1_slot, t_w1, cur_xgt)
            nw1_slot, nt_w1 = load_w1(1)
            t_xgt_next = emit_xgt(1)
            for e in range(E):
                if e > 0:
                    t_relu = emit_mm1s(e, w1_slot, t_w1, cur_xgt)
                w2_next = None
                if e < E - 1:
                    w1_slot, t_w1 = nw1_slot, nt_w1
                    cur_xgt = t_xgt_next
                    w2_next = load_w2(e + 1)
                    if e < E - 2:
                        t_xgt_next = emit_xgt(e + 2)
                t_mm2_last = emit_mm2s(e, w2_slot, t_w2, t_relu,
                                       t_smix_done)
                if e < E - 1:
                    w2_slot, t_w2 = w2_next
                    if e < E - 2:
                        nw1_slot, nt_w1 = load_w1(e + 2)
            t_mm2_prev = t_mm2_last
            t_sht_reader = t_mm2_last
            scrxn_reader[c % 2] = (last_ga[0], last_ga[1])
            if c < n_chunks - 1:
                emit_front_stats(c + 1,
                                 [t_smix_done, su_scrib["t"], su_tok[2],
                                  su_tok[3], sub_silu_tok[2],
                                  sub_silu_tok[3]],
                                 t_x_next)
            if dbg and c == 0:
                t_dmp = dump(dbg_yab, scr_yab[:],
                             [t_yscat_all, yp_tok[0], yp_tok[1], t_dmp])
            if c < n_chunks - 1:
                for _t in (0, 2, 1, 3):
                    emit_front_tile(c + 1, _t, first=(_t == 0))
            t_store = None
            t_fin = None
            t_g = None
            t_fin_tiles = [None] * NT
            sxe_tok = [None, None]
            soutb_tok = [None, None]
            t_su_free = [su_silu_tok[0], su_silu_tok[1], t_smix_done,
                         su_scrib["t"], su_tok[2], su_tok[3],
                         sub_silu_tok[2], sub_silu_tok[3]]
            yab = ar1b
            first_e = True
            for t in (0, 2, 1, 3):
                w = t % 2
                r0 = c * CH + t * T
                ya_s = yab[:, 4096 + 2048 * w:4096 + 2048 * w + 1024]
                yb_s = yab[:, 4096 + 2048 * w + 1024:4096 + 2048 * (w + 1)]
                t_ya = P.op("scalar", lambda s, t=t, ya_s=ya_s: s.dma_start(
                    ya_s, scr_yab[128 * t:128 * (t + 1), :]),
                    sem=f"gy{w}", amt=16,
                    waits=[t_yscat_all, yp_tok[0], yp_tok[1],
                           yab_tok[2 * w], xg_tok[0], xg_tok[1]])
                t_yb = P.op("scalar", lambda s, t=t, yb_s=yb_s: s.dma_start(
                    yb_s, scr_yab[512 + 128 * t:512 + 128 * (t + 1), :]),
                    sem=f"gy{w}", amt=16, waits=[yab_tok[2 * w + 1]])
                t_xe = P.op("scalar", lambda s, r0=r0, w=w: s.dma_start(
                    sxe[:, 1024 * w:1024 * (w + 1)], x_d[r0:r0 + T, :]),
                    sem=f"gy{w}", amt=16,
                    waits=[sxe_tok[w], t_p1_reader])
                t_mul = [None, None]
                tko = None
                for h in range(2):
                    P.op("vector", lambda v, t=t, h=h, ya_s=ya_s:
                         v.tensor_scalar(
                        su[h][:], ya_s[:, 512 * h:512 * (h + 1)],
                        swab[:, t:t + 1], None, ALU.mult),
                        waits=[t_ya, t_yb, t_xe, t_wab] +
                        (t_su_free if first_e else []))
                    tko = P.op("vector", lambda v, t=t, h=h, yb_s=yb_s:
                               v.scalar_tensor_tensor(
                        su[h][:], yb_s[:, 512 * h:512 * (h + 1)],
                        swab[:, 4 + t:5 + t], su[h][:],
                        ALU.mult, ALU.add), sem="dv")
                first_e = False
                yab_tok[2 * w] = tko
                yab_tok[2 * w + 1] = tko
                t_out_stt[t] = tko
                for h in range(2):
                    t_g = P.op("scalar", lambda s, h=h: s.activation(
                        su[h][:], su[h][:], AF.Silu),
                        sem="ac", waits=[tko, t_sout_reader
                                         if (dbg and c == 0) else None])
                    su_silu_tok[h] = t_g
                    t_mul[h] = P.op("vector", lambda v, t=t, w=w, h=h:
                                    v.tensor_tensor(
                        soutb[:, 2048 + 1024 * w + 512 * h:
                              2048 + 1024 * w + 512 * (h + 1)],
                        smix[:, 1024 * t + 512 * h:
                             1024 * t + 512 * (h + 1)],
                        su[h][:], ALU.mult), sem="dv",
                        waits=[t_g, soutb_tok[0] if h == 0 else None,
                               soutb_tok[1] if h == 0 else None,
                               t_smix_done])
                t_fin = P.op("vector", lambda v, w=w: v.tensor_tensor(
                    soutb[:, 2048 + 1024 * w:2048 + 1024 * (w + 1)],
                    soutb[:, 2048 + 1024 * w:2048 + 1024 * (w + 1)],
                    sxe[:, 1024 * w:1024 * (w + 1)], ALU.add),
                    sem="dv", waits=[t_xe, t_mul[0], t_mul[1]])
                sxe_tok[w] = t_fin
                yab_fin[w] = t_xe
                t_fin_tiles[t] = t_fin
                t_store = P.op("scalar", lambda s, r0=r0, w=w: s.dma_start(
                    out_d[r0:r0 + T, :],
                    soutb[:, 2048 + 1024 * w:2048 + 1024 * (w + 1)]),
                    sem="go", amt=16, waits=[t_fin])
                soutb_tok[w] = t_store
            t_sout_reader = t_fin
            t_e_bufs_reader = t_store

            swit_tok[0] = xgt_tok[0]
            swit_tok[1] = xgt_tok[1]
            front_tok["xnt_reader"] = t_xnt_reader

        P.streams["sync"].append(
            lambda s: s.wait_ge(sems["sd"], P.cnt["sd"]))
        for _sn in ("gt", "gy0", "gy1", "ap0", "ap1", "ap2",
                    "aw0", "aw1", "ar"):
            if P.cnt[_sn]:
                P.streams["sync"].append(
                    lambda s, _sn=_sn: s.wait_ge(sems[_sn], P.cnt[_sn]))
        P.streams["gpsimd"].append(
            lambda g: g.wait_ge(sems["gd"], P.cnt["gd"]))
        for _sn in ("gi0", "gi1", "gs0", "gs1", "gx", "gn0", "gn1", "gz"):
            if P.cnt[_sn]:
                P.streams["gpsimd"].append(
                    lambda g, _sn=_sn: g.wait_ge(sems[_sn], P.cnt[_sn]))
        P.streams["scalar"].append(
            lambda s: s.wait_ge(sems["ad"], P.cnt["ad"]))
        P.streams["scalar"].append(
            lambda s: s.wait_ge(sems["go"], P.cnt["go"]))
        P.replay(block)
    return nc


_NC_CACHE = {}


def _get_nc():
    if "nc" not in _NC_CACHE:
        _NC_CACHE["nc"] = build_nc()
    return _NC_CACHE["nc"]


def kernel(L0_attn_x, L0_attn_W_in, L0_attn_b_in, L0_attn_conv_w,
           L0_attn_conv_b, L0_attn_W_ssm, L0_attn_b_ssm, L0_attn_A,
           L0_attn_Bv, L0_attn_Cv, L0_attn_Dv, L0_attn_W_out, L0_attn_b_out,
           L0_ffn_Wg, L0_ffn_W1, L0_ffn_W2):
    nc = _get_nc()
    wmap = {
        "w_in": np.ascontiguousarray(L0_attn_W_in, dtype=np.float32),
        "b_in": np.ascontiguousarray(L0_attn_b_in, dtype=np.float32),
        "conv_w": np.ascontiguousarray(L0_attn_conv_w, dtype=np.float32),
        "conv_b": np.ascontiguousarray(L0_attn_conv_b, dtype=np.float32),
        "w_ssm": np.ascontiguousarray(L0_attn_W_ssm, dtype=np.float32),
        "b_ssm": np.ascontiguousarray(L0_attn_b_ssm, dtype=np.float32),
        "A": np.ascontiguousarray(L0_attn_A, dtype=np.float32),
        "Bv": np.ascontiguousarray(L0_attn_Bv, dtype=np.float32),
        "Cv": np.ascontiguousarray(L0_attn_Cv, dtype=np.float32),
        "Dv": np.ascontiguousarray(L0_attn_Dv, dtype=np.float32),
        "w_out": np.ascontiguousarray(L0_attn_W_out, dtype=np.float32),
        "b_out": np.ascontiguousarray(L0_attn_b_out, dtype=np.float32),
        "wg": np.ascontiguousarray(L0_ffn_Wg, dtype=np.float32),
        "w1": np.ascontiguousarray(L0_ffn_W1, dtype=np.float32),
        "w2": np.ascontiguousarray(L0_ffn_W2, dtype=np.float32),
    }
    xs = np.ascontiguousarray(L0_attn_x, dtype=np.float32)
    in_maps = [dict(wmap, x=xs[i]) for i in range(8)]
    res = run_bass_kernel_spmd(nc, in_maps, core_ids=list(range(8)))
    return np.stack([r["out"] for r in res.results], axis=0)
